# version 35
# baseline (speedup 1.0000x reference)
import math
import numpy as np
import ml_dtypes
import concourse.bass as bass
import concourse.mybir as mybir
from concourse.bass_utils import run_bass_kernel_spmd
from contextlib import ExitStack

F32 = mybir.dt.float32
BF16 = mybir.dt.bfloat16
AF = mybir.ActivationFunctionType
ALU = mybir.AluOpType
AX = mybir.AxisListType

D = 2048
T = 4096
HD = 128
NH = 16
KVH = 4
NC = 16
NTT = 32
HTT = 16
TH = 2048
EPS_LN = 1e-5
EPS_A = 1e-6
SM_SCALE = 1.0 / math.sqrt(HD)

SUBLAYERS = ("attn_a", "ffn", "attn_b", "ffn2")
ENGINES = ("pe", "act", "dve", "pool", "sp")


class Res:
    __slots__ = ("writer", "readers")

    def __init__(self):
        self.writer = None
        self.readers = []


class Step:
    __slots__ = ("engine", "emit", "queue", "idx", "dma_idx", "deps",
                 "needs_sig", "sig_val")

    def __init__(self, engine, emit, queue):
        self.engine = engine
        self.emit = emit
        self.queue = queue
        self.deps = []
        self.needs_sig = False
        self.sig_val = None


class Prog:
    QENG = {"sp": "sp", "pool": "pool", "act": "act"}

    def __init__(self):
        self.steps = []
        self.res = {}
        self.dma_counts = {"sp": 0, "pool": 0, "act": 0}

    def r(self, key):
        if key not in self.res:
            self.res[key] = Res()
        return self.res[key]

    def step(self, engine, emit, reads=(), writes=(), queue=None):
        s = Step(engine, emit, queue)
        s.idx = len(self.steps)
        if queue is not None:
            s.dma_idx = self.dma_counts[queue]
            self.dma_counts[queue] += 1
        deps = set()
        for k in reads:
            r = self.r(k)
            if r.writer is not None:
                deps.add(r.writer)
        for k in writes:
            r = self.r(k)
            if r.writer is not None:
                deps.add(r.writer)
            for rd in r.readers:
                deps.add(rd)
        for d in deps:
            if d is s:
                continue
            if (d.queue is None and s.queue is None
                    and d.engine == s.engine == "pe"):
                continue
            s.deps.append(d)
        for k in reads:
            self.res[k].readers.append(s)
        for k in writes:
            r = self.res[k]
            r.writer = s
            r.readers = []
        self.steps.append(s)
        return s

    def dma(self, queue, emit, reads=(), writes=()):
        return self.step(self.QENG[queue], emit, reads=reads, writes=writes,
                         queue=queue)

    DMA_POOL = 32

    def finalize(self, block, sems):
        for s in self.steps:
            for d in s.deps:
                if d.queue is None:
                    d.needs_sig = True
        cnt = {e: 0 for e in ENGINES}
        for s in self.steps:
            if s.queue is None and s.needs_sig:
                cnt[s.engine] += 1
                s.sig_val = cnt[s.engine]
        per_eng = {e: [] for e in ENGINES}
        for s in self.steps:
            per_eng[s.engine].append(s)
        K = self.DMA_POOL

        def dma_ev(d):
            return (sems[f"dq_{d.queue}_{d.dma_idx % K}"],
                    16 * (d.dma_idx // K + 1))

        def make_body(ename):
            def body(eng):
                known = {}

                def wait(sm, val):
                    if known.get(sm, 0) < val:
                        eng.wait_ge(sm, val)
                        known[sm] = val

                for s in per_eng[ename]:
                    for d in s.deps:
                        if d.queue is None:
                            wait(sems[d.engine], d.sig_val)
                        else:
                            wait(*dma_ev(d))
                    if s.queue is not None:
                        sm = sems[f"dq_{s.queue}_{s.dma_idx % K}"]
                        gen = s.dma_idx // K
                        if gen > 0:
                            wait(sm, 16 * gen)
                        ins = s.emit(eng)
                        ins.then_inc(sm, 16)
                    else:
                        ins = s.emit(eng)
                        if s.needs_sig:
                            ins.then_inc(sems[s.engine], 1)
                if ename == "pool":
                    for q in ("pool", "sp", "act"):
                        n = self.dma_counts[q]
                        for i in range(min(K, n)):
                            gens = (n - 1 - i) // K + 1
                            wait(sems[f"dq_{q}_{i}"], 16 * gens)
            return body

        block.tensor(make_body("pe"))
        block.scalar(make_body("act"))
        block.vector(make_body("dve"))
        block.gpsimd(make_body("pool"))
        block.sync(make_body("sp"))


class BG:

    def __init__(self):
        self.q = []

    def set(self, g):
        self.q.append(g)

    @property
    def g(self):
        return self.q[0] if self.q else None

    def step(self, n=1):
        for _ in range(n):
            if not self.q:
                return
            try:
                next(self.q[0])
            except StopIteration:
                self.q.pop(0)

    def finish(self):
        while self.q:
            self.step()

    def step_until(self, prog, keys):
        while self.q and any(
                prog.res.get(k) is None or prog.res[k].writer is None
                for k in keys):
            self.step()


def drive(main_gen, bg, k):
    for _ in main_gen:
        if bg is not None:
            bg.step(k)
    return


def _host_constants():
    bf = ml_dtypes.bfloat16
    ident16 = np.eye(128, dtype=bf)
    ident32 = np.eye(128, dtype=np.float32)
    k = np.arange(128)[:, None]
    q = np.arange(128)[None, :]
    m = (k <= q).astype(np.float32)
    maskha = np.tile(np.concatenate([m, np.ones((128, 128), np.float32)], 1),
                     (1, 2)).astype(bf)
    qq = np.arange(512)[None, :]
    dmasks = np.concatenate(
        [((np.arange(128)[:, None] + r * 128) <= qq).astype(np.float32)
         for r in range(4)], axis=1).astype(bf)
    inv_freq = (1.0 / (10000.0 ** (np.arange(0, HD, 2, dtype=np.float32) / HD))
                ).astype(np.float32)
    t = np.arange(T, dtype=np.float32)
    freqs = np.outer(t, inv_freq).astype(np.float32)
    cos = np.cos(freqs).astype(np.float32).T
    sin = np.sin(freqs).astype(np.float32).T
    cos_tbl = np.concatenate([cos, cos], axis=0)
    sin_tbl = np.concatenate([sin, -sin], axis=0)
    return dict(ident32=ident32, ident16=ident16,
                maskha=maskha, dmasks=dmasks, cos_tbl=cos_tbl,
                sin_tbl=sin_tbl)


def build_program(sublayers=None, debug=False):
    if sublayers is None:
        sublayers = SUBLAYERS
    nc = bass.Bass("TRN2", target_bir_lowering=False)
    p = Prog()

    din = lambda n, s: nc.dram_tensor(n, s, F32, kind="ExternalInput").ap()
    X = din("x", [T, D])
    WqA, WkA, WvA, WoA = din("wq_a", [D, D]), din("wk_a", [512, D]), \
        din("wv_a", [512, D]), din("wo_a", [D, D])
    WqB, WkB, WvB, WoB = din("wq_b", [D, D]), din("wk_b", [D, D]), \
        din("wv_b", [D, D]), din("wo_b", [D, D])
    BqB, BkB, BvB, BoB = din("bq_b", [D]), din("bk_b", [D]), \
        din("bv_b", [D]), din("bo_b", [D])
    Wf, Gam, Bet, LnW, LnB = din("w_ffn", [D, D]), din("gamma", [D]), \
        din("beta", [D]), din("ln_w", [D]), din("ln_b", [D])
    C_I16 = nc.dram_tensor("ident16", [128, 128], BF16, kind="ExternalInput").ap()
    C_MHA = nc.dram_tensor("maskha", [128, 512], BF16, kind="ExternalInput").ap()
    C_DM = nc.dram_tensor("dmasks", [128, 2048], BF16, kind="ExternalInput").ap()
    C_I32 = din("ident32", [128, 128])
    C_COS = din("cos_tbl", [128, T])
    C_SIN = din("sin_tbl", [128, T])
    OUT = nc.dram_tensor("out", [T, D], F32, kind="ExternalOutput").ap()

    dr = lambda n, s, d: nc.dram_tensor(n, s, d, kind="Internal").ap()
    xh = [dr(f"x_hbm{i}", [T, D], F32) for i in range(2)]
    wmax_d = dr("wmax_d", [128], F32)
    wq_a16 = dr("wq_a16", [D, D], BF16)
    wk_a16 = dr("wk_a16", [512, D], BF16)
    wv_a16 = dr("wv_a16", [512, D], BF16)
    wo_a16 = dr("wo_a16", [D, D], BF16)
    wf16 = dr("wf16", [D, D], BF16)
    wq_b16 = [dr(f"wq_b16_{g}", [D, 512], BF16) for g in range(4)]
    wk_b16 = [dr(f"wk_b16_{g}", [D, 512], BF16) for g in range(4)]
    wv_b16 = [dr(f"wv_b16_{g}", [D, 512], BF16) for g in range(4)]
    wo_b16 = [dr(f"wo_b16_{g}", [D, 512], BF16) for g in range(4)]
    wffh = [dr(f"wff_og{g}", [D, 512], BF16) for g in range(4)]
    qh_a = [dr(f"qa_h{h}", [128, T], BF16) for h in range(NH)]
    kh_a = [dr(f"ka_h{h}", [128, T], BF16) for h in range(KVH)]
    vh_a = [dr(f"va_h{h}", [T, 128], BF16) for h in range(KVH)]
    aoh_a = [dr(f"aoa_h{h}", [128, T], BF16) for h in range(NH)]
    qh_b = [dr(f"qb_h{h}", [128, T], BF16) for h in range(NH)]
    kh_b = [dr(f"kb_h{h}", [128, T], BF16) for h in range(NH)]
    vh_b = [dr(f"vb_h{h}", [T, 128], BF16) for h in range(NH)]
    aoh_b = [dr(f"aob_h{h}", [128, T], BF16) for h in range(NH)]

    ctx = ExitStack()
    sb = lambda n, s, d: ctx.enter_context(nc.sbuf_tensor(n, s, d))
    psmt = lambda n, s, d: ctx.enter_context(nc.psum_tensor(n, s, d))
    mksem = lambda n: ctx.enter_context(nc.semaphore(n))

    xn_fm = sb("xn_fm", [128, NC * TH], BF16)
    G0 = sb("g0", [128, T], BF16)
    G1 = sb("g1", [128, T], BF16)
    G2 = sb("g2", [128, T], BF16)
    QF2 = sb("qf2", [128, 2048], BF16)
    WS0 = sb("ws0", [128, 8192], BF16)
    WS1 = sb("ws1", [128, 8192], BF16)
    XO = sb("xo", [128, 1536], F32)
    XN2 = sb("xn2", [128, 1024], F32)
    XR = sb("xr", [128, 512], F32)
    XT = sb("xt", [128, 2048], BF16)
    RB = sb("rb", [128, 4096], F32)
    CSR = sb("csr", [128, 1024], F32)
    PST = sb("pst", [128, 2048], BF16)
    AOT = sb("aot", [128, 1024], BF16)
    SUM4 = sb("sum4", [128, 128], F32)
    SSQ4 = sb("ssq4", [128, 128], F32)
    STC = sb("stc", [128, 360], F32)
    ROWS16 = sb("rows16", [1, 6144], BF16)
    ONESM = sb("onesm", [128, 128], BF16)
    WROW = sb("wrow", [1, 128], F32)
    maskha = sb("mha", [128, 512], BF16)
    dmask = sb("dmask", [128, 2048], BF16)
    ones_r16 = sb("ones_r16", [1, 128], BF16)
    epsc = sb("epsc", [128, 2], F32)
    bqc = sb("bqc", [128, 16], F32)
    bkc = sb("bkc", [128, 16], F32)
    lnwc = sb("lnwc", [128, 16], F32)
    lnbc16 = sb("lnbc16", [128, 16], BF16)
    ST32 = sb("st32", [128, 128], F32)
    ZC = sb("zc", [128, 20], F32)
    XRB = sb("xrb", [128, 2048], F32)
    IDT = sb("idt", [128, 128], BF16)

    pb = [psmt(f"pb{i}", [128, 512], F32) for i in range(8)]

    sems = {e: mksem("sem_" + e) for e in ENGINES}
    for q in ("sp", "pool", "act"):
        for i in range(Prog.DMA_POOL):
            sems[f"dq_{q}_{i}"] = mksem(f"dq_{q}_{i}")

    XO_ALL = ["xo:0", "xo:1", "xo:2", "xo:3"]

    def rb(i):
        return RB[:, i * 512:(i + 1) * 512]

    stc_cols = {}
    _stc_next = [0]

    def stc(name, n=32):
        if name not in stc_cols:
            stc_cols[name] = (_stc_next[0], n)
            _stc_next[0] += n
            assert _stc_next[0] <= 360, "STC overflow"
        o, nn = stc_cols[name]
        assert nn == n
        return STC[:, o:o + n]

    def hb():
        p.step("pe", lambda e: e.matmul(
            pb[7][0:64, 0:64], IDT[:, 0:64], IDT[:, 0:64],
            start=True, stop=True), reads=["idt"], writes=["pb7"])

    p.step("dve", lambda e: e.memset(ONESM[:], 1.0), writes=["onesm"])
    p.step("dve", lambda e: e.memset(ones_r16[:], 1.0), writes=["ones_r16"])
    p.step("dve", lambda e: e.memset(epsc[:, 0:1], EPS_LN), writes=["epsc"])
    p.step("dve", lambda e: e.memset(epsc[:, 1:2], EPS_A), writes=["epsca"])
    p.dma("sp", lambda e: e.dma_start(out=maskha[:], in_=C_MHA), writes=["mha"])
    p.dma("sp", lambda e: e.dma_start(out=dmask[:], in_=C_DM), writes=["dm"])
    p.dma("sp", lambda e: e.dma_start(out=IDT[:], in_=C_I16), writes=["idt"])

    for nm, dst, src in [("wq_a16", wq_a16, WqA), ("wk_a16", wk_a16, WkA),
                         ("wv_a16", wv_a16, WvA), ("wo_a16", wo_a16, WoA),
                         ("wf16", wf16, Wf)]:
        p.dma("pool", (lambda d_, s_: lambda e: e.dma_start(out=d_, in_=s_))(dst, src),
              writes=[nm])
    def emit_attn_b_setup():
        for nm, dsts, src in [("wq_b16", wq_b16, WqB), ("wk_b16", wk_b16, WkB),
                              ("wv_b16", wv_b16, WvB), ("wo_b16", wo_b16, WoB)]:
            for g in range(4):
                p.dma("pool",
                      (lambda d_, s_, g_: lambda e: e.dma_start(
                          out=d_, in_=s_[:, g_ * 512:(g_ + 1) * 512]))(
                              dsts[g], src, g),
                      writes=[f"{nm}:{g}"])
        for nm, dst, src in [("bqc", bqc, BqB), ("bkc", bkc, BkB)]:
            p.dma("pool",
                  (lambda d_, s_: lambda e: e.dma_start(
                      out=d_[:], in_=s_.rearrange("(c p) -> p c", p=128),
                      allow_slow_non_contiguous=True))(dst, src),
                  writes=[nm])
        p.dma("pool", lambda e: e.dma_start(
            out=ROWS16[0:1, 2048:4096], in_=BoB.rearrange("(a d) -> a d", a=1)),
            writes=["borow"])
        p.dma("pool", lambda e: e.dma_start(
            out=ROWS16[0:1, 4096:6144], in_=BvB.rearrange("(a d) -> a d", a=1)),
            writes=["bvrow"])

    p.dma("pool", lambda e: e.dma_start(
        out=lnwc[:], in_=LnW.rearrange("(c p) -> p c", p=128),
        allow_slow_non_contiguous=True), writes=["lnwc"])
    p.dma("pool", lambda e: e.dma_start(
        out=lnbc16[:], in_=LnB.rearrange("(c p) -> p c", p=128),
        allow_slow_non_contiguous=True), writes=["lnbc16"])

    any_ffn = any(s.startswith("ffn") or s == "wff" for s in sublayers)

    def wff_scan_gen():
        RB_W = ["rb:0:0", "rb:0:1", "rb:0:2", "rb:0:3"]
        for c in range(16):
            p.dma("pool", (lambda c_: lambda e: e.dma_start(
                out=RB[:, 0:2048], in_=Wf[c_ * 128:(c_ + 1) * 128, :]))(c),
                writes=RB_W)
            p.step("dve", lambda e: e.reduce_max(
                stc("wtmp", 1), RB[:, 0:2048], axis=AX.X,
                apply_absolute_value=True),
                reads=RB_W, writes=["stc:wtmp"])
            if c == 0:
                p.step("dve", lambda e: e.tensor_copy(stc("wmax", 1), stc("wtmp", 1)),
                       reads=["stc:wtmp"], writes=["stc:wmax"])
            else:
                p.step("dve", lambda e: e.tensor_max(
                    stc("wmax", 1), stc("wmax", 1), stc("wtmp", 1)),
                    reads=["stc:wtmp", "stc:wmax"], writes=["stc:wmax"])
            yield

    def wff_xform_gen():
        RB_W = ["rb:0:0", "rb:0:1", "rb:0:2", "rb:0:3"]
        p.dma("pool", lambda e: e.dma_start(
            out=wmax_d.rearrange("(d a) -> d a", a=1), in_=stc("wmax", 1)),
            reads=["stc:wmax"], writes=["wmax_d"])
        p.dma("sp", lambda e: e.dma_start(
            out=WROW[0:1, 0:128],
            in_=wmax_d.rearrange("(a d) -> a d", a=1)),
            reads=["wmax_d"], writes=["wrow"])
        p.step("dve", lambda e: e.reduce_max(stc("scale", 1)[0:1, 0:1],
                                             WROW[0:1, 0:128], axis=AX.X),
               reads=["wrow"], writes=["stc:scale"])
        yield
        p.dma("pool", lambda e: e.dma_start(
            out=RB[0:1, 0:2048], in_=Gam.rearrange("(a d) -> a d", a=1)),
            writes=RB_W)
        p.step("dve", lambda e: e.tensor_scalar_mul(
            ROWS16[0:1, 0:2048], RB[0:1, 0:2048], stc("scale", 1)[0:1, 0:1]),
            reads=RB_W + ["stc:scale"], writes=["rrow"])
        for j in range(4):
            p.step("pe", (lambda j_: lambda e: e.matmul(
                pb[0][:, 0:512], ones_r16[:], ROWS16[0:1, j_ * 512:(j_ + 1) * 512],
                start=True, stop=True))(j),
                reads=["ones_r16", "rrow"], writes=["pb0"])
            p.step("act", (lambda j_: lambda e: e.copy(
                G0[:, j_ * 512:(j_ + 1) * 512], pb[0][:, 0:512]))(j),
                reads=["pb0"], writes=["g0:0"])
        p.dma("pool", lambda e: e.dma_start(
            out=RB[0:1, 2048:4096], in_=Bet.rearrange("(a d) -> a d", a=1)),
            writes=["rb:1:0", "rb:1:1", "rb:1:2", "rb:1:3"])
        yield
        for c in range(16):
            p.dma("sp", (lambda c_: lambda e: e.dma_start_transpose(
                WS1[:, 0:2048], wf16[:, c_ * 128:(c_ + 1) * 128]))(c),
                reads=["wf16"], writes=["wsB:0", "wsB:1", "wsB:2", "wsB:3"])
            p.step("act", lambda e: e.sign(WS1[:, 2048:4096], WS1[:, 0:2048]),
                   reads=["wsB:0", "wsB:1", "wsB:2", "wsB:3"],
                   writes=["wsB:4", "wsB:5", "wsB:6", "wsB:7"])
            for g in range(4):
                p.step("pe", (lambda c_, g_: lambda e: e.matmul(
                    pb[2 + g_][0:1, 0:512], lnbc16[:, c_:c_ + 1],
                    WS1[:, 2048 + g_ * 512:2048 + (g_ + 1) * 512],
                    start=(c_ == 0), stop=(c_ == 15)))(c, g),
                    reads=["lnbc16", f"wsB:{4 + g}"], writes=[f"pb{2 + g}"])
            p.step("dve", lambda e: e.tensor_mul(
                WS1[:, 4096:6144], WS1[:, 2048:4096], G0[:, 0:2048]),
                reads=["wsB:4", "wsB:5", "wsB:6", "wsB:7", "g0:0"],
                writes=["wsB:8", "wsB:9", "wsB:10", "wsB:11"])
            p.step("dve", (lambda c_: lambda e: e.tensor_scalar_mul(
                WS1[:, 6144:8192], WS1[:, 4096:6144], lnwc[:, c_:c_ + 1]))(c),
                reads=["wsB:8", "wsB:9", "wsB:10", "wsB:11", "lnwc"],
                writes=["wsB:12", "wsB:13", "wsB:14", "wsB:15"])
            for g in range(4):
                p.dma("pool", (lambda c_, g_: lambda e: e.dma_start(
                    out=wffh[g_][c_ * 128:(c_ + 1) * 128, :],
                    in_=WS1[:, 6144 + g_ * 512:6144 + (g_ + 1) * 512]))(c, g),
                    reads=[f"wsB:{12 + g}"], writes=[f"wff:{g}"])
            yield
        for g in range(4):
            p.step("dve", (lambda g_: lambda e: e.tensor_mul(
                XRB[0:1, 0:512], pb[2 + g_][0:1, 0:512],
                ROWS16[0:1, g_ * 512:(g_ + 1) * 512]))(g),
                reads=[f"pb{2 + g}", "rrow"], writes=["xrb:0"])
            p.step("dve", (lambda g_: lambda e: e.tensor_add(
                XRB[0:1, 512:1024], XRB[0:1, 0:512],
                RB[0:1, 2048 + g_ * 512:2048 + (g_ + 1) * 512]))(g),
                reads=["xrb:0", f"rb:1:{g}"], writes=["xrb:1"])
            p.step("dve", (lambda g_: lambda e: e.tensor_copy(
                ROWS16[0:1, g_ * 512:(g_ + 1) * 512], XRB[0:1, 512:1024]))(g),
                reads=["xrb:1"], writes=["rrow"])
        yield

    def wor_stage_a(s):
        og, h = s // 16, s % 16
        p.dma("sp", (lambda o_, h_, s_: lambda e: e.dma_start_transpose(
            xn_fm[:, s_ * 512:(s_ + 1) * 512],
            wo_a16[o_ * 512:(o_ + 1) * 512, h_ * 128:(h_ + 1) * 128]))(
                og, h, s),
            reads=["wo_a16"], writes=[f"xnfm:{s // 4}"])

    def wor_stage_b(s):
        og, h = s // 16, s % 16
        p.dma("sp", (lambda o_, h_, s_: lambda e: e.dma_start(
            out=xn_fm[:, s_ * 512:(s_ + 1) * 512],
            in_=wo_b16[o_][h_ * 128:(h_ + 1) * 128, :]))(og, h, s),
            reads=[f"wo_b16:{og}"], writes=[f"xnfm:{s // 4}"])

    def t_stats_slice(tt, og, src_slot):
        p.step("act", (lambda s_, c_: lambda e: e.activation(
            XR[:, 0:512], XN2[:, s_ * 512:(s_ + 1) * 512],
            AF.Square, accum_out=SSQ4[:, c_:c_ + 1]))(src_slot, tt * 4 + og),
            reads=[f"xn2:{src_slot}"], writes=["xr:0", f"ssq4:{tt}"])

    def emit_t1_tile(tt, og, pbi, x_src, x_src_key, x_dst, x_dst_key, final):
        oslot = (tt * 4 + og) % 3
        nslot = (tt * 4 + og) % 2
        src_key = "x_in" if x_src_key == "x_in" else f"{x_src_key}:{tt}:{og}"
        p.dma("act", (lambda o_, a_: lambda e: e.dma_start(
            out=XO[:, o_ * 512:(o_ + 1) * 512], in_=a_))(oslot,
            x_src[tt * 128:(tt + 1) * 128, og * 512:(og + 1) * 512]),
            reads=[src_key], writes=[f"xo:{oslot}"])
        p.step("dve", (lambda o_, n_, c_, i_: lambda e: e.scalar_tensor_tensor(
            XN2[:, n_ * 512:(n_ + 1) * 512], pb[i_][:, 0:512], 0.0,
            XO[:, o_ * 512:(o_ + 1) * 512], ALU.bypass, ALU.add,
            accum_out=SUM4[:, c_:c_ + 1]))(oslot, nslot, tt * 4 + og, pbi),
            reads=[f"pb{pbi}", f"xo:{oslot}"],
            writes=[f"xn2:{nslot}", f"sum4:{tt}"])
        p.dma("pool", (lambda n_: lambda e: e.dma_start(
            out=x_dst[tt * 128:(tt + 1) * 128, og * 512:(og + 1) * 512],
            in_=XN2[:, n_ * 512:(n_ + 1) * 512]))(nslot),
            reads=[f"xn2:{nslot}"], writes=[f"{x_dst_key}:{tt}:{og}"])
        if not final:
            t_stats_slice(tt, og, nslot)

    def emit_tstats(ffn_next, grp):
        lo, n = grp * 4, 4
        cs = lambda name: stc(name)[:, lo:lo + n]
        rkeys = [f"sum4:{tt}" for tt in range(lo, lo + n)]
        rkeys2 = [f"ssq4:{tt}" for tt in range(lo, lo + n)]
        p.step("dve", (lambda l_: lambda e: e.reduce_sum(
            stc("s1")[:, l_:l_ + 4],
            SUM4[:, l_ * 4:(l_ + 4) * 4].rearrange("p (t f) -> p t f", f=4),
            axis=AX.X))(lo),
            reads=rkeys, writes=[f"stc:s1:{grp}"])
        p.step("dve", (lambda l_: lambda e: e.reduce_sum(
            stc("q1")[:, l_:l_ + 4],
            SSQ4[:, l_ * 4:(l_ + 4) * 4].rearrange("p (t f) -> p t f", f=4),
            axis=AX.X))(lo),
            reads=rkeys2, writes=[f"stc:q1:{grp}"])
        p.step("dve", lambda e: e.tensor_scalar_mul(cs("m"), cs("s1"), 1.0 / D),
               reads=[f"stc:s1:{grp}"], writes=[f"stc:m:{grp}"])
        p.step("dve", lambda e: e.tensor_mul(cs("msq"), cs("m"), cs("m")),
               reads=[f"stc:m:{grp}"], writes=[f"stc:msq:{grp}"])
        p.step("dve", lambda e: e.scalar_tensor_tensor(
            cs("var"), cs("q1"), 1.0 / D, cs("msq"), ALU.mult, ALU.subtract),
            reads=[f"stc:q1:{grp}", f"stc:msq:{grp}"], writes=[f"stc:var:{grp}"])
        p.step("act", lambda e: e.activation(
            cs("t1"), cs("var"), AF.Ln, bias=epsc[:, 0:1]),
            reads=[f"stc:var:{grp}", "epsc"], writes=[f"stc:t1:{grp}"])
        p.step("act", lambda e: e.activation(cs("rs"), cs("t1"), AF.Exp,
                                             scale=-0.5),
               reads=[f"stc:t1:{grp}"], writes=[f"stc:rs:{grp}"])
        if not ffn_next:
            p.step("dve", lambda e: e.tensor_copy(cs("A"), cs("rs")),
                   reads=[f"stc:rs:{grp}"], writes=[f"stc:A:{grp}"])
        else:
            p.step("dve", lambda e: e.tensor_mul(cs("t2"), cs("var"), cs("rs")),
                   reads=[f"stc:var:{grp}", f"stc:rs:{grp}"],
                   writes=[f"stc:t2:{grp}"])
            p.step("dve", lambda e: e.tensor_mul(cs("t2"), cs("t2"), cs("rs")),
                   reads=[f"stc:rs:{grp}", f"stc:t2:{grp}"],
                   writes=[f"stc:t2:{grp}"])
            p.step("act", lambda e: e.activation(
                cs("t1"), cs("t2"), AF.Ln, bias=epsc[:, 0:1]),
                reads=[f"stc:t2:{grp}", "epsc"], writes=[f"stc:t1:{grp}"])
            p.step("act", lambda e: e.activation(cs("t3"), cs("t1"),
                                                 AF.Exp, scale=-0.5),
                   reads=[f"stc:t1:{grp}"], writes=[f"stc:t3:{grp}"])
            p.step("dve", lambda e: e.tensor_mul(cs("A"), cs("rs"), cs("t3")),
                   reads=[f"stc:rs:{grp}", f"stc:t3:{grp}"],
                   writes=[f"stc:A:{grp}"])
        p.step("dve", lambda e: e.scalar_tensor_tensor(
            cs("B"), cs("m"), -1.0, cs("A"), ALU.mult, ALU.mult),
            reads=[f"stc:m:{grp}", f"stc:A:{grp}"], writes=[f"stc:B:{grp}"])

    def t2_norm(i, tt, akey, buf, s_, bkey):
        ps = i % 4
        p.step("dve", (lambda s2, ps_, t_, b_: lambda e: e.tensor_scalar(
            PST[:, ps_ * 512:(ps_ + 1) * 512],
            b_[:, s2 * 512:(s2 + 1) * 512],
            stc("A")[:, t_:t_ + 1], stc("B")[:, t_:t_ + 1],
            ALU.mult, ALU.add))(s_, ps, tt, buf),
            reads=[bkey, f"stc:A:{akey}", f"stc:B:{akey}"],
            writes=[f"pst:{ps}"])

    def t2_load_norm(i, tt, og, x_src, x_src_key, akey):
        xs = i % 4
        src_key = ("x_in" if x_src_key == "x_in"
                   else f"{x_src_key}:{tt}:{og}")
        p.dma("act" if xs % 2 == 0 else "sp",
              (lambda xs_, a_: lambda e: e.dma_start(
            out=XRB[:, xs_ * 512:(xs_ + 1) * 512], in_=a_))(xs,
            x_src[tt * 128:(tt + 1) * 128, og * 512:(og + 1) * 512]),
            reads=[src_key], writes=[f"xrb:{xs}"])
        t2_norm(i, tt, akey, XRB, xs, f"xrb:{xs}")

    def t2_transpose(i, tt, og):
        ps = i % 4
        bank = 4 + (i % 2)
        for j in range(4):
            p.step("pe", (lambda ps_, j_, b_: lambda e: e.matmul(
                pb[b_][:, j_ * 128:(j_ + 1) * 128],
                PST[:, ps_ * 512 + j_ * 128:ps_ * 512 + (j_ + 1) * 128],
                IDT[:], start=True, stop=True))(ps, j, bank),
                reads=[f"pst:{ps}", "idt"], writes=[f"pb{bank}"])
        off = (tt % HTT) * 128
        p.step("act", (lambda b_, g_, o_: lambda e: e.copy(
            xn_fm[:].rearrange("p (c x) -> p c x", x=TH)[
                :, g_ * 4:(g_ + 1) * 4, o_:o_ + 128],
            pb[b_][:, 0:512].rearrange("p (c x) -> p c x", x=128)))(
                bank, og, off),
            reads=[f"pb{bank}"],
            writes=[f"xnfm:{og * 4 + j}" for j in range(4)])

    T2D = 3

    def t2_gen(x_src, x_src_key, grps, t0_keys=False):
        slices = []
        for grp in grps:
            for tt in range(grp * 4, (grp + 1) * 4):
                ak = f"t{tt}" if t0_keys else str(grp)
                for og in range(4):
                    slices.append((tt, og, ak))
        n = len(slices)
        for i, (tt, og, ak) in enumerate(slices):
            t2_load_norm(i, tt, og, x_src, x_src_key, ak)
            hb()
            if i >= T2D:
                t2_transpose(i - T2D, slices[i - T2D][0], slices[i - T2D][1])
                yield
        for i in range(max(0, n - T2D), n):
            t2_transpose(i, slices[i][0], slices[i][1])
            yield

    def drain(gen):
        if gen is None:
            return
        for _ in gen:
            pass

    def step_n(gen, n):
        if gen is None:
            return
        for _ in range(n):
            try:
                next(gen)
            except StopIteration:
                return

    def drive_t2(main_gen, bg, k, t2g, per, batch):
        i = 0
        for _ in main_gen:
            if bg is not None:
                bg.step(k)
            i += 1
            if i % per == 0:
                step_n(t2g, batch)

    def emit_tstats1(ffn_next, tt):
        cs = lambda name: stc(name)[:, tt:tt + 1]
        g = f"t{tt}"
        p.step("dve", lambda e: e.reduce_sum(
            cs("s1"), SUM4[:, tt * 4:tt * 4 + 4], axis=AX.X),
            reads=[f"sum4:{tt}"], writes=[f"stc:s1:{g}"])
        p.step("dve", lambda e: e.reduce_sum(
            cs("q1"), SSQ4[:, tt * 4:tt * 4 + 4], axis=AX.X),
            reads=[f"ssq4:{tt}"], writes=[f"stc:q1:{g}"])
        p.step("dve", lambda e: e.tensor_scalar_mul(cs("m"), cs("s1"), 1.0 / D),
               reads=[f"stc:s1:{g}"], writes=[f"stc:m:{g}"])
        p.step("dve", lambda e: e.tensor_mul(cs("msq"), cs("m"), cs("m")),
               reads=[f"stc:m:{g}"], writes=[f"stc:msq:{g}"])
        p.step("dve", lambda e: e.scalar_tensor_tensor(
            cs("var"), cs("q1"), 1.0 / D, cs("msq"), ALU.mult, ALU.subtract),
            reads=[f"stc:q1:{g}", f"stc:msq:{g}"], writes=[f"stc:var:{g}"])
        p.step("act", lambda e: e.activation(
            cs("t1"), cs("var"), AF.Ln, bias=epsc[:, 0:1]),
            reads=[f"stc:var:{g}", "epsc"], writes=[f"stc:t1:{g}"])
        p.step("act", lambda e: e.activation(cs("rs"), cs("t1"), AF.Exp,
                                             scale=-0.5),
               reads=[f"stc:t1:{g}"], writes=[f"stc:rs:{g}"])
        if not ffn_next:
            p.step("dve", lambda e: e.tensor_copy(cs("A"), cs("rs")),
                   reads=[f"stc:rs:{g}"], writes=[f"stc:A:{g}"])
        else:
            p.step("dve", lambda e: e.tensor_mul(cs("t2"), cs("var"), cs("rs")),
                   reads=[f"stc:var:{g}", f"stc:rs:{g}"], writes=[f"stc:t2:{g}"])
            p.step("dve", lambda e: e.tensor_mul(cs("t2"), cs("t2"), cs("rs")),
                   reads=[f"stc:rs:{g}", f"stc:t2:{g}"], writes=[f"stc:t2:{g}"])
            p.step("act", lambda e: e.activation(
                cs("t1"), cs("t2"), AF.Ln, bias=epsc[:, 0:1]),
                reads=[f"stc:t2:{g}", "epsc"], writes=[f"stc:t1:{g}"])
            p.step("act", lambda e: e.activation(cs("t3"), cs("t1"),
                                                 AF.Exp, scale=-0.5),
                   reads=[f"stc:t1:{g}"], writes=[f"stc:t3:{g}"])
            p.step("dve", lambda e: e.tensor_mul(cs("A"), cs("rs"), cs("t3")),
                   reads=[f"stc:rs:{g}", f"stc:t3:{g}"], writes=[f"stc:A:{g}"])
        p.step("dve", lambda e: e.scalar_tensor_tensor(
            cs("B"), cs("m"), -1.0, cs("A"), ALU.mult, ALU.mult),
            reads=[f"stc:m:{g}", f"stc:A:{g}"], writes=[f"stc:B:{g}"])

    def _t0_slot_of(tt, og):
        if tt < HTT:
            if og < 3:
                return XO, og, f"xo:{og}"
            return XRB, tt % 2, f"xrb:{tt % 2}"
        s_ = (tt * 4 + og) % 3
        return XO, s_, f"xo:{s_}"

    def _t0_tile_stats(tt, ffn_next):
        for og in range(4):
            buf, s_, bkey = _t0_slot_of(tt, og)
            p.dma("sp" if og % 2 == 0 else "act",
                  (lambda o_, t_, s2, b_: lambda e: e.dma_start(
                      out=b_[:, s2 * 512:(s2 + 1) * 512],
                      in_=X[t_ * 128:(t_ + 1) * 128,
                            o_ * 512:(o_ + 1) * 512]))(og, tt, s_, buf),
                  writes=[bkey])
            p.step("dve", (lambda s2, c_, b_: lambda e: e.reduce_sum(
                SUM4[:, c_:c_ + 1], b_[:, s2 * 512:(s2 + 1) * 512],
                axis=AX.X))(s_, tt * 4 + og, buf),
                reads=[bkey], writes=[f"sum4:{tt}"])
            p.step("act", (lambda s2, n_, c_, b_: lambda e: e.activation(
                XN2[:, n_ * 512:(n_ + 1) * 512],
                b_[:, s2 * 512:(s2 + 1) * 512],
                AF.Square, accum_out=SSQ4[:, c_:c_ + 1]))(
                    s_, og % 2, tt * 4 + og, buf),
                reads=[bkey],
                writes=[f"xn2:{og % 2}", f"ssq4:{tt}"])
        emit_tstats1(ffn_next, tt)

    def emit_t0_h0(ffn_next, wff_gen=None):
        t0i = 0
        t0q = []
        for tt in range(HTT):
            _t0_tile_stats(tt, ffn_next)
            hb()
            if tt == 12:
                stage_wT_og(wk_a16, "wk_a16", 0, "wsA")
            for og in range(4):
                buf, s_, bkey = _t0_slot_of(tt, og)
                t2_norm(t0i, tt, f"t{tt}", buf, s_, bkey)
                t0q.append((tt, og))
                if t0i >= T2D:
                    t2_transpose(t0i - T2D, *t0q[t0i - T2D])
                t0i += 1
            if tt % 2 == 1:
                step_n(wff_gen, 1)
        for i in range(max(0, t0i - T2D), t0i):
            t2_transpose(i, *t0q[i])

    def t0_h1_gen(ffn_next):
        for tt in range(HTT, NTT):
            _t0_tile_stats(tt, ffn_next)
            yield

    WSBUF = {"wsA": WS0, "wsB": WS1}

    def stage_rows(src_og, src_key, wskey="wsA"):
        ws = WSBUF[wskey]
        for c in range(NC):
            p.dma("sp", (lambda c_, w_: lambda e: e.dma_start(
                out=w_[:, c_ * 512:(c_ + 1) * 512],
                in_=src_og[c_ * 128:(c_ + 1) * 128, :]))(c, ws),
                reads=[src_key], writes=[f"{wskey}:{c}"])

    def stage_wT_og(src16, src_key, og, wskey="wsA"):
        ws = WSBUF[wskey]
        for c in range(NC):
            p.dma("sp", (lambda c_, g_, w_: lambda e: e.dma_start_transpose(
                w_[:, c_ * 512:(c_ + 1) * 512],
                src16[g_ * 512:(g_ + 1) * 512, c_ * 128:(c_ + 1) * 128]))(c, og, ws),
                reads=[src_key], writes=[f"{wskey}:{c}"])

    def stage_tables(tg_abs, ring):
        p.dma("sp", (lambda r_, t_: lambda e: e.dma_start(
            out=CSR[:, r_ * 1024:r_ * 1024 + 512],
            in_=C_COS[:, t_ * 512:(t_ + 1) * 512]))(ring, tg_abs),
            writes=[f"csrc:{ring}"])
        p.dma("sp", (lambda r_, t_: lambda e: e.dma_start(
            out=CSR[:, r_ * 1024 + 512:r_ * 1024 + 1024],
            in_=C_SIN[:, t_ * 512:(t_ + 1) * 512]))(ring, tg_abs),
            writes=[f"csrs:{ring}"])

    def proj_mms(pbi, och, tgl, wskey="wsA"):
        ws = WSBUF[wskey]
        for c in range(NC):
            p.step("pe", (lambda c_, o_, t_, i_, w_: lambda e: e.matmul(
                pb[i_][:, 0:512],
                w_[:, c_ * 512 + o_ * 128:c_ * 512 + (o_ + 1) * 128],
                xn_fm[:, c_ * TH + t_ * 512:c_ * TH + (t_ + 1) * 512],
                start=(c_ == 0), stop=(c_ == 15)))(c, och, tgl, pbi, ws),
                reads=[f"{wskey}:{c}", f"xnfm:{c}"], writes=[f"pb{pbi}"])

    def rope_elu_evac(pbi, ring, dst, dst_key, tg_abs, rr):
        cosap = CSR[:, ring * 1024:ring * 1024 + 512]
        sinap = CSR[:, ring * 1024 + 512:ring * 1024 + 1024]
        b0, b1, b2, b3 = rr * 4, rr * 4 + 1, rr * 4 + 2, rr * 4 + 3
        k0, k1, k2, k3 = (f"rb:{rr}:0", f"rb:{rr}:1", f"rb:{rr}:2", f"rb:{rr}:3")
        p.step("dve", (lambda i_, c_, a_: lambda e: e.tensor_mul(
            rb(a_), pb[i_][:, 0:512], c_))(pbi, cosap, b0),
            reads=[f"pb{pbi}", f"csrc:{ring}"], writes=[k0])
        p.step("dve", (lambda i_, s_, a_: lambda e: e.tensor_mul(
            rb(a_)[0:64, :], pb[i_][64:128, 0:512], s_[64:128, :]))(pbi, sinap, b1),
            reads=[f"pb{pbi}", f"csrs:{ring}"], writes=[k1 + "lo"])
        p.step("dve", (lambda i_, s_, a_: lambda e: e.tensor_mul(
            rb(a_)[64:128, :], pb[i_][0:64, 0:512], s_[0:64, :]))(pbi, sinap, b1),
            reads=[f"pb{pbi}", f"csrs:{ring}"], writes=[k1 + "hi"])
        p.step("dve", (lambda a_, x_, y_: lambda e: e.tensor_add(
            rb(a_), rb(x_), rb(y_)))(b2, b0, b1),
            reads=[k0, k1 + "lo", k1 + "hi"], writes=[k2])
        p.step("act", (lambda a_, x_: lambda e: e.activation(
            rb(a_), rb(x_), AF.Relu, scale=-1.0))(b3, b2),
            reads=[k2], writes=[k3])
        p.step("act", (lambda a_, x_: lambda e: e.activation(
            rb(a_), rb(x_), AF.Exp, scale=-1.0))(b0, b3),
            reads=[k3], writes=[k0])
        ps_slot = rr
        p.step("dve", (lambda s_, x_, y_: lambda e: e.scalar_tensor_tensor(
            XT[:, s_ * 512:(s_ + 1) * 512], rb(x_), 0.0, rb(y_),
            ALU.max, ALU.add))(ps_slot, b2, b0),
            reads=[k0, k2], writes=[f"xt:{ps_slot}"])
        p.dma("pool", (lambda s_, t_: lambda e: e.dma_start(
            out=dst[:, t_ * 512:(t_ + 1) * 512],
            in_=XT[:, s_ * 512:(s_ + 1) * 512]))(ps_slot, tg_abs),
            reads=[f"xt:{ps_slot}"], writes=[dst_key])

    def emit_attn_a(pre, bg0=None):
        wsrot = [1]

        def nextws():
            wsrot[0] ^= 1
            return "wsA" if wsrot[0] == 0 else "wsB"

        def q_proj_og(half, og):
            wk_ = nextws()
            stage_wT_og(wq_a16, "wq_a16", og, wk_)
            for tgl in range(4):
                tg_abs = half * 4 + tgl
                ring = 0
                stage_tables(tg_abs, ring)
                for och in range(4):
                    h = og * 4 + och
                    pbi = (tg_abs * 4 + och) % 2
                    proj_mms(pbi, och, tgl, wk_)
                    rope_elu_evac(pbi, ring, qh_a[h], f"qh_a:{h}", tg_abs,
                                  (tg_abs * 4 + och) % 2)
                    yield

        def k_proj(half):
            wk_ = nextws()
            if half != 0:
                stage_wT_og(wk_a16, "wk_a16", 0, wk_)
            for tgl in range(4):
                tg_abs = half * 4 + tgl
                ring = 0
                stage_tables(tg_abs, ring)
                for och in range(4):
                    pbi = (tg_abs * 4 + och) % 2
                    proj_mms(pbi, och, tgl, wk_)
                    rope_elu_evac(pbi, ring, kh_a[och], f"kh_a:{och}", tg_abs,
                                  (tg_abs * 4 + och) % 2)
                    yield

        def v_proj(half):
            wk_ = nextws()
            stage_wT_og(wv_a16, "wv_a16", 0, wk_)
            wsv = WSBUF[wk_]
            for ttl in range(HTT):
                tt = half * HTT + ttl
                pbi = tt % 2
                for c in range(NC):
                    p.step("pe", (lambda c_, t_, i_, w_: lambda e: e.matmul(
                        pb[i_][:, 0:512],
                        xn_fm[:, c_ * TH + t_ * 128:c_ * TH + (t_ + 1) * 128],
                        w_[:, c_ * 512:(c_ + 1) * 512],
                        start=(c_ == 0), stop=(c_ == 15)))(c, ttl, pbi, wsv),
                        reads=[f"{wk_}:{c}", f"xnfm:{c}"], writes=[f"pb{pbi}"])
                aslot = 2 + tt % 2
                p.step("act", (lambda s_, i_: lambda e: e.copy(
                    XT[:, s_ * 512:(s_ + 1) * 512], pb[i_][:, 0:512]))(aslot, pbi),
                    reads=[f"pb{pbi}"], writes=[f"xt:{aslot}"])
                for kvj in range(4):
                    p.dma("pool", (lambda s_, k_, t_: lambda e: e.dma_start(
                        out=vh_a[k_][t_ * 128:(t_ + 1) * 128, :],
                        in_=XT[:, s_ * 512 + k_ * 128:s_ * 512 + (k_ + 1) * 128]))(
                            aslot, kvj, tt),
                        reads=[f"xt:{aslot}"], writes=[f"vh_a:{kvj}"])
                yield

        def scan_kv(kv):
            yield
            for part in range(4):
                p.dma("sp", (lambda k_, i_: lambda e: e.dma_start(
                    out=G0[:, i_ * 1024:(i_ + 1) * 1024],
                    in_=kh_a[k_][:, i_ * 1024:(i_ + 1) * 1024]))(kv, part),
                    reads=[f"kh_a:{kv}"], writes=[f"g0:{part}"])
                p.dma("pool", (lambda k_, i_: lambda e: e.dma_start(
                    out=G1[:].rearrange("p (i j) -> p i j", j=128)[
                        :, i_ * 8:(i_ + 1) * 8, :],
                    in_=vh_a[k_].rearrange("(i p) j -> p i j", p=128)[
                        :, i_ * 8:(i_ + 1) * 8, :]))(kv, part),
                    reads=[f"vh_a:{kv}"], writes=[f"g1:{part}"])
            for i in range(32):
                p.dma("sp", (lambda k_, i_: lambda e: e.dma_start_transpose(
                    G2[:, i_ * 128:(i_ + 1) * 128],
                    kh_a[k_][:, i_ * 128:(i_ + 1) * 128]))(kv, i),
                    reads=[f"kh_a:{kv}"], writes=[f"g2:{i // 8}"])
            p.step("dve", lambda e: e.memset(ZC[:, 0:1], 0.0), writes=["z32"])
            p.step("dve", lambda e: e.memset(ST32[:], 0.0), writes=["st32"])
            for si in range(16):
                ca, cb = 2 * si, 2 * si + 1
                qp = si // 4
                p.step("pe", (lambda c_: lambda e: e.matmul(
                    pb[6][:, 0:128], G2[:, c_ * 128:(c_ + 1) * 128],
                    G1[:, c_ * 128:(c_ + 1) * 128],
                    start=True, stop=False))(ca),
                    reads=[f"g2:{qp}", f"g1:{qp}"], writes=["pb6"])
                p.step("pe", (lambda c_: lambda e: e.matmul(
                    pb[6][:, 0:128], G2[:, c_ * 128:(c_ + 1) * 128],
                    G1[:, c_ * 128:(c_ + 1) * 128],
                    start=False, stop=True))(cb),
                    reads=[f"g2:{qp}", f"g1:{qp}"], writes=["pb6"])
                p.step("dve", lambda e: e.tensor_add(
                    ST32[:], ST32[:], pb[6][:, 0:128]),
                    reads=["pb6", "st32"], writes=["st32"])
                p.step("act", (lambda s_: lambda e: e.copy(
                    G2[:, s_ * 128:(s_ + 1) * 128], ST32[:]))(si),
                    reads=["st32"], writes=[f"g2:{si // 8}"])
                p.step("dve", (lambda s_: lambda e: e.reduce_sum(
                    ZC[:, 1:2], G0[:, s_ * 256:(s_ + 1) * 256],
                    axis=AX.X))(si),
                    reads=[f"g0:{qp}"], writes=["ztmp"])
                p.step("dve", lambda e: e.tensor_add(
                    ZC[:, 0:1], ZC[:, 0:1], ZC[:, 1:2]),
                    reads=["ztmp", "z32"], writes=["z32"])
                p.step("dve", (lambda s_: lambda e: e.tensor_copy(
                    ZC[:, 4 + s_:5 + s_], ZC[:, 0:1]))(si),
                    reads=["z32"], writes=[f"zs:{si}"])
                yield
            for pp in range(2):
                hA, hB = kv * 4 + 2 * pp, kv * 4 + 2 * pp + 1

                def qwin_load(w):
                    ws_ = w % 4
                    p.dma("sp", (lambda w_, s_: lambda e: e.dma_start(
                        out=QF2[:, s_ * 512:s_ * 512 + 256],
                        in_=qh_a[hA][:, w_ * 256:(w_ + 1) * 256]))(w, ws_),
                        reads=[f"qh_a:{hA}"], writes=[f"qw:{ws_}"])
                    p.dma("pool", (lambda w_, s_: lambda e: e.dma_start(
                        out=QF2[:, s_ * 512 + 256:(s_ + 1) * 512],
                        in_=qh_a[hB][:, w_ * 256:(w_ + 1) * 256]))(w, ws_),
                        reads=[f"qh_a:{hB}"], writes=[f"qw:{ws_}"])

                qwin_load(0)
                pend = [None]

                def flush_epi():
                    if pend[0] is not None:
                        fn = pend[0]
                        pend[0] = None
                        fn()

                for si in range(17):
                    if si < 16:
                        ca, cb = 2 * si, 2 * si + 1
                        qp = si // 4
                        qs = si % 4
                        rhs_full = QF2[:, qs * 512:(qs + 1) * 512]
                        rhs_half = QF2[:, qs * 512:(qs + 1) * 512].rearrange(
                            "p (a b) -> p a b", b=256)[:, :, 128:256]
                        pslot = si % 2
                        qkeys = [f"qw:{qs}"]
                        p.step("pe", (lambda c_, r_, q_, ks: lambda e: e.matmul(
                            pb[2][:, 0:512], G0[:, c_ * 128:(c_ + 1) * 128], r_,
                            start=True, stop=True))(ca, rhs_full, qp, qkeys),
                            reads=[f"g0:{qp}"] + qkeys, writes=["pb2"])
                        if si < 15:
                            qwin_load(si + 1)
                        p.step("act", (lambda s_: lambda e: e.copy(
                            PST[:, s_ * 512:(s_ + 1) * 512],
                            pb[2][:, 0:512]))(pslot),
                            reads=["pb2"], writes=[f"pst:{pslot}"])
                        p.step("dve", (lambda s_: lambda e: e.tensor_mul(
                            PST[:, s_ * 512:(s_ + 1) * 512].rearrange(
                                "p (a b) -> p a b", b=256)[:, :, 0:128],
                            PST[:, s_ * 512:(s_ + 1) * 512].rearrange(
                                "p (a b) -> p a b", b=256)[:, :, 0:128],
                            maskha[:].rearrange(
                                "p (a b) -> p a b", b=256)[:, :, 0:128]))(pslot),
                            reads=[f"pst:{pslot}"], writes=[f"pst:{pslot}"])
                        p.step("pe", (lambda c_, r_: lambda e: e.matmul(
                            pb[2][:, 0:256], G0[:, c_ * 128:(c_ + 1) * 128], r_,
                            start=True, stop=True))(cb, rhs_half),
                            reads=[f"g0:{qp}"] + qkeys, writes=["pb2"])
                        p.step("dve", (lambda s_: lambda e: e.tensor_mul(
                            PST[:, (2 + s_) * 512:(2 + s_) * 512 + 256].rearrange(
                                "p (a b) -> p a b", b=128),
                            pb[2][:, 0:256].rearrange("p (a b) -> p a b", b=128),
                            maskha[:].rearrange(
                                "p (a b) -> p a b", b=256)[:, :, 0:128]))(pslot),
                            reads=["pb2", "mha"], writes=[f"pstm:{pslot}"])
                    hb()
                    if si >= 1:
                        sj = si - 1
                        caj, cbj = 2 * sj, 2 * sj + 1
                        qpj = sj // 4
                        qsj = sj % 4
                        rhs_fj = QF2[:, qsj * 512:(qsj + 1) * 512]
                        nb = 3 + (sj % 2)
                        psj = sj % 2
                        dr_ = sj % 2
                        qkj = [f"qw:{qsj}"]
                        papA = PST[:, psj * 512:(psj + 1) * 512]
                        papB = PST[:, (2 + psj) * 512:(2 + psj) * 512 + 256]
                        flush_epi()
                        if sj > 0:
                            p.step("act", (lambda s_: lambda e:
                                   e.activation(
                                       PST[:, 1280 + (s_ % 2) * 128:
                                           1280 + (s_ % 2 + 1) * 128],
                                       ONESM[:], AF.Identity,
                                       scale=ZC[:, 3 + s_:4 + s_]))(sj),
                                reads=["onesm", f"zs:{sj - 1}"],
                                writes=[f"zm:{sj % 2}"])
                            p.step("pe", (lambda r_, b_, s_: lambda e: e.matmul(
                                pb[b_][:, 0:512],
                                G2[:, (s_ - 1) * 128:s_ * 128], r_,
                                start=True, stop=False))(rhs_fj, nb, sj),
                                reads=[f"g2:{(sj - 1) // 8}"] + qkj,
                                writes=[f"pb{nb}"])
                            p.step("pe", (lambda r_, s_: lambda e: e.matmul(
                                pb[5][:, 0:512],
                                PST[:, 1280 + (s_ % 2) * 128:
                                    1280 + (s_ % 2 + 1) * 128], r_,
                                start=True, stop=False))(rhs_fj, sj),
                                reads=[f"zm:{sj % 2}"] + qkj, writes=["pb5"])
                        p.step("pe", (lambda c_, a_, b_, f_: lambda e: e.matmul(
                            pb[b_][:, 0:512], G1[:, c_ * 128:(c_ + 1) * 128],
                            a_, start=f_, stop=False))(caj, papA, nb, sj == 0),
                            reads=[f"g1:{qpj}", f"pst:{psj}"],
                            writes=[f"pb{nb}"])
                        p.step("pe", (lambda c_, a_, b_: lambda e: e.matmul(
                            pb[b_][:].rearrange("p (h q) -> p h q", h=2)[
                                :, :, 128:256],
                            G1[:, c_ * 128:(c_ + 1) * 128], a_,
                            start=False, stop=True))(cbj, papB, nb),
                            reads=[f"g1:{qpj}", f"pstm:{psj}"],
                            writes=[f"pb{nb}"])
                        p.step("pe", (lambda a_, f_: lambda e: e.matmul(
                            pb[5][:, 0:512], ONESM[:], a_,
                            start=f_, stop=False))(papA, sj == 0),
                            reads=["onesm", f"pst:{psj}"],
                            writes=["pb5"])
                        p.step("pe", (lambda a_: lambda e: e.matmul(
                            pb[5][:].rearrange("p (h q) -> p h q", h=2)[
                                :, :, 128:256],
                            ONESM[:], a_, start=False, stop=True))(papB),
                            reads=["onesm", f"pstm:{psj}"], writes=["pb5"])

                        def mk_epi(si_, dr2, nb2, hA_, hB_):
                            def epi():
                                p.step("act", lambda e: e.activation(
                                    rb(dr2 * 4), pb[5][:, 0:512], AF.Identity,
                                    bias=epsc[:, 1:2]),
                                    reads=["pb5", "epsca"],
                                    writes=[f"rb:{dr2}:0"])
                                p.step("dve", lambda e: e.reciprocal(
                                    rb(dr2 * 4 + 2), rb(dr2 * 4)),
                                    reads=[f"rb:{dr2}:0"],
                                    writes=[f"rb:{dr2}:2"])
                                p.step("dve", lambda e: e.tensor_mul(
                                    AOT[:, dr2 * 512:(dr2 + 1) * 512],
                                    pb[nb2][:, 0:512], rb(dr2 * 4 + 2)),
                                    reads=[f"pb{nb2}", f"rb:{dr2}:2"],
                                    writes=[f"aot:{dr2}"])
                                for hj, h in enumerate((hA_, hB_)):
                                    p.dma("pool", (lambda j_, h_: lambda e:
                                          e.dma_start(
                                              out=aoh_a[h_][:, si_ * 256:
                                                            (si_ + 1) * 256],
                                              in_=AOT[:, dr2 * 512 + j_ * 256:
                                                      dr2 * 512 +
                                                      (j_ + 1) * 256]))(
                                                  hj, h),
                                        reads=[f"aot:{dr2}"],
                                        writes=[f"aoh_a:{h}:{2 * si_}",
                                                f"aoh_a:{h}:{2 * si_ + 1}"])
                            return epi
                        pend[0] = mk_epi(sj, dr_, nb, hA, hB)
                    yield
                flush_epi()

        bg = bg0 if bg0 is not None else BG()
        for half in range(2):
            if half == 1:
                drain(pre.get("h1"))
            drive(k_proj(half), bg, 4)
            drive(v_proj(half), bg, 4)
            for og in range(4):
                if half == 0 and og == 3:
                    drive_t2(q_proj_og(half, og), bg, 4,
                             pre.get("h1"), 4, 16)
                else:
                    drive(q_proj_og(half, og), bg, 4)
                if half == 1:
                    bg.set(scan_kv(og))
        return bg

    def emit_attn_b(pre):
        emit_attn_b_setup()
        wsrot = [1]

        def nextws():
            wsrot[0] ^= 1
            return "wsA" if wsrot[0] == 0 else "wsB"

        def qk_og(half, og, wog, wkey, dsts, dkey, bias, bias_key):
            wk_ = nextws()
            stage_rows(wog[og], f"{wkey}:{og}", wk_)
            for tgl in range(4):
                tg_abs = half * 4 + tgl
                for och in range(4):
                    h = og * 4 + och
                    pbi = (tg_abs * 4 + och) % 2
                    if half == 0:
                        hb()
                    proj_mms(pbi, och, tgl, wk_)
                    ps_slot = (tg_abs * 4 + och) % 2
                    p.step("act", (lambda i_, s_, h_, b_: lambda e:
                           e.activation(
                               XT[:, s_ * 512:(s_ + 1) * 512],
                               pb[i_][:, 0:512], AF.Identity,
                               bias=b_[:, h_:h_ + 1]))(pbi, ps_slot, h, bias),
                        reads=[f"pb{pbi}", bias_key],
                        writes=[f"xt:{ps_slot}"])
                    p.dma("pool", (lambda s_, h_, t_, d_: lambda e:
                          e.dma_start(
                              out=d_[:, t_ * 512:(t_ + 1) * 512],
                              in_=XT[:, s_ * 512:(s_ + 1) * 512]))(
                                  ps_slot, h, tg_abs, dsts[h]),
                        reads=[f"xt:{ps_slot}"], writes=[f"{dkey}:{h}"])
                    yield

        def v_og(half, og):
            wk_ = nextws()
            stage_rows(wv_b16[og], f"wv_b16:{og}", wk_)
            wsv = WSBUF[wk_]
            for ttl in range(HTT):
                tt = half * HTT + ttl
                pbi = tt % 2
                for c in range(NC):
                    p.step("pe", (lambda c_, t_, i_, w_: lambda e: e.matmul(
                        pb[i_][:, 0:512],
                        xn_fm[:, c_ * TH + t_ * 128:c_ * TH + (t_ + 1) * 128],
                        w_[:, c_ * 512:(c_ + 1) * 512],
                        start=(c_ == 0), stop=False))(c, ttl, pbi, wsv),
                        reads=[f"{wk_}:{c}", f"xnfm:{c}"], writes=[f"pb{pbi}"])
                p.step("pe", (lambda g_, i_: lambda e: e.matmul(
                    pb[i_][:, 0:512], ones_r16[:],
                    ROWS16[0:1, 4096 + g_ * 512:4096 + (g_ + 1) * 512],
                    start=False, stop=True))(og, pbi),
                    reads=["ones_r16", "bvrow"], writes=[f"pb{pbi}"])
                aslot = 2 + tt % 2
                p.step("act", (lambda s_, i_: lambda e: e.copy(
                    XT[:, s_ * 512:(s_ + 1) * 512], pb[i_][:, 0:512]))(
                        aslot, pbi),
                    reads=[f"pb{pbi}"], writes=[f"xt:{aslot}"])
                for hh in range(4):
                    h = og * 4 + hh
                    p.dma("pool", (lambda s_, j_, h_, t_: lambda e: e.dma_start(
                        out=vh_b[h_][t_ * 128:(t_ + 1) * 128, :],
                        in_=XT[:, s_ * 512 + j_ * 128:
                                s_ * 512 + (j_ + 1) * 128]))(aslot, hh, h, tt),
                        reads=[f"xt:{aslot}"], writes=[f"vh_b:{h}"])
                yield

        pend_f = [None]

        def flush_f():
            if pend_f[0] is not None:
                fn = pend_f[0]
                pend_f[0] = None
                fn()

        def flash_head(h):
            yield
            for part in range(4):
                p.dma("sp", (lambda h_, i_: lambda e: e.dma_start(
                    out=G0[:, i_ * 1024:(i_ + 1) * 1024],
                    in_=kh_b[h_][:, i_ * 1024:(i_ + 1) * 1024]))(h, part),
                    reads=[f"kh_b:{h}"], writes=[f"g0:{part}"])
                p.dma("pool", (lambda h_, i_: lambda e: e.dma_start(
                    out=G1[:].rearrange("p (i j) -> p i j", j=128)[
                        :, i_ * 8:(i_ + 1) * 8, :],
                    in_=vh_b[h_].rearrange("(i p) j -> p i j", p=128)[
                        :, i_ * 8:(i_ + 1) * 8, :]))(h, part),
                    reads=[f"vh_b:{h}"], writes=[f"g1:{part}"])
                p.dma("sp", (lambda h_, i_: lambda e: e.dma_start(
                    out=G2[:, i_ * 1024:(i_ + 1) * 1024],
                    in_=qh_b[h_][:, i_ * 1024:(i_ + 1) * 1024]))(h, part),
                    reads=[f"qh_b:{h}"], writes=[f"g2:{part}"])
            for qg in range(8):
                nkt = 4 * qg + 4
                ob = 4 if qg % 2 == 0 else 5
                db = 6
                dr_ = qg % 2
                DEP = 3
                for kt in range(nkt + DEP):
                    if kt < nkt:
                        sbank = (2, 3, 7)[kt % 3]
                        p.step("pe", (lambda k_, b_, q_: lambda e: e.matmul(
                            pb[b_][:, 0:512], G0[:, k_ * 128:(k_ + 1) * 128],
                            G2[:, q_ * 512:(q_ + 1) * 512],
                            start=True, stop=True))(kt, sbank, qg),
                            reads=[f"g0:{kt // 8}", f"g2:{qg // 2}"],
                            writes=[f"pb{sbank}"])
                        pslot = kt % 4
                        p.step("act", (lambda b_, s_: lambda e: e.activation(
                            PST[:, s_ * 512:(s_ + 1) * 512], pb[b_][:, 0:512],
                            AF.Exp, scale=SM_SCALE))(sbank, pslot),
                            reads=[f"pb{sbank}"], writes=[f"pst:{pslot}"])
                        if kt >= 4 * qg:
                            r = kt - 4 * qg
                            p.step("dve", (lambda s_, r_: lambda e: e.tensor_mul(
                                PST[:, s_ * 512:(s_ + 1) * 512],
                                PST[:, s_ * 512:(s_ + 1) * 512],
                                dmask[:, r_ * 512:(r_ + 1) * 512]))(pslot, r),
                                reads=[f"pst:{pslot}", "dm"],
                                writes=[f"pst:{pslot}"])
                    if kt == 1:
                        flush_f()
                    if kt >= DEP:
                        j = kt - DEP
                        js = j % 4
                        pk = f"pst:{js}"
                        pap = PST[:, js * 512:(js + 1) * 512]
                        p.step("pe", (lambda k_, a_, f_, l_, b_: lambda e:
                               e.matmul(
                                   pb[b_][:, 0:512],
                                   G1[:, k_ * 128:(k_ + 1) * 128], a_,
                                   start=f_, stop=l_))(
                                       j, pap, j == 0, j == nkt - 1, ob),
                            reads=[f"g1:{j // 8}", pk], writes=[f"pb{ob}"])
                        p.step("pe", (lambda a_, f_, l_, d_: lambda e: e.matmul(
                            pb[d_][:, 0:512], ONESM[:], a_,
                            start=f_, stop=l_))(pap, j == 0, j == nkt - 1, db),
                            reads=["onesm", pk], writes=[f"pb{db}"])
                    yield
                p.step("act", (lambda d_, b_: lambda e: e.copy(
                    rb(d_ * 4), pb[b_][:, 0:512]))(dr_, db),
                    reads=[f"pb{db}"], writes=[f"rb:{dr_}:0"])

                def mk_epi_f(h_, qg_, dr2, ob2):
                    def epi():
                        p.step("dve", lambda e: e.reciprocal(
                            rb(dr2 * 4 + 2), rb(dr2 * 4)),
                            reads=[f"rb:{dr2}:0"], writes=[f"rb:{dr2}:2"])
                        p.step("dve", lambda e: e.tensor_mul(
                            AOT[:, dr2 * 512:(dr2 + 1) * 512],
                            pb[ob2][:, 0:512], rb(dr2 * 4 + 2)),
                            reads=[f"pb{ob2}", f"rb:{dr2}:2"],
                            writes=[f"aot:{dr2}"])
                        p.dma("pool", lambda e: e.dma_start(
                            out=aoh_b[h_][:, qg_ * 512:(qg_ + 1) * 512],
                            in_=AOT[:, dr2 * 512:(dr2 + 1) * 512]),
                            reads=[f"aot:{dr2}"], writes=[f"aoh_b:{h_}:{qg_}"])
                    return epi
                pend_f[0] = mk_epi_f(h, qg, dr_, ob)

        def flash_og(og):
            for hh in range(4):
                yield from flash_head(og * 4 + hh)

        bg = BG()
        for half in range(2):
            if half == 1:
                drain(pre.get("h1"))
            for og in range(4):
                if half == 0 and og == 0:
                    step_n(pre.get("h0"), 16)
                    drive_t2(qk_og(half, og, wq_b16, "wq_b16", qh_b, "qh_b",
                                   bqc, "bqc"), bg, 9, pre.get("h0"), 4, 16)
                else:
                    drive(qk_og(half, og, wq_b16, "wq_b16", qh_b, "qh_b",
                                bqc, "bqc"), bg, 9)
                drive(qk_og(half, og, wk_b16, "wk_b16", kh_b, "kh_b",
                            bkc, "bkc"), bg, 9)
                if half == 0 and og == 3:
                    drive_t2(v_og(half, og), bg, 9, pre.get("h1"), 1, 4)
                else:
                    drive(v_og(half, og), bg, 9)
                if half == 1:
                    bg.set(flash_og(og))
        bg.finish()
        flush_f()
        return bg

    def emit_outproj_attn(aoh, aoh_key, wor_stage, with_bo, x_src, x_src_key,
                          x_dst, x_dst_key, final, boundary_ffn=(False,),
                          bg=None, subkeys=None, hb_ok=False):
        if subkeys is None:
            subkeys = lambda hh, blk: [f"{aoh_key}:{hh}:{blk}"]
        for s in range(64):
            wor_stage(s)
            if bg is not None:
                bg.step(3)
        for blk in range(8):
            need = [k for hh in range(NH) for k in subkeys(hh, blk)]
            if bg is not None:
                bg.step(12)
                bg.step_until(p, need)
            for hh in range(NH):
                p.dma("sp", (lambda h_, b_: lambda e: e.dma_start(
                    out=WS0[:, h_ * 512:(h_ + 1) * 512],
                    in_=aoh[h_][:, b_ * 512:(b_ + 1) * 512]))(hh, blk),
                    reads=subkeys(hh, blk), writes=[f"wsA:{hh}"])
            for og in range(4):
                for ttl in range(4):
                    if hb_ok:
                        hb()
                    tt = blk * 4 + ttl
                    pbi = tt % 2
                    for hh in range(NH):
                        s = og * 16 + hh
                        p.step("pe", (lambda h_, t_, i_, s_: lambda e: e.matmul(
                            pb[i_][:, 0:512],
                            WS0[:, h_ * 512 + t_ * 128:h_ * 512 + (t_ + 1) * 128],
                            xn_fm[:, s_ * 512:(s_ + 1) * 512],
                            start=(h_ == 0),
                            stop=(not with_bo and h_ == NH - 1)))(
                                hh, ttl, pbi, s),
                            reads=[f"wsA:{hh}", f"xnfm:{s // 4}"],
                            writes=[f"pb{pbi}"])
                    if with_bo:
                        p.step("pe", (lambda g_, i_: lambda e: e.matmul(
                            pb[i_][:, 0:512], ones_r16[:],
                            ROWS16[0:1, 2048 + g_ * 512:2048 + (g_ + 1) * 512],
                            start=False, stop=True))(og, pbi),
                            reads=["ones_r16", "borow"], writes=[f"pb{pbi}"])
                    emit_t1_tile(tt, og, pbi, x_src, x_src_key,
                                 x_dst, x_dst_key, final)
            if not final:
                emit_tstats(boundary_ffn[0], blk)
        if bg is not None:
            bg.finish()

    def emit_ffn_mm(x_src, x_src_key, x_dst, x_dst_key, final,
                    boundary_ffn=(False,), pre=None):
        pre = pre or {}
        wsrot = [1]

        def nextws():
            wsrot[0] ^= 1
            return "wsA" if wsrot[0] == 0 else "wsB"

        for half in range(2):
            if half == 1:
                drain(pre.get("h0"))
                drain(pre.get("h1"))
            for og in range(4):
                wk_ = nextws()
                stage_rows(wffh[og], f"wff:{og}", wk_)
                wsv = WSBUF[wk_]
                for ttl in range(HTT):
                    hb()
                    if half == 0 and og == 0:
                        step_n(pre.get("h0"), 4)
                    tt = half * HTT + ttl
                    pbi = tt % 4
                    for c in range(NC):
                        p.step("pe", (lambda c_, t_, i_, w_: lambda e: e.matmul(
                            pb[i_][:, 0:512],
                            xn_fm[:, c_ * TH + t_ * 128:c_ * TH + (t_ + 1) * 128],
                            w_[:, c_ * 512:(c_ + 1) * 512],
                            start=(c_ == 0), stop=False))(c, ttl, pbi, wsv),
                            reads=[f"{wk_}:{c}", f"xnfm:{c}"], writes=[f"pb{pbi}"])
                    p.step("pe", (lambda g_, i_: lambda e: e.matmul(
                        pb[i_][:, 0:512], ones_r16[:],
                        ROWS16[0:1, g_ * 512:(g_ + 1) * 512],
                        start=False, stop=True))(og, pbi),
                        reads=["ones_r16", "rrow"], writes=[f"pb{pbi}"])
                    emit_t1_tile(tt, og, pbi, x_src, x_src_key,
                                 x_dst, x_dst_key, final)
                    if og == 3 and not final and ttl % 4 == 3:
                        emit_tstats(boundary_ffn[0], tt // 4)
                    if half == 0 and og == 3:
                        step_n(pre.get("h1"), 4)

    cur_src, cur_key = X, "x_in"
    n_sub = len(sublayers)
    pre = {}
    for si, sl in enumerate(sublayers):
        final = (si == n_sub - 1)
        x_dst = OUT if final else xh[si % 2]
        x_dst_key = "out" if final else f"xh{si % 2}"
        bffn = (not final and sublayers[si + 1].startswith("ffn"),)
        if si == 0:
            wffg = wff_scan_gen() if any_ffn else None
            emit_t0_h0(sl.startswith("ffn"), wffg)
            bg0 = BG()
            bg0.set(t0_h1_gen(sl.startswith("ffn")))
            if wffg is not None:
                bg0.set(wffg)
            pre = {"h1": t2_gen(X, "x_in", range(4, 8), t0_keys=True)}
        if sl in ("t0", "wff"):
            continue
        if sl.startswith("ffn"):
            emit_ffn_mm(cur_src, cur_key, x_dst, x_dst_key, final, bffn,
                        pre=pre)
        elif sl == "attn_a":
            bg_a = emit_attn_a(pre, bg0)
            if any_ffn:
                bg_a.set(wff_xform_gen())
            emit_outproj_attn(aoh_a, "aoh_a", wor_stage_a, False,
                              cur_src, cur_key, x_dst, x_dst_key, final, bffn,
                              bg=bg_a, hb_ok=True,
                              subkeys=lambda hh, blk: [
                                  f"aoh_a:{hh}:{4 * blk + j}" for j in range(4)])
        elif sl == "attn_b":
            bg_b = emit_attn_b(pre)
            emit_outproj_attn(aoh_b, "aoh_b", wor_stage_b, True,
                              cur_src, cur_key, x_dst, x_dst_key, final, bffn,
                              bg=bg_b)
        if not final:
            pre = {"h0": t2_gen(x_dst, x_dst_key, range(0, 4)),
                   "h1": t2_gen(x_dst, x_dst_key, range(4, 8))}
        cur_src, cur_key = x_dst, x_dst_key

    if debug:
        want = debug if isinstance(debug, (set, list, tuple)) else None
        sel = lambda nm: want is None or nm in want
        dbg_specs = [
            ("dbg_wff0", wffh[0], [D, 512], BF16, ["wff:0"]),
            ("dbg_qa0", qh_a[0], [128, T], BF16, ["qh_a:0"]),
            ("dbg_ka0", kh_a[0], [128, T], BF16, ["kh_a:0"]),
            ("dbg_va0", vh_a[0], [T, 128], BF16, ["vh_a:0"]),
            ("dbg_aoa0", aoh_a[0], [128, T], BF16, [f"aoh_a:0:{b}" for b in range(32)]),
            ("dbg_aob0", aoh_b[0], [128, T], BF16, [f"aoh_b:0:{b}" for b in range(8)]),
            ("dbg_aob5", aoh_b[5], [128, T], BF16, [f"aoh_b:5:{b}" for b in range(8)]),
            ("dbg_qb0", qh_b[0], [128, T], BF16, ["qh_b:0"]),
            ("dbg_kb0", kh_b[0], [128, T], BF16, ["kh_b:0"]),
            ("dbg_vb0", vh_b[0], [T, 128], BF16, ["vh_b:0"]),
        ]
        for nm, src, shp, dt, rkeys in dbg_specs:
            if not sel(nm):
                continue
            dout = nc.dram_tensor(nm, shp, dt, kind="ExternalOutput").ap()
            p.dma("pool", (lambda d_, s_: lambda e: e.dma_start(out=d_, in_=s_))(
                dout, src), reads=rkeys, writes=[nm])
    block = ctx.enter_context(nc.Block())
    p.finalize(block, sems)
    ctx.close()
    return nc


_NC_CACHE = {}


def kernel(**inputs):
    key = tuple(SUBLAYERS)
    if key not in _NC_CACHE:
        _NC_CACHE[key] = build_program(SUBLAYERS)
    nc = _NC_CACHE[key]
    consts = _host_constants()
    x = np.ascontiguousarray(np.asarray(inputs["L0_attn_a_x"], dtype=np.float32))
    base = dict(
        wq_a=np.asarray(inputs["L0_attn_a_Wq"], np.float32),
        wk_a=np.asarray(inputs["L0_attn_a_Wk"], np.float32),
        wv_a=np.asarray(inputs["L0_attn_a_Wv"], np.float32),
        wo_a=np.asarray(inputs["L0_attn_a_Wo"], np.float32),
        wq_b=np.asarray(inputs["L0_attn_b_Wq"], np.float32),
        wk_b=np.asarray(inputs["L0_attn_b_Wk"], np.float32),
        wv_b=np.asarray(inputs["L0_attn_b_Wv"], np.float32),
        wo_b=np.asarray(inputs["L0_attn_b_Wo"], np.float32),
        bq_b=np.asarray(inputs["L0_attn_b_bq"], np.float32),
        bk_b=np.asarray(inputs["L0_attn_b_bk"], np.float32),
        bv_b=np.asarray(inputs["L0_attn_b_bv"], np.float32),
        bo_b=np.asarray(inputs["L0_attn_b_bo"], np.float32),
        w_ffn=np.asarray(inputs["L0_ffn_weight"], np.float32),
        gamma=np.asarray(inputs["L0_ffn_gamma"], np.float32),
        beta=np.asarray(inputs["L0_ffn_beta"], np.float32),
        ln_w=np.asarray(inputs["L0_ffn_ln_weight"], np.float32),
        ln_b=np.asarray(inputs["L0_ffn_ln_bias"], np.float32),
        ident32=consts["ident32"], ident16=consts["ident16"],
        maskha=consts["maskha"],
        dmasks=consts["dmasks"],
        cos_tbl=consts["cos_tbl"], sin_tbl=consts["sin_tbl"],
    )
    in_maps = [dict(base, x=x[c]) for c in range(8)]
    res = run_bass_kernel_spmd(nc, in_maps, core_ids=list(range(8)))
    return np.stack([np.asarray(r["out"], dtype=np.float32)
                     for r in res.results])


# version 36 (final)
# speedup vs baseline: 1.0016x; 1.0016x over previous
import math
import numpy as np
import ml_dtypes
import concourse.bass as bass
import concourse.mybir as mybir
from concourse.bass_utils import run_bass_kernel_spmd
from contextlib import ExitStack

F32 = mybir.dt.float32
BF16 = mybir.dt.bfloat16
AF = mybir.ActivationFunctionType
ALU = mybir.AluOpType
AX = mybir.AxisListType

D = 2048
T = 4096
HD = 128
NH = 16
KVH = 4
NC = 16
NTT = 32
HTT = 16
TH = 2048
EPS_LN = 1e-5
EPS_A = 1e-6
SM_SCALE = 1.0 / math.sqrt(HD)

SUBLAYERS = ("attn_a", "ffn", "attn_b", "ffn2")
ENGINES = ("pe", "act", "dve", "pool", "sp")


class Res:
    __slots__ = ("writer", "readers")

    def __init__(self):
        self.writer = None
        self.readers = []


class Step:
    __slots__ = ("engine", "emit", "queue", "idx", "dma_idx", "deps",
                 "needs_sig", "sig_val")

    def __init__(self, engine, emit, queue):
        self.engine = engine
        self.emit = emit
        self.queue = queue
        self.deps = []
        self.needs_sig = False
        self.sig_val = None


class Prog:
    QENG = {"sp": "sp", "pool": "pool", "act": "act"}

    def __init__(self):
        self.steps = []
        self.res = {}
        self.dma_counts = {"sp": 0, "pool": 0, "act": 0}

    def r(self, key):
        if key not in self.res:
            self.res[key] = Res()
        return self.res[key]

    def step(self, engine, emit, reads=(), writes=(), queue=None):
        s = Step(engine, emit, queue)
        s.idx = len(self.steps)
        if queue is not None:
            s.dma_idx = self.dma_counts[queue]
            self.dma_counts[queue] += 1
        deps = set()
        for k in reads:
            r = self.r(k)
            if r.writer is not None:
                deps.add(r.writer)
        for k in writes:
            r = self.r(k)
            if r.writer is not None:
                deps.add(r.writer)
            for rd in r.readers:
                deps.add(rd)
        for d in deps:
            if d is s:
                continue
            if (d.queue is None and s.queue is None
                    and d.engine == s.engine == "pe"):
                continue
            s.deps.append(d)
        for k in reads:
            self.res[k].readers.append(s)
        for k in writes:
            r = self.res[k]
            r.writer = s
            r.readers = []
        self.steps.append(s)
        return s

    def dma(self, queue, emit, reads=(), writes=()):
        return self.step(self.QENG[queue], emit, reads=reads, writes=writes,
                         queue=queue)

    DMA_POOL = 32

    def finalize(self, block, sems):
        for s in self.steps:
            for d in s.deps:
                if d.queue is None:
                    d.needs_sig = True
        cnt = {e: 0 for e in ENGINES}
        for s in self.steps:
            if s.queue is None and s.needs_sig:
                cnt[s.engine] += 1
                s.sig_val = cnt[s.engine]
        per_eng = {e: [] for e in ENGINES}
        for s in self.steps:
            per_eng[s.engine].append(s)
        K = self.DMA_POOL

        def dma_ev(d):
            return (sems[f"dq_{d.queue}_{d.dma_idx % K}"],
                    16 * (d.dma_idx // K + 1))

        def make_body(ename):
            def body(eng):
                known = {}

                def wait(sm, val):
                    if known.get(sm, 0) < val:
                        eng.wait_ge(sm, val)
                        known[sm] = val

                for s in per_eng[ename]:
                    for d in s.deps:
                        if d.queue is None:
                            wait(sems[d.engine], d.sig_val)
                        else:
                            wait(*dma_ev(d))
                    if s.queue is not None:
                        sm = sems[f"dq_{s.queue}_{s.dma_idx % K}"]
                        gen = s.dma_idx // K
                        if gen > 0:
                            wait(sm, 16 * gen)
                        ins = s.emit(eng)
                        ins.then_inc(sm, 16)
                    else:
                        ins = s.emit(eng)
                        if s.needs_sig:
                            ins.then_inc(sems[s.engine], 1)
                if ename == "pool":
                    for q in ("pool", "sp", "act"):
                        n = self.dma_counts[q]
                        for i in range(min(K, n)):
                            gens = (n - 1 - i) // K + 1
                            wait(sems[f"dq_{q}_{i}"], 16 * gens)
            return body

        block.tensor(make_body("pe"))
        block.scalar(make_body("act"))
        block.vector(make_body("dve"))
        block.gpsimd(make_body("pool"))
        block.sync(make_body("sp"))


class BG:

    def __init__(self):
        self.q = []

    def set(self, g):
        self.q.append(g)

    @property
    def g(self):
        return self.q[0] if self.q else None

    def step(self, n=1):
        for _ in range(n):
            if not self.q:
                return
            try:
                next(self.q[0])
            except StopIteration:
                self.q.pop(0)

    def finish(self):
        while self.q:
            self.step()

    def step_until(self, prog, keys):
        while self.q and any(
                prog.res.get(k) is None or prog.res[k].writer is None
                for k in keys):
            self.step()


def drive(main_gen, bg, k):
    for _ in main_gen:
        if bg is not None:
            bg.step(k)
    return


def _host_constants():
    bf = ml_dtypes.bfloat16
    ident16 = np.eye(128, dtype=bf)
    ident32 = np.eye(128, dtype=np.float32)
    k = np.arange(128)[:, None]
    q = np.arange(128)[None, :]
    m = (k <= q).astype(np.float32)
    maskha = np.tile(np.concatenate([m, np.ones((128, 128), np.float32)], 1),
                     (1, 2)).astype(bf)
    qq = np.arange(512)[None, :]
    dmasks = np.concatenate(
        [((np.arange(128)[:, None] + r * 128) <= qq).astype(np.float32)
         for r in range(4)], axis=1).astype(bf)
    inv_freq = (1.0 / (10000.0 ** (np.arange(0, HD, 2, dtype=np.float32) / HD))
                ).astype(np.float32)
    t = np.arange(T, dtype=np.float32)
    freqs = np.outer(t, inv_freq).astype(np.float32)
    cos = np.cos(freqs).astype(np.float32).T
    sin = np.sin(freqs).astype(np.float32).T
    cos_tbl = np.concatenate([cos, cos], axis=0)
    sin_tbl = np.concatenate([sin, -sin], axis=0)
    return dict(ident32=ident32, ident16=ident16,
                maskha=maskha, dmasks=dmasks, cos_tbl=cos_tbl,
                sin_tbl=sin_tbl)


def build_program(sublayers=None, debug=False):
    if sublayers is None:
        sublayers = SUBLAYERS
    nc = bass.Bass("TRN2", target_bir_lowering=False)
    p = Prog()

    din = lambda n, s: nc.dram_tensor(n, s, F32, kind="ExternalInput").ap()
    X = din("x", [T, D])
    WqA, WkA, WvA, WoA = din("wq_a", [D, D]), din("wk_a", [512, D]), \
        din("wv_a", [512, D]), din("wo_a", [D, D])
    WqB, WkB, WvB, WoB = din("wq_b", [D, D]), din("wk_b", [D, D]), \
        din("wv_b", [D, D]), din("wo_b", [D, D])
    BqB, BkB, BvB, BoB = din("bq_b", [D]), din("bk_b", [D]), \
        din("bv_b", [D]), din("bo_b", [D])
    Wf, Gam, Bet, LnW, LnB = din("w_ffn", [D, D]), din("gamma", [D]), \
        din("beta", [D]), din("ln_w", [D]), din("ln_b", [D])
    C_I16 = nc.dram_tensor("ident16", [128, 128], BF16, kind="ExternalInput").ap()
    C_MHA = nc.dram_tensor("maskha", [128, 512], BF16, kind="ExternalInput").ap()
    C_DM = nc.dram_tensor("dmasks", [128, 2048], BF16, kind="ExternalInput").ap()
    C_I32 = din("ident32", [128, 128])
    C_COS = din("cos_tbl", [128, T])
    C_SIN = din("sin_tbl", [128, T])
    OUT = nc.dram_tensor("out", [T, D], F32, kind="ExternalOutput").ap()

    dr = lambda n, s, d: nc.dram_tensor(n, s, d, kind="Internal").ap()
    xh = [dr(f"x_hbm{i}", [T, D], F32) for i in range(2)]
    wmax_d = dr("wmax_d", [128], F32)
    wq_a16 = dr("wq_a16", [D, D], BF16)
    wk_a16 = dr("wk_a16", [512, D], BF16)
    wv_a16 = dr("wv_a16", [512, D], BF16)
    wo_a16 = dr("wo_a16", [D, D], BF16)
    wf16 = dr("wf16", [D, D], BF16)
    wq_b16 = [dr(f"wq_b16_{g}", [D, 512], BF16) for g in range(4)]
    wk_b16 = [dr(f"wk_b16_{g}", [D, 512], BF16) for g in range(4)]
    wv_b16 = [dr(f"wv_b16_{g}", [D, 512], BF16) for g in range(4)]
    wo_b16 = [dr(f"wo_b16_{g}", [D, 512], BF16) for g in range(4)]
    wffh = [dr(f"wff_og{g}", [D, 512], BF16) for g in range(4)]
    qh_a = [dr(f"qa_h{h}", [128, T], BF16) for h in range(NH)]
    kh_a = [dr(f"ka_h{h}", [128, T], BF16) for h in range(KVH)]
    vh_a = [dr(f"va_h{h}", [T, 128], BF16) for h in range(KVH)]
    aoh_a = [dr(f"aoa_h{h}", [128, T], BF16) for h in range(NH)]
    qh_b = [dr(f"qb_h{h}", [128, T], BF16) for h in range(NH)]
    kh_b = [dr(f"kb_h{h}", [128, T], BF16) for h in range(NH)]
    vh_b = [dr(f"vb_h{h}", [T, 128], BF16) for h in range(NH)]
    aoh_b = [dr(f"aob_h{h}", [128, T], BF16) for h in range(NH)]

    ctx = ExitStack()
    sb = lambda n, s, d: ctx.enter_context(nc.sbuf_tensor(n, s, d))
    psmt = lambda n, s, d: ctx.enter_context(nc.psum_tensor(n, s, d))
    mksem = lambda n: ctx.enter_context(nc.semaphore(n))

    xn_fm = sb("xn_fm", [128, NC * TH], BF16)
    G0 = sb("g0", [128, T], BF16)
    G1 = sb("g1", [128, T], BF16)
    G2 = sb("g2", [128, T], BF16)
    QF2 = sb("qf2", [128, 2048], BF16)
    WS0 = sb("ws0", [128, 8192], BF16)
    WS1 = sb("ws1", [128, 8192], BF16)
    XO = sb("xo", [128, 2048], F32)
    XN2 = sb("xn2", [128, 1024], F32)
    XR = sb("xr", [128, 512], F32)
    XT = sb("xt", [128, 2048], BF16)
    RB = sb("rb", [128, 4096], F32)
    CSR = sb("csr", [128, 1024], F32)
    PST = sb("pst", [128, 2048], BF16)
    AOT = sb("aot", [128, 1024], BF16)
    SUM4 = sb("sum4", [128, 128], F32)
    SSQ4 = sb("ssq4", [128, 128], F32)
    STC = sb("stc", [128, 360], F32)
    ROWS16 = sb("rows16", [1, 6144], BF16)
    ONESM = sb("onesm", [128, 128], BF16)
    WROW = sb("wrow", [1, 128], F32)
    maskha = sb("mha", [128, 512], BF16)
    dmask = sb("dmask", [128, 2048], BF16)
    ones_r16 = sb("ones_r16", [1, 128], BF16)
    epsc = sb("epsc", [128, 2], F32)
    bqc = sb("bqc", [128, 16], F32)
    bkc = sb("bkc", [128, 16], F32)
    lnwc = sb("lnwc", [128, 16], F32)
    lnbc16 = sb("lnbc16", [128, 16], BF16)
    ST32 = sb("st32", [128, 128], F32)
    ZC = sb("zc", [128, 20], F32)
    XRB = sb("xrb", [128, 2048], F32)
    IDT = sb("idt", [128, 128], BF16)

    pb = [psmt(f"pb{i}", [128, 512], F32) for i in range(8)]

    sems = {e: mksem("sem_" + e) for e in ENGINES}
    for q in ("sp", "pool", "act"):
        for i in range(Prog.DMA_POOL):
            sems[f"dq_{q}_{i}"] = mksem(f"dq_{q}_{i}")

    XO_ALL = ["xo:0", "xo:1", "xo:2", "xo:3"]

    def rb(i):
        return RB[:, i * 512:(i + 1) * 512]

    stc_cols = {}
    _stc_next = [0]

    def stc(name, n=32):
        if name not in stc_cols:
            stc_cols[name] = (_stc_next[0], n)
            _stc_next[0] += n
            assert _stc_next[0] <= 360, "STC overflow"
        o, nn = stc_cols[name]
        assert nn == n
        return STC[:, o:o + n]

    def hb():
        p.step("pe", lambda e: e.matmul(
            pb[7][0:64, 0:64], IDT[:, 0:64], IDT[:, 0:64],
            start=True, stop=True), reads=["idt"], writes=["pb7"])

    p.step("dve", lambda e: e.memset(ONESM[:], 1.0), writes=["onesm"])
    p.step("dve", lambda e: e.memset(ones_r16[:], 1.0), writes=["ones_r16"])
    p.step("dve", lambda e: e.memset(epsc[:, 0:1], EPS_LN), writes=["epsc"])
    p.step("dve", lambda e: e.memset(epsc[:, 1:2], EPS_A), writes=["epsca"])
    p.dma("sp", lambda e: e.dma_start(out=maskha[:], in_=C_MHA), writes=["mha"])
    p.dma("sp", lambda e: e.dma_start(out=dmask[:], in_=C_DM), writes=["dm"])
    p.dma("sp", lambda e: e.dma_start(out=IDT[:], in_=C_I16), writes=["idt"])

    for nm, dst, src in [("wq_a16", wq_a16, WqA), ("wk_a16", wk_a16, WkA),
                         ("wv_a16", wv_a16, WvA), ("wo_a16", wo_a16, WoA),
                         ("wf16", wf16, Wf)]:
        p.dma("pool", (lambda d_, s_: lambda e: e.dma_start(out=d_, in_=s_))(dst, src),
              writes=[nm])
    def emit_attn_b_setup():
        for nm, dsts, src in [("wq_b16", wq_b16, WqB), ("wk_b16", wk_b16, WkB),
                              ("wv_b16", wv_b16, WvB), ("wo_b16", wo_b16, WoB)]:
            for g in range(4):
                p.dma("pool",
                      (lambda d_, s_, g_: lambda e: e.dma_start(
                          out=d_, in_=s_[:, g_ * 512:(g_ + 1) * 512]))(
                              dsts[g], src, g),
                      writes=[f"{nm}:{g}"])
        for nm, dst, src in [("bqc", bqc, BqB), ("bkc", bkc, BkB)]:
            p.dma("pool",
                  (lambda d_, s_: lambda e: e.dma_start(
                      out=d_[:], in_=s_.rearrange("(c p) -> p c", p=128),
                      allow_slow_non_contiguous=True))(dst, src),
                  writes=[nm])
        p.dma("pool", lambda e: e.dma_start(
            out=ROWS16[0:1, 2048:4096], in_=BoB.rearrange("(a d) -> a d", a=1)),
            writes=["borow"])
        p.dma("pool", lambda e: e.dma_start(
            out=ROWS16[0:1, 4096:6144], in_=BvB.rearrange("(a d) -> a d", a=1)),
            writes=["bvrow"])

    p.dma("pool", lambda e: e.dma_start(
        out=lnwc[:], in_=LnW.rearrange("(c p) -> p c", p=128),
        allow_slow_non_contiguous=True), writes=["lnwc"])
    p.dma("pool", lambda e: e.dma_start(
        out=lnbc16[:], in_=LnB.rearrange("(c p) -> p c", p=128),
        allow_slow_non_contiguous=True), writes=["lnbc16"])

    any_ffn = any(s.startswith("ffn") or s == "wff" for s in sublayers)

    def wff_scan_gen():
        RB_W = ["rb:0:0", "rb:0:1", "rb:0:2", "rb:0:3"]
        for c in range(16):
            p.dma("pool", (lambda c_: lambda e: e.dma_start(
                out=RB[:, 0:2048], in_=Wf[c_ * 128:(c_ + 1) * 128, :]))(c),
                writes=RB_W)
            p.step("dve", lambda e: e.reduce_max(
                stc("wtmp", 1), RB[:, 0:2048], axis=AX.X,
                apply_absolute_value=True),
                reads=RB_W, writes=["stc:wtmp"])
            if c == 0:
                p.step("dve", lambda e: e.tensor_copy(stc("wmax", 1), stc("wtmp", 1)),
                       reads=["stc:wtmp"], writes=["stc:wmax"])
            else:
                p.step("dve", lambda e: e.tensor_max(
                    stc("wmax", 1), stc("wmax", 1), stc("wtmp", 1)),
                    reads=["stc:wtmp", "stc:wmax"], writes=["stc:wmax"])
            yield

    def wff_xform_gen():
        RB_W = ["rb:0:0", "rb:0:1", "rb:0:2", "rb:0:3"]
        p.dma("pool", lambda e: e.dma_start(
            out=wmax_d.rearrange("(d a) -> d a", a=1), in_=stc("wmax", 1)),
            reads=["stc:wmax"], writes=["wmax_d"])
        p.dma("sp", lambda e: e.dma_start(
            out=WROW[0:1, 0:128],
            in_=wmax_d.rearrange("(a d) -> a d", a=1)),
            reads=["wmax_d"], writes=["wrow"])
        p.step("dve", lambda e: e.reduce_max(stc("scale", 1)[0:1, 0:1],
                                             WROW[0:1, 0:128], axis=AX.X),
               reads=["wrow"], writes=["stc:scale"])
        yield
        p.dma("pool", lambda e: e.dma_start(
            out=RB[0:1, 0:2048], in_=Gam.rearrange("(a d) -> a d", a=1)),
            writes=RB_W)
        p.step("dve", lambda e: e.tensor_scalar_mul(
            ROWS16[0:1, 0:2048], RB[0:1, 0:2048], stc("scale", 1)[0:1, 0:1]),
            reads=RB_W + ["stc:scale"], writes=["rrow"])
        for j in range(4):
            p.step("pe", (lambda j_: lambda e: e.matmul(
                pb[0][:, 0:512], ones_r16[:], ROWS16[0:1, j_ * 512:(j_ + 1) * 512],
                start=True, stop=True))(j),
                reads=["ones_r16", "rrow"], writes=["pb0"])
            p.step("act", (lambda j_: lambda e: e.copy(
                G0[:, j_ * 512:(j_ + 1) * 512], pb[0][:, 0:512]))(j),
                reads=["pb0"], writes=["g0:0"])
        p.dma("pool", lambda e: e.dma_start(
            out=RB[0:1, 2048:4096], in_=Bet.rearrange("(a d) -> a d", a=1)),
            writes=["rb:1:0", "rb:1:1", "rb:1:2", "rb:1:3"])
        yield
        for c in range(16):
            p.dma("sp", (lambda c_: lambda e: e.dma_start_transpose(
                WS1[:, 0:2048], wf16[:, c_ * 128:(c_ + 1) * 128]))(c),
                reads=["wf16"], writes=["wsB:0", "wsB:1", "wsB:2", "wsB:3"])
            p.step("act", lambda e: e.sign(WS1[:, 2048:4096], WS1[:, 0:2048]),
                   reads=["wsB:0", "wsB:1", "wsB:2", "wsB:3"],
                   writes=["wsB:4", "wsB:5", "wsB:6", "wsB:7"])
            for g in range(4):
                p.step("pe", (lambda c_, g_: lambda e: e.matmul(
                    pb[2 + g_][0:1, 0:512], lnbc16[:, c_:c_ + 1],
                    WS1[:, 2048 + g_ * 512:2048 + (g_ + 1) * 512],
                    start=(c_ == 0), stop=(c_ == 15)))(c, g),
                    reads=["lnbc16", f"wsB:{4 + g}"], writes=[f"pb{2 + g}"])
            p.step("dve", lambda e: e.tensor_mul(
                WS1[:, 4096:6144], WS1[:, 2048:4096], G0[:, 0:2048]),
                reads=["wsB:4", "wsB:5", "wsB:6", "wsB:7", "g0:0"],
                writes=["wsB:8", "wsB:9", "wsB:10", "wsB:11"])
            p.step("dve", (lambda c_: lambda e: e.tensor_scalar_mul(
                WS1[:, 6144:8192], WS1[:, 4096:6144], lnwc[:, c_:c_ + 1]))(c),
                reads=["wsB:8", "wsB:9", "wsB:10", "wsB:11", "lnwc"],
                writes=["wsB:12", "wsB:13", "wsB:14", "wsB:15"])
            for g in range(4):
                p.dma("pool", (lambda c_, g_: lambda e: e.dma_start(
                    out=wffh[g_][c_ * 128:(c_ + 1) * 128, :],
                    in_=WS1[:, 6144 + g_ * 512:6144 + (g_ + 1) * 512]))(c, g),
                    reads=[f"wsB:{12 + g}"], writes=[f"wff:{g}"])
            yield
        for g in range(4):
            p.step("dve", (lambda g_: lambda e: e.tensor_mul(
                XRB[0:1, 0:512], pb[2 + g_][0:1, 0:512],
                ROWS16[0:1, g_ * 512:(g_ + 1) * 512]))(g),
                reads=[f"pb{2 + g}", "rrow"], writes=["xrb:0"])
            p.step("dve", (lambda g_: lambda e: e.tensor_add(
                XRB[0:1, 512:1024], XRB[0:1, 0:512],
                RB[0:1, 2048 + g_ * 512:2048 + (g_ + 1) * 512]))(g),
                reads=["xrb:0", f"rb:1:{g}"], writes=["xrb:1"])
            p.step("dve", (lambda g_: lambda e: e.tensor_copy(
                ROWS16[0:1, g_ * 512:(g_ + 1) * 512], XRB[0:1, 512:1024]))(g),
                reads=["xrb:1"], writes=["rrow"])
        yield

    def wor_stage_a(s):
        og, h = s // 16, s % 16
        p.dma("sp", (lambda o_, h_, s_: lambda e: e.dma_start_transpose(
            xn_fm[:, s_ * 512:(s_ + 1) * 512],
            wo_a16[o_ * 512:(o_ + 1) * 512, h_ * 128:(h_ + 1) * 128]))(
                og, h, s),
            reads=["wo_a16"], writes=[f"xnfm:{s // 4}"])

    def wor_stage_b(s):
        og, h = s // 16, s % 16
        p.dma("sp", (lambda o_, h_, s_: lambda e: e.dma_start(
            out=xn_fm[:, s_ * 512:(s_ + 1) * 512],
            in_=wo_b16[o_][h_ * 128:(h_ + 1) * 128, :]))(og, h, s),
            reads=[f"wo_b16:{og}"], writes=[f"xnfm:{s // 4}"])

    def t_stats_slice(tt, og, src_slot):
        p.step("act", (lambda s_, c_: lambda e: e.activation(
            XR[:, 0:512], XN2[:, s_ * 512:(s_ + 1) * 512],
            AF.Square, accum_out=SSQ4[:, c_:c_ + 1]))(src_slot, tt * 4 + og),
            reads=[f"xn2:{src_slot}"], writes=["xr:0", f"ssq4:{tt}"])

    def emit_t1_tile(tt, og, pbi, x_src, x_src_key, x_dst, x_dst_key, final):
        oslot = (tt * 4 + og) % 4
        nslot = (tt * 4 + og) % 2
        src_key = "x_in" if x_src_key == "x_in" else f"{x_src_key}:{tt}:{og}"
        p.dma("act", (lambda o_, a_: lambda e: e.dma_start(
            out=XO[:, o_ * 512:(o_ + 1) * 512], in_=a_))(oslot,
            x_src[tt * 128:(tt + 1) * 128, og * 512:(og + 1) * 512]),
            reads=[src_key], writes=[f"xo:{oslot}"])
        p.step("dve", (lambda o_, n_, c_, i_: lambda e: e.scalar_tensor_tensor(
            XN2[:, n_ * 512:(n_ + 1) * 512], pb[i_][:, 0:512], 0.0,
            XO[:, o_ * 512:(o_ + 1) * 512], ALU.bypass, ALU.add,
            accum_out=SUM4[:, c_:c_ + 1]))(oslot, nslot, tt * 4 + og, pbi),
            reads=[f"pb{pbi}", f"xo:{oslot}"],
            writes=[f"xn2:{nslot}", f"sum4:{tt}"])
        p.dma("pool", (lambda n_: lambda e: e.dma_start(
            out=x_dst[tt * 128:(tt + 1) * 128, og * 512:(og + 1) * 512],
            in_=XN2[:, n_ * 512:(n_ + 1) * 512]))(nslot),
            reads=[f"xn2:{nslot}"], writes=[f"{x_dst_key}:{tt}:{og}"])
        if not final:
            t_stats_slice(tt, og, nslot)

    def emit_tstats(ffn_next, grp):
        lo, n = grp * 4, 4
        cs = lambda name: stc(name)[:, lo:lo + n]
        rkeys = [f"sum4:{tt}" for tt in range(lo, lo + n)]
        rkeys2 = [f"ssq4:{tt}" for tt in range(lo, lo + n)]
        p.step("dve", (lambda l_: lambda e: e.reduce_sum(
            stc("s1")[:, l_:l_ + 4],
            SUM4[:, l_ * 4:(l_ + 4) * 4].rearrange("p (t f) -> p t f", f=4),
            axis=AX.X))(lo),
            reads=rkeys, writes=[f"stc:s1:{grp}"])
        p.step("dve", (lambda l_: lambda e: e.reduce_sum(
            stc("q1")[:, l_:l_ + 4],
            SSQ4[:, l_ * 4:(l_ + 4) * 4].rearrange("p (t f) -> p t f", f=4),
            axis=AX.X))(lo),
            reads=rkeys2, writes=[f"stc:q1:{grp}"])
        p.step("dve", lambda e: e.tensor_scalar_mul(cs("m"), cs("s1"), 1.0 / D),
               reads=[f"stc:s1:{grp}"], writes=[f"stc:m:{grp}"])
        p.step("dve", lambda e: e.tensor_mul(cs("msq"), cs("m"), cs("m")),
               reads=[f"stc:m:{grp}"], writes=[f"stc:msq:{grp}"])
        p.step("dve", lambda e: e.scalar_tensor_tensor(
            cs("var"), cs("q1"), 1.0 / D, cs("msq"), ALU.mult, ALU.subtract),
            reads=[f"stc:q1:{grp}", f"stc:msq:{grp}"], writes=[f"stc:var:{grp}"])
        p.step("act", lambda e: e.activation(
            cs("t1"), cs("var"), AF.Ln, bias=epsc[:, 0:1]),
            reads=[f"stc:var:{grp}", "epsc"], writes=[f"stc:t1:{grp}"])
        p.step("act", lambda e: e.activation(cs("rs"), cs("t1"), AF.Exp,
                                             scale=-0.5),
               reads=[f"stc:t1:{grp}"], writes=[f"stc:rs:{grp}"])
        if not ffn_next:
            p.step("dve", lambda e: e.tensor_copy(cs("A"), cs("rs")),
                   reads=[f"stc:rs:{grp}"], writes=[f"stc:A:{grp}"])
        else:
            p.step("dve", lambda e: e.tensor_mul(cs("t2"), cs("var"), cs("rs")),
                   reads=[f"stc:var:{grp}", f"stc:rs:{grp}"],
                   writes=[f"stc:t2:{grp}"])
            p.step("dve", lambda e: e.tensor_mul(cs("t2"), cs("t2"), cs("rs")),
                   reads=[f"stc:rs:{grp}", f"stc:t2:{grp}"],
                   writes=[f"stc:t2:{grp}"])
            p.step("act", lambda e: e.activation(
                cs("t1"), cs("t2"), AF.Ln, bias=epsc[:, 0:1]),
                reads=[f"stc:t2:{grp}", "epsc"], writes=[f"stc:t1:{grp}"])
            p.step("act", lambda e: e.activation(cs("t3"), cs("t1"),
                                                 AF.Exp, scale=-0.5),
                   reads=[f"stc:t1:{grp}"], writes=[f"stc:t3:{grp}"])
            p.step("dve", lambda e: e.tensor_mul(cs("A"), cs("rs"), cs("t3")),
                   reads=[f"stc:rs:{grp}", f"stc:t3:{grp}"],
                   writes=[f"stc:A:{grp}"])
        p.step("dve", lambda e: e.scalar_tensor_tensor(
            cs("B"), cs("m"), -1.0, cs("A"), ALU.mult, ALU.mult),
            reads=[f"stc:m:{grp}", f"stc:A:{grp}"], writes=[f"stc:B:{grp}"])

    def t2_norm(i, tt, akey, buf, s_, bkey):
        ps = i % 4
        p.step("dve", (lambda s2, ps_, t_, b_: lambda e: e.tensor_scalar(
            PST[:, ps_ * 512:(ps_ + 1) * 512],
            b_[:, s2 * 512:(s2 + 1) * 512],
            stc("A")[:, t_:t_ + 1], stc("B")[:, t_:t_ + 1],
            ALU.mult, ALU.add))(s_, ps, tt, buf),
            reads=[bkey, f"stc:A:{akey}", f"stc:B:{akey}"],
            writes=[f"pst:{ps}"])

    def t2_load_norm(i, tt, og, x_src, x_src_key, akey):
        xs = i % 4
        src_key = ("x_in" if x_src_key == "x_in"
                   else f"{x_src_key}:{tt}:{og}")
        p.dma("act" if xs % 2 == 0 else "sp",
              (lambda xs_, a_: lambda e: e.dma_start(
            out=XRB[:, xs_ * 512:(xs_ + 1) * 512], in_=a_))(xs,
            x_src[tt * 128:(tt + 1) * 128, og * 512:(og + 1) * 512]),
            reads=[src_key], writes=[f"xrb:{xs}"])
        t2_norm(i, tt, akey, XRB, xs, f"xrb:{xs}")

    def t2_transpose(i, tt, og):
        ps = i % 4
        bank = 4 + (i % 2)
        for j in range(4):
            p.step("pe", (lambda ps_, j_, b_: lambda e: e.matmul(
                pb[b_][:, j_ * 128:(j_ + 1) * 128],
                PST[:, ps_ * 512 + j_ * 128:ps_ * 512 + (j_ + 1) * 128],
                IDT[:], start=True, stop=True))(ps, j, bank),
                reads=[f"pst:{ps}", "idt"], writes=[f"pb{bank}"])
        off = (tt % HTT) * 128
        p.step("act", (lambda b_, g_, o_: lambda e: e.copy(
            xn_fm[:].rearrange("p (c x) -> p c x", x=TH)[
                :, g_ * 4:(g_ + 1) * 4, o_:o_ + 128],
            pb[b_][:, 0:512].rearrange("p (c x) -> p c x", x=128)))(
                bank, og, off),
            reads=[f"pb{bank}"],
            writes=[f"xnfm:{og * 4 + j}" for j in range(4)])

    T2D = 3

    def t2_gen(x_src, x_src_key, grps, t0_keys=False):
        slices = []
        for grp in grps:
            for tt in range(grp * 4, (grp + 1) * 4):
                ak = f"t{tt}" if t0_keys else str(grp)
                for og in range(4):
                    slices.append((tt, og, ak))
        n = len(slices)
        for i, (tt, og, ak) in enumerate(slices):
            t2_load_norm(i, tt, og, x_src, x_src_key, ak)
            hb()
            if i >= T2D:
                t2_transpose(i - T2D, slices[i - T2D][0], slices[i - T2D][1])
                yield
        for i in range(max(0, n - T2D), n):
            t2_transpose(i, slices[i][0], slices[i][1])
            yield

    def drain(gen):
        if gen is None:
            return
        for _ in gen:
            pass

    def step_n(gen, n):
        if gen is None:
            return
        for _ in range(n):
            try:
                next(gen)
            except StopIteration:
                return

    def drive_t2(main_gen, bg, k, t2g, per, batch):
        i = 0
        for _ in main_gen:
            if bg is not None:
                bg.step(k)
            i += 1
            if i % per == 0:
                step_n(t2g, batch)

    def emit_tstats1(ffn_next, tt):
        cs = lambda name: stc(name)[:, tt:tt + 1]
        g = f"t{tt}"
        p.step("dve", lambda e: e.reduce_sum(
            cs("s1"), SUM4[:, tt * 4:tt * 4 + 4], axis=AX.X),
            reads=[f"sum4:{tt}"], writes=[f"stc:s1:{g}"])
        p.step("dve", lambda e: e.reduce_sum(
            cs("q1"), SSQ4[:, tt * 4:tt * 4 + 4], axis=AX.X),
            reads=[f"ssq4:{tt}"], writes=[f"stc:q1:{g}"])
        p.step("dve", lambda e: e.tensor_scalar_mul(cs("m"), cs("s1"), 1.0 / D),
               reads=[f"stc:s1:{g}"], writes=[f"stc:m:{g}"])
        p.step("dve", lambda e: e.tensor_mul(cs("msq"), cs("m"), cs("m")),
               reads=[f"stc:m:{g}"], writes=[f"stc:msq:{g}"])
        p.step("dve", lambda e: e.scalar_tensor_tensor(
            cs("var"), cs("q1"), 1.0 / D, cs("msq"), ALU.mult, ALU.subtract),
            reads=[f"stc:q1:{g}", f"stc:msq:{g}"], writes=[f"stc:var:{g}"])
        p.step("act", lambda e: e.activation(
            cs("t1"), cs("var"), AF.Ln, bias=epsc[:, 0:1]),
            reads=[f"stc:var:{g}", "epsc"], writes=[f"stc:t1:{g}"])
        p.step("act", lambda e: e.activation(cs("rs"), cs("t1"), AF.Exp,
                                             scale=-0.5),
               reads=[f"stc:t1:{g}"], writes=[f"stc:rs:{g}"])
        if not ffn_next:
            p.step("dve", lambda e: e.tensor_copy(cs("A"), cs("rs")),
                   reads=[f"stc:rs:{g}"], writes=[f"stc:A:{g}"])
        else:
            p.step("dve", lambda e: e.tensor_mul(cs("t2"), cs("var"), cs("rs")),
                   reads=[f"stc:var:{g}", f"stc:rs:{g}"], writes=[f"stc:t2:{g}"])
            p.step("dve", lambda e: e.tensor_mul(cs("t2"), cs("t2"), cs("rs")),
                   reads=[f"stc:rs:{g}", f"stc:t2:{g}"], writes=[f"stc:t2:{g}"])
            p.step("act", lambda e: e.activation(
                cs("t1"), cs("t2"), AF.Ln, bias=epsc[:, 0:1]),
                reads=[f"stc:t2:{g}", "epsc"], writes=[f"stc:t1:{g}"])
            p.step("act", lambda e: e.activation(cs("t3"), cs("t1"),
                                                 AF.Exp, scale=-0.5),
                   reads=[f"stc:t1:{g}"], writes=[f"stc:t3:{g}"])
            p.step("dve", lambda e: e.tensor_mul(cs("A"), cs("rs"), cs("t3")),
                   reads=[f"stc:rs:{g}", f"stc:t3:{g}"], writes=[f"stc:A:{g}"])
        p.step("dve", lambda e: e.scalar_tensor_tensor(
            cs("B"), cs("m"), -1.0, cs("A"), ALU.mult, ALU.mult),
            reads=[f"stc:m:{g}", f"stc:A:{g}"], writes=[f"stc:B:{g}"])

    def _t0_slot_of(tt, og):
        if tt < HTT:
            if og < 3:
                return XO, og, f"xo:{og}"
            return XRB, tt % 2, f"xrb:{tt % 2}"
        s_ = (tt * 4 + og) % 4
        return XO, s_, f"xo:{s_}"

    def _t0_tile_stats(tt, ffn_next):
        for og in range(4):
            buf, s_, bkey = _t0_slot_of(tt, og)
            p.dma("sp" if og % 2 == 0 else "act",
                  (lambda o_, t_, s2, b_: lambda e: e.dma_start(
                      out=b_[:, s2 * 512:(s2 + 1) * 512],
                      in_=X[t_ * 128:(t_ + 1) * 128,
                            o_ * 512:(o_ + 1) * 512]))(og, tt, s_, buf),
                  writes=[bkey])
            p.step("dve", (lambda s2, c_, b_: lambda e: e.reduce_sum(
                SUM4[:, c_:c_ + 1], b_[:, s2 * 512:(s2 + 1) * 512],
                axis=AX.X))(s_, tt * 4 + og, buf),
                reads=[bkey], writes=[f"sum4:{tt}"])
            p.step("act", (lambda s2, n_, c_, b_: lambda e: e.activation(
                XN2[:, n_ * 512:(n_ + 1) * 512],
                b_[:, s2 * 512:(s2 + 1) * 512],
                AF.Square, accum_out=SSQ4[:, c_:c_ + 1]))(
                    s_, og % 2, tt * 4 + og, buf),
                reads=[bkey],
                writes=[f"xn2:{og % 2}", f"ssq4:{tt}"])
        emit_tstats1(ffn_next, tt)

    def emit_t0_h0(ffn_next, wff_gen=None):
        t0i = 0
        t0q = []
        for tt in range(HTT):
            _t0_tile_stats(tt, ffn_next)
            hb()
            if tt == 12:
                stage_wT_og(wk_a16, "wk_a16", 0, "wsA")
            for og in range(4):
                buf, s_, bkey = _t0_slot_of(tt, og)
                t2_norm(t0i, tt, f"t{tt}", buf, s_, bkey)
                t0q.append((tt, og))
                if t0i >= T2D:
                    t2_transpose(t0i - T2D, *t0q[t0i - T2D])
                t0i += 1
            if tt % 2 == 1:
                step_n(wff_gen, 1)
        for i in range(max(0, t0i - T2D), t0i):
            t2_transpose(i, *t0q[i])

    def t0_h1_gen(ffn_next):
        for tt in range(HTT, NTT):
            _t0_tile_stats(tt, ffn_next)
            yield

    WSBUF = {"wsA": WS0, "wsB": WS1}

    def stage_rows(src_og, src_key, wskey="wsA"):
        ws = WSBUF[wskey]
        for c in range(NC):
            p.dma("sp", (lambda c_, w_: lambda e: e.dma_start(
                out=w_[:, c_ * 512:(c_ + 1) * 512],
                in_=src_og[c_ * 128:(c_ + 1) * 128, :]))(c, ws),
                reads=[src_key], writes=[f"{wskey}:{c}"])

    def stage_wT_og(src16, src_key, og, wskey="wsA"):
        ws = WSBUF[wskey]
        for c in range(NC):
            p.dma("sp", (lambda c_, g_, w_: lambda e: e.dma_start_transpose(
                w_[:, c_ * 512:(c_ + 1) * 512],
                src16[g_ * 512:(g_ + 1) * 512, c_ * 128:(c_ + 1) * 128]))(c, og, ws),
                reads=[src_key], writes=[f"{wskey}:{c}"])

    def stage_tables(tg_abs, ring):
        p.dma("sp", (lambda r_, t_: lambda e: e.dma_start(
            out=CSR[:, r_ * 1024:r_ * 1024 + 512],
            in_=C_COS[:, t_ * 512:(t_ + 1) * 512]))(ring, tg_abs),
            writes=[f"csrc:{ring}"])
        p.dma("sp", (lambda r_, t_: lambda e: e.dma_start(
            out=CSR[:, r_ * 1024 + 512:r_ * 1024 + 1024],
            in_=C_SIN[:, t_ * 512:(t_ + 1) * 512]))(ring, tg_abs),
            writes=[f"csrs:{ring}"])

    def proj_mms(pbi, och, tgl, wskey="wsA"):
        ws = WSBUF[wskey]
        for c in range(NC):
            p.step("pe", (lambda c_, o_, t_, i_, w_: lambda e: e.matmul(
                pb[i_][:, 0:512],
                w_[:, c_ * 512 + o_ * 128:c_ * 512 + (o_ + 1) * 128],
                xn_fm[:, c_ * TH + t_ * 512:c_ * TH + (t_ + 1) * 512],
                start=(c_ == 0), stop=(c_ == 15)))(c, och, tgl, pbi, ws),
                reads=[f"{wskey}:{c}", f"xnfm:{c}"], writes=[f"pb{pbi}"])

    def rope_elu_evac(pbi, ring, dst, dst_key, tg_abs, rr):
        cosap = CSR[:, ring * 1024:ring * 1024 + 512]
        sinap = CSR[:, ring * 1024 + 512:ring * 1024 + 1024]
        b0, b1, b2, b3 = rr * 4, rr * 4 + 1, rr * 4 + 2, rr * 4 + 3
        k0, k1, k2, k3 = (f"rb:{rr}:0", f"rb:{rr}:1", f"rb:{rr}:2", f"rb:{rr}:3")
        p.step("dve", (lambda i_, c_, a_: lambda e: e.tensor_mul(
            rb(a_), pb[i_][:, 0:512], c_))(pbi, cosap, b0),
            reads=[f"pb{pbi}", f"csrc:{ring}"], writes=[k0])
        p.step("dve", (lambda i_, s_, a_: lambda e: e.tensor_mul(
            rb(a_)[0:64, :], pb[i_][64:128, 0:512], s_[64:128, :]))(pbi, sinap, b1),
            reads=[f"pb{pbi}", f"csrs:{ring}"], writes=[k1 + "lo"])
        p.step("dve", (lambda i_, s_, a_: lambda e: e.tensor_mul(
            rb(a_)[64:128, :], pb[i_][0:64, 0:512], s_[0:64, :]))(pbi, sinap, b1),
            reads=[f"pb{pbi}", f"csrs:{ring}"], writes=[k1 + "hi"])
        p.step("dve", (lambda a_, x_, y_: lambda e: e.tensor_add(
            rb(a_), rb(x_), rb(y_)))(b2, b0, b1),
            reads=[k0, k1 + "lo", k1 + "hi"], writes=[k2])
        p.step("act", (lambda a_, x_: lambda e: e.activation(
            rb(a_), rb(x_), AF.Relu, scale=-1.0))(b3, b2),
            reads=[k2], writes=[k3])
        p.step("act", (lambda a_, x_: lambda e: e.activation(
            rb(a_), rb(x_), AF.Exp, scale=-1.0))(b0, b3),
            reads=[k3], writes=[k0])
        ps_slot = rr
        p.step("dve", (lambda s_, x_, y_: lambda e: e.scalar_tensor_tensor(
            XT[:, s_ * 512:(s_ + 1) * 512], rb(x_), 0.0, rb(y_),
            ALU.max, ALU.add))(ps_slot, b2, b0),
            reads=[k0, k2], writes=[f"xt:{ps_slot}"])
        p.dma("pool", (lambda s_, t_: lambda e: e.dma_start(
            out=dst[:, t_ * 512:(t_ + 1) * 512],
            in_=XT[:, s_ * 512:(s_ + 1) * 512]))(ps_slot, tg_abs),
            reads=[f"xt:{ps_slot}"], writes=[dst_key])

    def emit_attn_a(pre, bg0=None):
        wsrot = [1]

        def nextws():
            wsrot[0] ^= 1
            return "wsA" if wsrot[0] == 0 else "wsB"

        def q_proj_og(half, og):
            wk_ = nextws()
            stage_wT_og(wq_a16, "wq_a16", og, wk_)
            for tgl in range(4):
                tg_abs = half * 4 + tgl
                ring = 0
                stage_tables(tg_abs, ring)
                for och in range(4):
                    h = og * 4 + och
                    pbi = (tg_abs * 4 + och) % 2
                    proj_mms(pbi, och, tgl, wk_)
                    rope_elu_evac(pbi, ring, qh_a[h], f"qh_a:{h}", tg_abs,
                                  (tg_abs * 4 + och) % 2)
                    yield

        def k_proj(half):
            wk_ = nextws()
            if half != 0:
                stage_wT_og(wk_a16, "wk_a16", 0, wk_)
            for tgl in range(4):
                tg_abs = half * 4 + tgl
                ring = 0
                stage_tables(tg_abs, ring)
                for och in range(4):
                    pbi = (tg_abs * 4 + och) % 2
                    proj_mms(pbi, och, tgl, wk_)
                    rope_elu_evac(pbi, ring, kh_a[och], f"kh_a:{och}", tg_abs,
                                  (tg_abs * 4 + och) % 2)
                    yield

        def v_proj(half):
            wk_ = nextws()
            stage_wT_og(wv_a16, "wv_a16", 0, wk_)
            wsv = WSBUF[wk_]
            for ttl in range(HTT):
                tt = half * HTT + ttl
                pbi = tt % 2
                for c in range(NC):
                    p.step("pe", (lambda c_, t_, i_, w_: lambda e: e.matmul(
                        pb[i_][:, 0:512],
                        xn_fm[:, c_ * TH + t_ * 128:c_ * TH + (t_ + 1) * 128],
                        w_[:, c_ * 512:(c_ + 1) * 512],
                        start=(c_ == 0), stop=(c_ == 15)))(c, ttl, pbi, wsv),
                        reads=[f"{wk_}:{c}", f"xnfm:{c}"], writes=[f"pb{pbi}"])
                aslot = 2 + tt % 2
                p.step("act", (lambda s_, i_: lambda e: e.copy(
                    XT[:, s_ * 512:(s_ + 1) * 512], pb[i_][:, 0:512]))(aslot, pbi),
                    reads=[f"pb{pbi}"], writes=[f"xt:{aslot}"])
                for kvj in range(4):
                    p.dma("pool", (lambda s_, k_, t_: lambda e: e.dma_start(
                        out=vh_a[k_][t_ * 128:(t_ + 1) * 128, :],
                        in_=XT[:, s_ * 512 + k_ * 128:s_ * 512 + (k_ + 1) * 128]))(
                            aslot, kvj, tt),
                        reads=[f"xt:{aslot}"], writes=[f"vh_a:{kvj}"])
                yield

        def scan_kv(kv):
            yield
            for part in range(4):
                p.dma("sp", (lambda k_, i_: lambda e: e.dma_start(
                    out=G0[:, i_ * 1024:(i_ + 1) * 1024],
                    in_=kh_a[k_][:, i_ * 1024:(i_ + 1) * 1024]))(kv, part),
                    reads=[f"kh_a:{kv}"], writes=[f"g0:{part}"])
                p.dma("pool", (lambda k_, i_: lambda e: e.dma_start(
                    out=G1[:].rearrange("p (i j) -> p i j", j=128)[
                        :, i_ * 8:(i_ + 1) * 8, :],
                    in_=vh_a[k_].rearrange("(i p) j -> p i j", p=128)[
                        :, i_ * 8:(i_ + 1) * 8, :]))(kv, part),
                    reads=[f"vh_a:{kv}"], writes=[f"g1:{part}"])
            for i in range(32):
                p.dma("sp", (lambda k_, i_: lambda e: e.dma_start_transpose(
                    G2[:, i_ * 128:(i_ + 1) * 128],
                    kh_a[k_][:, i_ * 128:(i_ + 1) * 128]))(kv, i),
                    reads=[f"kh_a:{kv}"], writes=[f"g2:{i // 8}"])
            p.step("dve", lambda e: e.memset(ZC[:, 0:1], 0.0), writes=["z32"])
            p.step("dve", lambda e: e.memset(ST32[:], 0.0), writes=["st32"])
            for si in range(16):
                ca, cb = 2 * si, 2 * si + 1
                qp = si // 4
                p.step("pe", (lambda c_: lambda e: e.matmul(
                    pb[6][:, 0:128], G2[:, c_ * 128:(c_ + 1) * 128],
                    G1[:, c_ * 128:(c_ + 1) * 128],
                    start=True, stop=False))(ca),
                    reads=[f"g2:{qp}", f"g1:{qp}"], writes=["pb6"])
                p.step("pe", (lambda c_: lambda e: e.matmul(
                    pb[6][:, 0:128], G2[:, c_ * 128:(c_ + 1) * 128],
                    G1[:, c_ * 128:(c_ + 1) * 128],
                    start=False, stop=True))(cb),
                    reads=[f"g2:{qp}", f"g1:{qp}"], writes=["pb6"])
                p.step("dve", lambda e: e.tensor_add(
                    ST32[:], ST32[:], pb[6][:, 0:128]),
                    reads=["pb6", "st32"], writes=["st32"])
                p.step("act", (lambda s_: lambda e: e.copy(
                    G2[:, s_ * 128:(s_ + 1) * 128], ST32[:]))(si),
                    reads=["st32"], writes=[f"g2:{si // 8}"])
                p.step("dve", (lambda s_: lambda e: e.reduce_sum(
                    ZC[:, 1:2], G0[:, s_ * 256:(s_ + 1) * 256],
                    axis=AX.X))(si),
                    reads=[f"g0:{qp}"], writes=["ztmp"])
                p.step("dve", lambda e: e.tensor_add(
                    ZC[:, 0:1], ZC[:, 0:1], ZC[:, 1:2]),
                    reads=["ztmp", "z32"], writes=["z32"])
                p.step("dve", (lambda s_: lambda e: e.tensor_copy(
                    ZC[:, 4 + s_:5 + s_], ZC[:, 0:1]))(si),
                    reads=["z32"], writes=[f"zs:{si}"])
                yield
            for pp in range(2):
                hA, hB = kv * 4 + 2 * pp, kv * 4 + 2 * pp + 1

                def qwin_load(w):
                    ws_ = w % 4
                    p.dma("sp", (lambda w_, s_: lambda e: e.dma_start(
                        out=QF2[:, s_ * 512:s_ * 512 + 256],
                        in_=qh_a[hA][:, w_ * 256:(w_ + 1) * 256]))(w, ws_),
                        reads=[f"qh_a:{hA}"], writes=[f"qw:{ws_}"])
                    p.dma("pool", (lambda w_, s_: lambda e: e.dma_start(
                        out=QF2[:, s_ * 512 + 256:(s_ + 1) * 512],
                        in_=qh_a[hB][:, w_ * 256:(w_ + 1) * 256]))(w, ws_),
                        reads=[f"qh_a:{hB}"], writes=[f"qw:{ws_}"])

                qwin_load(0)
                pend = [None]

                def flush_epi():
                    if pend[0] is not None:
                        fn = pend[0]
                        pend[0] = None
                        fn()

                for si in range(17):
                    if si < 16:
                        ca, cb = 2 * si, 2 * si + 1
                        qp = si // 4
                        qs = si % 4
                        rhs_full = QF2[:, qs * 512:(qs + 1) * 512]
                        rhs_half = QF2[:, qs * 512:(qs + 1) * 512].rearrange(
                            "p (a b) -> p a b", b=256)[:, :, 128:256]
                        pslot = si % 2
                        qkeys = [f"qw:{qs}"]
                        p.step("pe", (lambda c_, r_, q_, ks: lambda e: e.matmul(
                            pb[2][:, 0:512], G0[:, c_ * 128:(c_ + 1) * 128], r_,
                            start=True, stop=True))(ca, rhs_full, qp, qkeys),
                            reads=[f"g0:{qp}"] + qkeys, writes=["pb2"])
                        if si < 15:
                            qwin_load(si + 1)
                        p.step("act", (lambda s_: lambda e: e.copy(
                            PST[:, s_ * 512:(s_ + 1) * 512],
                            pb[2][:, 0:512]))(pslot),
                            reads=["pb2"], writes=[f"pst:{pslot}"])
                        p.step("dve", (lambda s_: lambda e: e.tensor_mul(
                            PST[:, s_ * 512:(s_ + 1) * 512].rearrange(
                                "p (a b) -> p a b", b=256)[:, :, 0:128],
                            PST[:, s_ * 512:(s_ + 1) * 512].rearrange(
                                "p (a b) -> p a b", b=256)[:, :, 0:128],
                            maskha[:].rearrange(
                                "p (a b) -> p a b", b=256)[:, :, 0:128]))(pslot),
                            reads=[f"pst:{pslot}"], writes=[f"pst:{pslot}"])
                        p.step("pe", (lambda c_, r_: lambda e: e.matmul(
                            pb[2][:, 0:256], G0[:, c_ * 128:(c_ + 1) * 128], r_,
                            start=True, stop=True))(cb, rhs_half),
                            reads=[f"g0:{qp}"] + qkeys, writes=["pb2"])
                        p.step("dve", (lambda s_: lambda e: e.tensor_mul(
                            PST[:, (2 + s_) * 512:(2 + s_) * 512 + 256].rearrange(
                                "p (a b) -> p a b", b=128),
                            pb[2][:, 0:256].rearrange("p (a b) -> p a b", b=128),
                            maskha[:].rearrange(
                                "p (a b) -> p a b", b=256)[:, :, 0:128]))(pslot),
                            reads=["pb2", "mha"], writes=[f"pstm:{pslot}"])
                    hb()
                    if si >= 1:
                        sj = si - 1
                        caj, cbj = 2 * sj, 2 * sj + 1
                        qpj = sj // 4
                        qsj = sj % 4
                        rhs_fj = QF2[:, qsj * 512:(qsj + 1) * 512]
                        nb = 3 + (sj % 2)
                        psj = sj % 2
                        dr_ = sj % 2
                        qkj = [f"qw:{qsj}"]
                        papA = PST[:, psj * 512:(psj + 1) * 512]
                        papB = PST[:, (2 + psj) * 512:(2 + psj) * 512 + 256]
                        flush_epi()
                        if sj > 0:
                            p.step("act", (lambda s_: lambda e:
                                   e.activation(
                                       PST[:, 1280 + (s_ % 2) * 128:
                                           1280 + (s_ % 2 + 1) * 128],
                                       ONESM[:], AF.Identity,
                                       scale=ZC[:, 3 + s_:4 + s_]))(sj),
                                reads=["onesm", f"zs:{sj - 1}"],
                                writes=[f"zm:{sj % 2}"])
                            p.step("pe", (lambda r_, b_, s_: lambda e: e.matmul(
                                pb[b_][:, 0:512],
                                G2[:, (s_ - 1) * 128:s_ * 128], r_,
                                start=True, stop=False))(rhs_fj, nb, sj),
                                reads=[f"g2:{(sj - 1) // 8}"] + qkj,
                                writes=[f"pb{nb}"])
                            p.step("pe", (lambda r_, s_: lambda e: e.matmul(
                                pb[5][:, 0:512],
                                PST[:, 1280 + (s_ % 2) * 128:
                                    1280 + (s_ % 2 + 1) * 128], r_,
                                start=True, stop=False))(rhs_fj, sj),
                                reads=[f"zm:{sj % 2}"] + qkj, writes=["pb5"])
                        p.step("pe", (lambda c_, a_, b_, f_: lambda e: e.matmul(
                            pb[b_][:, 0:512], G1[:, c_ * 128:(c_ + 1) * 128],
                            a_, start=f_, stop=False))(caj, papA, nb, sj == 0),
                            reads=[f"g1:{qpj}", f"pst:{psj}"],
                            writes=[f"pb{nb}"])
                        p.step("pe", (lambda c_, a_, b_: lambda e: e.matmul(
                            pb[b_][:].rearrange("p (h q) -> p h q", h=2)[
                                :, :, 128:256],
                            G1[:, c_ * 128:(c_ + 1) * 128], a_,
                            start=False, stop=True))(cbj, papB, nb),
                            reads=[f"g1:{qpj}", f"pstm:{psj}"],
                            writes=[f"pb{nb}"])
                        p.step("pe", (lambda a_, f_: lambda e: e.matmul(
                            pb[5][:, 0:512], ONESM[:], a_,
                            start=f_, stop=False))(papA, sj == 0),
                            reads=["onesm", f"pst:{psj}"],
                            writes=["pb5"])
                        p.step("pe", (lambda a_: lambda e: e.matmul(
                            pb[5][:].rearrange("p (h q) -> p h q", h=2)[
                                :, :, 128:256],
                            ONESM[:], a_, start=False, stop=True))(papB),
                            reads=["onesm", f"pstm:{psj}"], writes=["pb5"])

                        def mk_epi(si_, dr2, nb2, hA_, hB_):
                            def epi():
                                p.step("act", lambda e: e.activation(
                                    rb(dr2 * 4), pb[5][:, 0:512], AF.Identity,
                                    bias=epsc[:, 1:2]),
                                    reads=["pb5", "epsca"],
                                    writes=[f"rb:{dr2}:0"])
                                p.step("dve", lambda e: e.reciprocal(
                                    rb(dr2 * 4 + 2), rb(dr2 * 4)),
                                    reads=[f"rb:{dr2}:0"],
                                    writes=[f"rb:{dr2}:2"])
                                p.step("dve", lambda e: e.tensor_mul(
                                    AOT[:, dr2 * 512:(dr2 + 1) * 512],
                                    pb[nb2][:, 0:512], rb(dr2 * 4 + 2)),
                                    reads=[f"pb{nb2}", f"rb:{dr2}:2"],
                                    writes=[f"aot:{dr2}"])
                                for hj, h in enumerate((hA_, hB_)):
                                    p.dma("pool", (lambda j_, h_: lambda e:
                                          e.dma_start(
                                              out=aoh_a[h_][:, si_ * 256:
                                                            (si_ + 1) * 256],
                                              in_=AOT[:, dr2 * 512 + j_ * 256:
                                                      dr2 * 512 +
                                                      (j_ + 1) * 256]))(
                                                  hj, h),
                                        reads=[f"aot:{dr2}"],
                                        writes=[f"aoh_a:{h}:{2 * si_}",
                                                f"aoh_a:{h}:{2 * si_ + 1}"])
                            return epi
                        pend[0] = mk_epi(sj, dr_, nb, hA, hB)
                    yield
                flush_epi()

        bg = bg0 if bg0 is not None else BG()
        for half in range(2):
            if half == 1:
                drain(pre.get("h1"))
            drive(k_proj(half), bg, 4)
            drive(v_proj(half), bg, 4)
            for og in range(4):
                if half == 0 and og == 3:
                    drive_t2(q_proj_og(half, og), bg, 4,
                             pre.get("h1"), 4, 16)
                else:
                    drive(q_proj_og(half, og), bg, 4)
                if half == 1:
                    bg.set(scan_kv(og))
        return bg

    def emit_attn_b(pre):
        emit_attn_b_setup()
        wsrot = [1]

        def nextws():
            wsrot[0] ^= 1
            return "wsA" if wsrot[0] == 0 else "wsB"

        def qk_og(half, og, wog, wkey, dsts, dkey, bias, bias_key):
            wk_ = nextws()
            stage_rows(wog[og], f"{wkey}:{og}", wk_)
            for tgl in range(4):
                tg_abs = half * 4 + tgl
                for och in range(4):
                    h = og * 4 + och
                    pbi = (tg_abs * 4 + och) % 2
                    if half == 0:
                        hb()
                    proj_mms(pbi, och, tgl, wk_)
                    ps_slot = (tg_abs * 4 + och) % 2
                    p.step("act", (lambda i_, s_, h_, b_: lambda e:
                           e.activation(
                               XT[:, s_ * 512:(s_ + 1) * 512],
                               pb[i_][:, 0:512], AF.Identity,
                               bias=b_[:, h_:h_ + 1]))(pbi, ps_slot, h, bias),
                        reads=[f"pb{pbi}", bias_key],
                        writes=[f"xt:{ps_slot}"])
                    p.dma("pool", (lambda s_, h_, t_, d_: lambda e:
                          e.dma_start(
                              out=d_[:, t_ * 512:(t_ + 1) * 512],
                              in_=XT[:, s_ * 512:(s_ + 1) * 512]))(
                                  ps_slot, h, tg_abs, dsts[h]),
                        reads=[f"xt:{ps_slot}"], writes=[f"{dkey}:{h}"])
                    yield

        def v_og(half, og):
            wk_ = nextws()
            stage_rows(wv_b16[og], f"wv_b16:{og}", wk_)
            wsv = WSBUF[wk_]
            for ttl in range(HTT):
                tt = half * HTT + ttl
                pbi = tt % 2
                for c in range(NC):
                    p.step("pe", (lambda c_, t_, i_, w_: lambda e: e.matmul(
                        pb[i_][:, 0:512],
                        xn_fm[:, c_ * TH + t_ * 128:c_ * TH + (t_ + 1) * 128],
                        w_[:, c_ * 512:(c_ + 1) * 512],
                        start=(c_ == 0), stop=False))(c, ttl, pbi, wsv),
                        reads=[f"{wk_}:{c}", f"xnfm:{c}"], writes=[f"pb{pbi}"])
                p.step("pe", (lambda g_, i_: lambda e: e.matmul(
                    pb[i_][:, 0:512], ones_r16[:],
                    ROWS16[0:1, 4096 + g_ * 512:4096 + (g_ + 1) * 512],
                    start=False, stop=True))(og, pbi),
                    reads=["ones_r16", "bvrow"], writes=[f"pb{pbi}"])
                aslot = 2 + tt % 2
                p.step("act", (lambda s_, i_: lambda e: e.copy(
                    XT[:, s_ * 512:(s_ + 1) * 512], pb[i_][:, 0:512]))(
                        aslot, pbi),
                    reads=[f"pb{pbi}"], writes=[f"xt:{aslot}"])
                for hh in range(4):
                    h = og * 4 + hh
                    p.dma("pool", (lambda s_, j_, h_, t_: lambda e: e.dma_start(
                        out=vh_b[h_][t_ * 128:(t_ + 1) * 128, :],
                        in_=XT[:, s_ * 512 + j_ * 128:
                                s_ * 512 + (j_ + 1) * 128]))(aslot, hh, h, tt),
                        reads=[f"xt:{aslot}"], writes=[f"vh_b:{h}"])
                yield

        pend_f = [None]

        def flush_f():
            if pend_f[0] is not None:
                fn = pend_f[0]
                pend_f[0] = None
                fn()

        def flash_head(h):
            yield
            for part in range(4):
                p.dma("sp", (lambda h_, i_: lambda e: e.dma_start(
                    out=G0[:, i_ * 1024:(i_ + 1) * 1024],
                    in_=kh_b[h_][:, i_ * 1024:(i_ + 1) * 1024]))(h, part),
                    reads=[f"kh_b:{h}"], writes=[f"g0:{part}"])
                p.dma("pool", (lambda h_, i_: lambda e: e.dma_start(
                    out=G1[:].rearrange("p (i j) -> p i j", j=128)[
                        :, i_ * 8:(i_ + 1) * 8, :],
                    in_=vh_b[h_].rearrange("(i p) j -> p i j", p=128)[
                        :, i_ * 8:(i_ + 1) * 8, :]))(h, part),
                    reads=[f"vh_b:{h}"], writes=[f"g1:{part}"])
                p.dma("sp", (lambda h_, i_: lambda e: e.dma_start(
                    out=G2[:, i_ * 1024:(i_ + 1) * 1024],
                    in_=qh_b[h_][:, i_ * 1024:(i_ + 1) * 1024]))(h, part),
                    reads=[f"qh_b:{h}"], writes=[f"g2:{part}"])
            for qg in range(8):
                nkt = 4 * qg + 4
                ob = 4 if qg % 2 == 0 else 5
                db = 6
                dr_ = qg % 2
                DEP = 3
                for kt in range(nkt + DEP):
                    if kt < nkt:
                        sbank = (2, 3, 7)[kt % 3]
                        p.step("pe", (lambda k_, b_, q_: lambda e: e.matmul(
                            pb[b_][:, 0:512], G0[:, k_ * 128:(k_ + 1) * 128],
                            G2[:, q_ * 512:(q_ + 1) * 512],
                            start=True, stop=True))(kt, sbank, qg),
                            reads=[f"g0:{kt // 8}", f"g2:{qg // 2}"],
                            writes=[f"pb{sbank}"])
                        pslot = kt % 4
                        p.step("act", (lambda b_, s_: lambda e: e.activation(
                            PST[:, s_ * 512:(s_ + 1) * 512], pb[b_][:, 0:512],
                            AF.Exp, scale=SM_SCALE))(sbank, pslot),
                            reads=[f"pb{sbank}"], writes=[f"pst:{pslot}"])
                        if kt >= 4 * qg:
                            r = kt - 4 * qg
                            p.step("dve", (lambda s_, r_: lambda e: e.tensor_mul(
                                PST[:, s_ * 512:(s_ + 1) * 512],
                                PST[:, s_ * 512:(s_ + 1) * 512],
                                dmask[:, r_ * 512:(r_ + 1) * 512]))(pslot, r),
                                reads=[f"pst:{pslot}", "dm"],
                                writes=[f"pst:{pslot}"])
                    if kt == 1:
                        flush_f()
                    if kt >= DEP:
                        j = kt - DEP
                        js = j % 4
                        pk = f"pst:{js}"
                        pap = PST[:, js * 512:(js + 1) * 512]
                        p.step("pe", (lambda k_, a_, f_, l_, b_: lambda e:
                               e.matmul(
                                   pb[b_][:, 0:512],
                                   G1[:, k_ * 128:(k_ + 1) * 128], a_,
                                   start=f_, stop=l_))(
                                       j, pap, j == 0, j == nkt - 1, ob),
                            reads=[f"g1:{j // 8}", pk], writes=[f"pb{ob}"])
                        p.step("pe", (lambda a_, f_, l_, d_: lambda e: e.matmul(
                            pb[d_][:, 0:512], ONESM[:], a_,
                            start=f_, stop=l_))(pap, j == 0, j == nkt - 1, db),
                            reads=["onesm", pk], writes=[f"pb{db}"])
                    yield
                p.step("act", (lambda d_, b_: lambda e: e.copy(
                    rb(d_ * 4), pb[b_][:, 0:512]))(dr_, db),
                    reads=[f"pb{db}"], writes=[f"rb:{dr_}:0"])

                def mk_epi_f(h_, qg_, dr2, ob2):
                    def epi():
                        p.step("dve", lambda e: e.reciprocal(
                            rb(dr2 * 4 + 2), rb(dr2 * 4)),
                            reads=[f"rb:{dr2}:0"], writes=[f"rb:{dr2}:2"])
                        p.step("dve", lambda e: e.tensor_mul(
                            AOT[:, dr2 * 512:(dr2 + 1) * 512],
                            pb[ob2][:, 0:512], rb(dr2 * 4 + 2)),
                            reads=[f"pb{ob2}", f"rb:{dr2}:2"],
                            writes=[f"aot:{dr2}"])
                        p.dma("pool", lambda e: e.dma_start(
                            out=aoh_b[h_][:, qg_ * 512:(qg_ + 1) * 512],
                            in_=AOT[:, dr2 * 512:(dr2 + 1) * 512]),
                            reads=[f"aot:{dr2}"], writes=[f"aoh_b:{h_}:{qg_}"])
                    return epi
                pend_f[0] = mk_epi_f(h, qg, dr_, ob)

        def flash_og(og):
            for hh in range(4):
                yield from flash_head(og * 4 + hh)

        bg = BG()
        for half in range(2):
            if half == 1:
                drain(pre.get("h1"))
            for og in range(4):
                if half == 0 and og == 0:
                    step_n(pre.get("h0"), 16)
                    drive_t2(qk_og(half, og, wq_b16, "wq_b16", qh_b, "qh_b",
                                   bqc, "bqc"), bg, 9, pre.get("h0"), 4, 16)
                else:
                    drive(qk_og(half, og, wq_b16, "wq_b16", qh_b, "qh_b",
                                bqc, "bqc"), bg, 9)
                drive(qk_og(half, og, wk_b16, "wk_b16", kh_b, "kh_b",
                            bkc, "bkc"), bg, 9)
                if half == 0 and og == 3:
                    drive_t2(v_og(half, og), bg, 9, pre.get("h1"), 1, 4)
                else:
                    drive(v_og(half, og), bg, 9)
                if half == 1:
                    bg.set(flash_og(og))
        bg.finish()
        flush_f()
        return bg

    def emit_outproj_attn(aoh, aoh_key, wor_stage, with_bo, x_src, x_src_key,
                          x_dst, x_dst_key, final, boundary_ffn=(False,),
                          bg=None, subkeys=None, hb_ok=False):
        if subkeys is None:
            subkeys = lambda hh, blk: [f"{aoh_key}:{hh}:{blk}"]
        for s in range(64):
            wor_stage(s)
            if bg is not None:
                bg.step(3)
        for blk in range(8):
            need = [k for hh in range(NH) for k in subkeys(hh, blk)]
            if bg is not None:
                bg.step(12)
                bg.step_until(p, need)
            for hh in range(NH):
                p.dma("sp", (lambda h_, b_: lambda e: e.dma_start(
                    out=WS0[:, h_ * 512:(h_ + 1) * 512],
                    in_=aoh[h_][:, b_ * 512:(b_ + 1) * 512]))(hh, blk),
                    reads=subkeys(hh, blk), writes=[f"wsA:{hh}"])
            for og in range(4):
                for ttl in range(4):
                    if hb_ok:
                        hb()
                    tt = blk * 4 + ttl
                    pbi = tt % 2
                    for hh in range(NH):
                        s = og * 16 + hh
                        p.step("pe", (lambda h_, t_, i_, s_: lambda e: e.matmul(
                            pb[i_][:, 0:512],
                            WS0[:, h_ * 512 + t_ * 128:h_ * 512 + (t_ + 1) * 128],
                            xn_fm[:, s_ * 512:(s_ + 1) * 512],
                            start=(h_ == 0),
                            stop=(not with_bo and h_ == NH - 1)))(
                                hh, ttl, pbi, s),
                            reads=[f"wsA:{hh}", f"xnfm:{s // 4}"],
                            writes=[f"pb{pbi}"])
                    if with_bo:
                        p.step("pe", (lambda g_, i_: lambda e: e.matmul(
                            pb[i_][:, 0:512], ones_r16[:],
                            ROWS16[0:1, 2048 + g_ * 512:2048 + (g_ + 1) * 512],
                            start=False, stop=True))(og, pbi),
                            reads=["ones_r16", "borow"], writes=[f"pb{pbi}"])
                    emit_t1_tile(tt, og, pbi, x_src, x_src_key,
                                 x_dst, x_dst_key, final)
            if not final:
                emit_tstats(boundary_ffn[0], blk)
        if bg is not None:
            bg.finish()

    def emit_ffn_mm(x_src, x_src_key, x_dst, x_dst_key, final,
                    boundary_ffn=(False,), pre=None):
        pre = pre or {}
        wsrot = [1]

        def nextws():
            wsrot[0] ^= 1
            return "wsA" if wsrot[0] == 0 else "wsB"

        for half in range(2):
            if half == 1:
                drain(pre.get("h0"))
                drain(pre.get("h1"))
            for og in range(4):
                wk_ = nextws()
                stage_rows(wffh[og], f"wff:{og}", wk_)
                wsv = WSBUF[wk_]
                for ttl in range(HTT):
                    hb()
                    if half == 0 and og == 0:
                        step_n(pre.get("h0"), 4)
                    tt = half * HTT + ttl
                    pbi = tt % 4
                    for c in range(NC):
                        p.step("pe", (lambda c_, t_, i_, w_: lambda e: e.matmul(
                            pb[i_][:, 0:512],
                            xn_fm[:, c_ * TH + t_ * 128:c_ * TH + (t_ + 1) * 128],
                            w_[:, c_ * 512:(c_ + 1) * 512],
                            start=(c_ == 0), stop=False))(c, ttl, pbi, wsv),
                            reads=[f"{wk_}:{c}", f"xnfm:{c}"], writes=[f"pb{pbi}"])
                    p.step("pe", (lambda g_, i_: lambda e: e.matmul(
                        pb[i_][:, 0:512], ones_r16[:],
                        ROWS16[0:1, g_ * 512:(g_ + 1) * 512],
                        start=False, stop=True))(og, pbi),
                        reads=["ones_r16", "rrow"], writes=[f"pb{pbi}"])
                    emit_t1_tile(tt, og, pbi, x_src, x_src_key,
                                 x_dst, x_dst_key, final)
                    if og == 3 and not final and ttl % 4 == 3:
                        emit_tstats(boundary_ffn[0], tt // 4)
                    if half == 0 and og == 3:
                        step_n(pre.get("h1"), 4)

    cur_src, cur_key = X, "x_in"
    n_sub = len(sublayers)
    pre = {}
    for si, sl in enumerate(sublayers):
        final = (si == n_sub - 1)
        x_dst = OUT if final else xh[si % 2]
        x_dst_key = "out" if final else f"xh{si % 2}"
        bffn = (not final and sublayers[si + 1].startswith("ffn"),)
        if si == 0:
            wffg = wff_scan_gen() if any_ffn else None
            emit_t0_h0(sl.startswith("ffn"), wffg)
            bg0 = BG()
            bg0.set(t0_h1_gen(sl.startswith("ffn")))
            if wffg is not None:
                bg0.set(wffg)
            pre = {"h1": t2_gen(X, "x_in", range(4, 8), t0_keys=True)}
        if sl in ("t0", "wff"):
            continue
        if sl.startswith("ffn"):
            emit_ffn_mm(cur_src, cur_key, x_dst, x_dst_key, final, bffn,
                        pre=pre)
        elif sl == "attn_a":
            bg_a = emit_attn_a(pre, bg0)
            if any_ffn:
                bg_a.set(wff_xform_gen())
            emit_outproj_attn(aoh_a, "aoh_a", wor_stage_a, False,
                              cur_src, cur_key, x_dst, x_dst_key, final, bffn,
                              bg=bg_a, hb_ok=True,
                              subkeys=lambda hh, blk: [
                                  f"aoh_a:{hh}:{4 * blk + j}" for j in range(4)])
        elif sl == "attn_b":
            bg_b = emit_attn_b(pre)
            emit_outproj_attn(aoh_b, "aoh_b", wor_stage_b, True,
                              cur_src, cur_key, x_dst, x_dst_key, final, bffn,
                              bg=bg_b)
        if not final:
            pre = {"h0": t2_gen(x_dst, x_dst_key, range(0, 4)),
                   "h1": t2_gen(x_dst, x_dst_key, range(4, 8))}
        cur_src, cur_key = x_dst, x_dst_key

    if debug:
        want = debug if isinstance(debug, (set, list, tuple)) else None
        sel = lambda nm: want is None or nm in want
        dbg_specs = [
            ("dbg_wff0", wffh[0], [D, 512], BF16, ["wff:0"]),
            ("dbg_qa0", qh_a[0], [128, T], BF16, ["qh_a:0"]),
            ("dbg_ka0", kh_a[0], [128, T], BF16, ["kh_a:0"]),
            ("dbg_va0", vh_a[0], [T, 128], BF16, ["vh_a:0"]),
            ("dbg_aoa0", aoh_a[0], [128, T], BF16, [f"aoh_a:0:{b}" for b in range(32)]),
            ("dbg_aob0", aoh_b[0], [128, T], BF16, [f"aoh_b:0:{b}" for b in range(8)]),
            ("dbg_aob5", aoh_b[5], [128, T], BF16, [f"aoh_b:5:{b}" for b in range(8)]),
            ("dbg_qb0", qh_b[0], [128, T], BF16, ["qh_b:0"]),
            ("dbg_kb0", kh_b[0], [128, T], BF16, ["kh_b:0"]),
            ("dbg_vb0", vh_b[0], [T, 128], BF16, ["vh_b:0"]),
        ]
        for nm, src, shp, dt, rkeys in dbg_specs:
            if not sel(nm):
                continue
            dout = nc.dram_tensor(nm, shp, dt, kind="ExternalOutput").ap()
            p.dma("pool", (lambda d_, s_: lambda e: e.dma_start(out=d_, in_=s_))(
                dout, src), reads=rkeys, writes=[nm])
    block = ctx.enter_context(nc.Block())
    p.finalize(block, sems)
    ctx.close()
    return nc


_NC_CACHE = {}


def kernel(**inputs):
    key = tuple(SUBLAYERS)
    if key not in _NC_CACHE:
        _NC_CACHE[key] = build_program(SUBLAYERS)
    nc = _NC_CACHE[key]
    consts = _host_constants()
    x = np.ascontiguousarray(np.asarray(inputs["L0_attn_a_x"], dtype=np.float32))
    base = dict(
        wq_a=np.asarray(inputs["L0_attn_a_Wq"], np.float32),
        wk_a=np.asarray(inputs["L0_attn_a_Wk"], np.float32),
        wv_a=np.asarray(inputs["L0_attn_a_Wv"], np.float32),
        wo_a=np.asarray(inputs["L0_attn_a_Wo"], np.float32),
        wq_b=np.asarray(inputs["L0_attn_b_Wq"], np.float32),
        wk_b=np.asarray(inputs["L0_attn_b_Wk"], np.float32),
        wv_b=np.asarray(inputs["L0_attn_b_Wv"], np.float32),
        wo_b=np.asarray(inputs["L0_attn_b_Wo"], np.float32),
        bq_b=np.asarray(inputs["L0_attn_b_bq"], np.float32),
        bk_b=np.asarray(inputs["L0_attn_b_bk"], np.float32),
        bv_b=np.asarray(inputs["L0_attn_b_bv"], np.float32),
        bo_b=np.asarray(inputs["L0_attn_b_bo"], np.float32),
        w_ffn=np.asarray(inputs["L0_ffn_weight"], np.float32),
        gamma=np.asarray(inputs["L0_ffn_gamma"], np.float32),
        beta=np.asarray(inputs["L0_ffn_beta"], np.float32),
        ln_w=np.asarray(inputs["L0_ffn_ln_weight"], np.float32),
        ln_b=np.asarray(inputs["L0_ffn_ln_bias"], np.float32),
        ident32=consts["ident32"], ident16=consts["ident16"],
        maskha=consts["maskha"],
        dmasks=consts["dmasks"],
        cos_tbl=consts["cos_tbl"], sin_tbl=consts["sin_tbl"],
    )
    in_maps = [dict(base, x=x[c]) for c in range(8)]
    res = run_bass_kernel_spmd(nc, in_maps, core_ids=list(range(8)))
    return np.stack([np.asarray(r["out"], dtype=np.float32)
                     for r in res.results])


# version 37
# speedup vs baseline: 1.0165x; 1.0150x over previous
import math
import numpy as np
import ml_dtypes
import concourse.bass as bass
import concourse.mybir as mybir
from concourse.bass_utils import run_bass_kernel_spmd
from contextlib import ExitStack

F32 = mybir.dt.float32
BF16 = mybir.dt.bfloat16
AF = mybir.ActivationFunctionType
ALU = mybir.AluOpType
AX = mybir.AxisListType

D = 2048
T = 4096
HD = 128
NH = 16
KVH = 4
NC = 16
NTT = 32
HTT = 16
TH = 2048
EPS_LN = 1e-5
EPS_A = 1e-6
SM_SCALE = 1.0 / math.sqrt(HD)

SUBLAYERS = ("attn_a", "ffn", "attn_b", "ffn2")
ENGINES = ("pe", "act", "dve", "pool", "sp")


class Res:
    __slots__ = ("writer", "readers")

    def __init__(self):
        self.writer = None
        self.readers = []


class Step:
    __slots__ = ("engine", "emit", "queue", "idx", "dma_idx", "deps",
                 "needs_sig", "sig_val")

    def __init__(self, engine, emit, queue):
        self.engine = engine
        self.emit = emit
        self.queue = queue
        self.deps = []
        self.needs_sig = False
        self.sig_val = None


class Prog:
    QENG = {"sp": "sp", "pool": "pool", "act": "act"}

    def __init__(self):
        self.steps = []
        self.res = {}
        self.dma_counts = {"sp": 0, "pool": 0, "act": 0}

    def r(self, key):
        if key not in self.res:
            self.res[key] = Res()
        return self.res[key]

    def step(self, engine, emit, reads=(), writes=(), queue=None):
        s = Step(engine, emit, queue)
        s.idx = len(self.steps)
        if queue is not None:
            s.dma_idx = self.dma_counts[queue]
            self.dma_counts[queue] += 1
        deps = set()
        for k in reads:
            r = self.r(k)
            if r.writer is not None:
                deps.add(r.writer)
        for k in writes:
            r = self.r(k)
            if r.writer is not None:
                deps.add(r.writer)
            for rd in r.readers:
                deps.add(rd)
        for d in deps:
            if d is s:
                continue
            if (d.queue is None and s.queue is None
                    and d.engine == s.engine == "pe"):
                continue
            s.deps.append(d)
        for k in reads:
            self.res[k].readers.append(s)
        for k in writes:
            r = self.res[k]
            r.writer = s
            r.readers = []
        self.steps.append(s)
        return s

    def dma(self, queue, emit, reads=(), writes=()):
        return self.step(self.QENG[queue], emit, reads=reads, writes=writes,
                         queue=queue)

    DMA_POOL = 32

    def finalize(self, block, sems):
        for s in self.steps:
            for d in s.deps:
                if d.queue is None:
                    d.needs_sig = True
        cnt = {e: 0 for e in ENGINES}
        for s in self.steps:
            if s.queue is None and s.needs_sig:
                cnt[s.engine] += 1
                s.sig_val = cnt[s.engine]
        per_eng = {e: [] for e in ENGINES}
        for s in self.steps:
            per_eng[s.engine].append(s)
        K = self.DMA_POOL

        def dma_ev(d):
            return (sems[f"dq_{d.queue}_{d.dma_idx % K}"],
                    16 * (d.dma_idx // K + 1))

        def make_body(ename):
            def body(eng):
                known = {}

                def wait(sm, val):
                    if known.get(sm, 0) < val:
                        eng.wait_ge(sm, val)
                        known[sm] = val

                for s in per_eng[ename]:
                    for d in s.deps:
                        if d.queue is None:
                            wait(sems[d.engine], d.sig_val)
                        else:
                            wait(*dma_ev(d))
                    if s.queue is not None:
                        sm = sems[f"dq_{s.queue}_{s.dma_idx % K}"]
                        gen = s.dma_idx // K
                        if gen > 0:
                            wait(sm, 16 * gen)
                        ins = s.emit(eng)
                        ins.then_inc(sm, 16)
                    else:
                        ins = s.emit(eng)
                        if s.needs_sig:
                            ins.then_inc(sems[s.engine], 1)
                if ename == "pool":
                    for q in ("pool", "sp", "act"):
                        n = self.dma_counts[q]
                        for i in range(min(K, n)):
                            gens = (n - 1 - i) // K + 1
                            wait(sems[f"dq_{q}_{i}"], 16 * gens)
            return body

        block.tensor(make_body("pe"))
        block.scalar(make_body("act"))
        block.vector(make_body("dve"))
        block.gpsimd(make_body("pool"))
        block.sync(make_body("sp"))


class BG:

    def __init__(self):
        self.q = []

    def set(self, g):
        self.q.append(g)

    @property
    def g(self):
        return self.q[0] if self.q else None

    def step(self, n=1):
        for _ in range(n):
            if not self.q:
                return
            try:
                next(self.q[0])
            except StopIteration:
                self.q.pop(0)

    def finish(self):
        while self.q:
            self.step()

    def step_until(self, prog, keys):
        while self.q and any(
                prog.res.get(k) is None or prog.res[k].writer is None
                for k in keys):
            self.step()


def drive(main_gen, bg, k):
    for _ in main_gen:
        if bg is not None:
            bg.step(k)
    return


def _host_constants():
    bf = ml_dtypes.bfloat16
    ident16 = np.eye(128, dtype=bf)
    ident32 = np.eye(128, dtype=np.float32)
    k = np.arange(128)[:, None]
    q = np.arange(128)[None, :]
    m = (k <= q).astype(np.float32)
    maskha = np.tile(np.concatenate([m, np.ones((128, 128), np.float32)], 1),
                     (1, 2)).astype(bf)
    qq = np.arange(512)[None, :]
    dmasks = np.concatenate(
        [((np.arange(128)[:, None] + r * 128) <= qq).astype(np.float32)
         for r in range(4)], axis=1).astype(bf)
    inv_freq = (1.0 / (10000.0 ** (np.arange(0, HD, 2, dtype=np.float32) / HD))
                ).astype(np.float32)
    t = np.arange(T, dtype=np.float32)
    freqs = np.outer(t, inv_freq).astype(np.float32)
    cos = np.cos(freqs).astype(np.float32).T
    sin = np.sin(freqs).astype(np.float32).T
    cos_tbl = np.concatenate([cos, cos], axis=0)
    sin_tbl = np.concatenate([sin, -sin], axis=0)
    return dict(ident32=ident32, ident16=ident16,
                maskha=maskha, dmasks=dmasks, cos_tbl=cos_tbl,
                sin_tbl=sin_tbl)


def build_program(sublayers=None, debug=False):
    if sublayers is None:
        sublayers = SUBLAYERS
    nc = bass.Bass("TRN2", target_bir_lowering=False)
    p = Prog()

    din = lambda n, s: nc.dram_tensor(n, s, F32, kind="ExternalInput").ap()
    X = din("x", [T, D])
    WqA, WkA, WvA, WoA = din("wq_a", [D, D]), din("wk_a", [512, D]), \
        din("wv_a", [512, D]), din("wo_a", [D, D])
    WqB, WkB, WvB, WoB = din("wq_b", [D, D]), din("wk_b", [D, D]), \
        din("wv_b", [D, D]), din("wo_b", [D, D])
    BqB, BkB, BvB, BoB = din("bq_b", [D]), din("bk_b", [D]), \
        din("bv_b", [D]), din("bo_b", [D])
    Wf, Gam, Bet, LnW, LnB = din("w_ffn", [D, D]), din("gamma", [D]), \
        din("beta", [D]), din("ln_w", [D]), din("ln_b", [D])
    C_I16 = nc.dram_tensor("ident16", [128, 128], BF16, kind="ExternalInput").ap()
    C_MHA = nc.dram_tensor("maskha", [128, 512], BF16, kind="ExternalInput").ap()
    C_DM = nc.dram_tensor("dmasks", [128, 2048], BF16, kind="ExternalInput").ap()
    C_I32 = din("ident32", [128, 128])
    C_COS = din("cos_tbl", [128, T])
    C_SIN = din("sin_tbl", [128, T])
    OUT = nc.dram_tensor("out", [T, D], F32, kind="ExternalOutput").ap()

    dr = lambda n, s, d: nc.dram_tensor(n, s, d, kind="Internal").ap()
    xh = [dr(f"x_hbm{i}", [T, D], F32) for i in range(2)]
    wmax_d = dr("wmax_d", [128], F32)
    wq_a16 = dr("wq_a16", [D, D], BF16)
    wk_a16 = dr("wk_a16", [512, D], BF16)
    wv_a16 = dr("wv_a16", [512, D], BF16)
    wo_a16 = dr("wo_a16", [D, D], BF16)
    wf16 = dr("wf16", [D, D], BF16)
    wq_b16 = [dr(f"wq_b16_{g}", [D, 512], BF16) for g in range(4)]
    wk_b16 = [dr(f"wk_b16_{g}", [D, 512], BF16) for g in range(4)]
    wv_b16 = [dr(f"wv_b16_{g}", [D, 512], BF16) for g in range(4)]
    wo_b16 = [dr(f"wo_b16_{g}", [D, 512], BF16) for g in range(4)]
    wffh = [dr(f"wff_og{g}", [D, 512], BF16) for g in range(4)]
    qh_a = [dr(f"qa_h{h}", [128, T], BF16) for h in range(NH)]
    kh_a = [dr(f"ka_h{h}", [128, T], BF16) for h in range(KVH)]
    vh_a = [dr(f"va_h{h}", [T, 128], BF16) for h in range(KVH)]
    aoh_a = [dr(f"aoa_h{h}", [128, T], BF16) for h in range(NH)]
    qh_b = [dr(f"qb_h{h}", [128, T], BF16) for h in range(NH)]
    kh_b = [dr(f"kb_h{h}", [128, T], BF16) for h in range(NH)]
    vh_b = [dr(f"vb_h{h}", [T, 128], BF16) for h in range(NH)]
    aoh_b = [dr(f"aob_h{h}", [128, T], BF16) for h in range(NH)]

    ctx = ExitStack()
    sb = lambda n, s, d: ctx.enter_context(nc.sbuf_tensor(n, s, d))
    psmt = lambda n, s, d: ctx.enter_context(nc.psum_tensor(n, s, d))
    mksem = lambda n: ctx.enter_context(nc.semaphore(n))

    xn_fm = sb("xn_fm", [128, NC * TH], BF16)
    G0 = sb("g0", [128, T], BF16)
    G1 = sb("g1", [128, T], BF16)
    G2 = sb("g2", [128, T], BF16)
    QF2 = sb("qf2", [128, 2048], BF16)
    WS0 = sb("ws0", [128, 8192], BF16)
    WS1 = sb("ws1", [128, 8192], BF16)
    XO = sb("xo", [128, 2048], F32)
    XN2 = sb("xn2", [128, 2048], F32)
    XR = sb("xr", [128, 512], F32)
    XT = sb("xt", [128, 2048], BF16)
    RB = sb("rb", [128, 4096], F32)
    CSR = sb("csr", [128, 1024], F32)
    PST = sb("pst", [128, 2048], BF16)
    AOT = sb("aot", [128, 1024], BF16)
    SUM4 = sb("sum4", [128, 128], F32)
    SSQ4 = sb("ssq4", [128, 128], F32)
    STC = sb("stc", [128, 360], F32)
    ROWS16 = sb("rows16", [1, 6144], BF16)
    ONESM = sb("onesm", [128, 128], BF16)
    WROW = sb("wrow", [1, 128], F32)
    maskha = sb("mha", [128, 512], BF16)
    dmask = sb("dmask", [128, 2048], BF16)
    ones_r16 = sb("ones_r16", [1, 128], BF16)
    epsc = sb("epsc", [128, 2], F32)
    bqc = sb("bqc", [128, 16], F32)
    bkc = sb("bkc", [128, 16], F32)
    lnwc = sb("lnwc", [128, 16], F32)
    lnbc16 = sb("lnbc16", [128, 16], BF16)
    ST32 = sb("st32", [128, 128], F32)
    ZC = sb("zc", [128, 20], F32)
    XRB = sb("xrb", [128, 2048], F32)
    IDT = sb("idt", [128, 128], BF16)

    pb = [psmt(f"pb{i}", [128, 512], F32) for i in range(8)]

    sems = {e: mksem("sem_" + e) for e in ENGINES}
    for q in ("sp", "pool", "act"):
        for i in range(Prog.DMA_POOL):
            sems[f"dq_{q}_{i}"] = mksem(f"dq_{q}_{i}")

    XO_ALL = ["xo:0", "xo:1", "xo:2", "xo:3"]

    def rb(i):
        return RB[:, i * 512:(i + 1) * 512]

    stc_cols = {}
    _stc_next = [0]

    def stc(name, n=32):
        if name not in stc_cols:
            stc_cols[name] = (_stc_next[0], n)
            _stc_next[0] += n
            assert _stc_next[0] <= 360, "STC overflow"
        o, nn = stc_cols[name]
        assert nn == n
        return STC[:, o:o + n]

    def hb():
        p.step("pe", lambda e: e.matmul(
            pb[7][0:64, 0:64], IDT[:, 0:64], IDT[:, 0:64],
            start=True, stop=True), reads=["idt"], writes=["pb7"])

    p.step("dve", lambda e: e.memset(ONESM[:], 1.0), writes=["onesm"])
    p.step("dve", lambda e: e.memset(ones_r16[:], 1.0), writes=["ones_r16"])
    p.step("dve", lambda e: e.memset(epsc[:, 0:1], EPS_LN), writes=["epsc"])
    p.step("dve", lambda e: e.memset(epsc[:, 1:2], EPS_A), writes=["epsca"])
    p.dma("sp", lambda e: e.dma_start(out=maskha[:], in_=C_MHA), writes=["mha"])
    p.dma("sp", lambda e: e.dma_start(out=dmask[:], in_=C_DM), writes=["dm"])
    p.dma("sp", lambda e: e.dma_start(out=IDT[:], in_=C_I16), writes=["idt"])

    for nm, dst, src in [("wq_a16", wq_a16, WqA), ("wk_a16", wk_a16, WkA),
                         ("wv_a16", wv_a16, WvA), ("wo_a16", wo_a16, WoA),
                         ("wf16", wf16, Wf)]:
        p.dma("pool", (lambda d_, s_: lambda e: e.dma_start(out=d_, in_=s_))(dst, src),
              writes=[nm])
    def emit_attn_b_setup():
        for nm, dsts, src in [("wq_b16", wq_b16, WqB), ("wk_b16", wk_b16, WkB),
                              ("wv_b16", wv_b16, WvB), ("wo_b16", wo_b16, WoB)]:
            for g in range(4):
                p.dma("pool",
                      (lambda d_, s_, g_: lambda e: e.dma_start(
                          out=d_, in_=s_[:, g_ * 512:(g_ + 1) * 512]))(
                              dsts[g], src, g),
                      writes=[f"{nm}:{g}"])
        for nm, dst, src in [("bqc", bqc, BqB), ("bkc", bkc, BkB)]:
            p.dma("pool",
                  (lambda d_, s_: lambda e: e.dma_start(
                      out=d_[:], in_=s_.rearrange("(c p) -> p c", p=128),
                      allow_slow_non_contiguous=True))(dst, src),
                  writes=[nm])
        p.dma("pool", lambda e: e.dma_start(
            out=ROWS16[0:1, 2048:4096], in_=BoB.rearrange("(a d) -> a d", a=1)),
            writes=["borow"])
        p.dma("pool", lambda e: e.dma_start(
            out=ROWS16[0:1, 4096:6144], in_=BvB.rearrange("(a d) -> a d", a=1)),
            writes=["bvrow"])

    p.dma("pool", lambda e: e.dma_start(
        out=lnwc[:], in_=LnW.rearrange("(c p) -> p c", p=128),
        allow_slow_non_contiguous=True), writes=["lnwc"])
    p.dma("pool", lambda e: e.dma_start(
        out=lnbc16[:], in_=LnB.rearrange("(c p) -> p c", p=128),
        allow_slow_non_contiguous=True), writes=["lnbc16"])

    any_ffn = any(s.startswith("ffn") or s == "wff" for s in sublayers)

    def wff_scan_gen():
        RB_W = ["rb:0:0", "rb:0:1", "rb:0:2", "rb:0:3"]
        for c in range(16):
            p.dma("pool", (lambda c_: lambda e: e.dma_start(
                out=RB[:, 0:2048], in_=Wf[c_ * 128:(c_ + 1) * 128, :]))(c),
                writes=RB_W)
            p.step("dve", lambda e: e.reduce_max(
                stc("wtmp", 1), RB[:, 0:2048], axis=AX.X,
                apply_absolute_value=True),
                reads=RB_W, writes=["stc:wtmp"])
            if c == 0:
                p.step("dve", lambda e: e.tensor_copy(stc("wmax", 1), stc("wtmp", 1)),
                       reads=["stc:wtmp"], writes=["stc:wmax"])
            else:
                p.step("dve", lambda e: e.tensor_max(
                    stc("wmax", 1), stc("wmax", 1), stc("wtmp", 1)),
                    reads=["stc:wtmp", "stc:wmax"], writes=["stc:wmax"])
            yield

    def wff_xform_gen():
        RB_W = ["rb:0:0", "rb:0:1", "rb:0:2", "rb:0:3"]
        p.dma("pool", lambda e: e.dma_start(
            out=wmax_d.rearrange("(d a) -> d a", a=1), in_=stc("wmax", 1)),
            reads=["stc:wmax"], writes=["wmax_d"])
        p.dma("sp", lambda e: e.dma_start(
            out=WROW[0:1, 0:128],
            in_=wmax_d.rearrange("(a d) -> a d", a=1)),
            reads=["wmax_d"], writes=["wrow"])
        p.step("dve", lambda e: e.reduce_max(stc("scale", 1)[0:1, 0:1],
                                             WROW[0:1, 0:128], axis=AX.X),
               reads=["wrow"], writes=["stc:scale"])
        yield
        p.dma("pool", lambda e: e.dma_start(
            out=RB[0:1, 0:2048], in_=Gam.rearrange("(a d) -> a d", a=1)),
            writes=RB_W)
        p.step("dve", lambda e: e.tensor_scalar_mul(
            ROWS16[0:1, 0:2048], RB[0:1, 0:2048], stc("scale", 1)[0:1, 0:1]),
            reads=RB_W + ["stc:scale"], writes=["rrow"])
        for j in range(4):
            p.step("pe", (lambda j_: lambda e: e.matmul(
                pb[0][:, 0:512], ones_r16[:], ROWS16[0:1, j_ * 512:(j_ + 1) * 512],
                start=True, stop=True))(j),
                reads=["ones_r16", "rrow"], writes=["pb0"])
            p.step("act", (lambda j_: lambda e: e.copy(
                G0[:, j_ * 512:(j_ + 1) * 512], pb[0][:, 0:512]))(j),
                reads=["pb0"], writes=["g0:0"])
        p.dma("pool", lambda e: e.dma_start(
            out=RB[0:1, 2048:4096], in_=Bet.rearrange("(a d) -> a d", a=1)),
            writes=["rb:1:0", "rb:1:1", "rb:1:2", "rb:1:3"])
        yield
        for c in range(16):
            p.dma("sp", (lambda c_: lambda e: e.dma_start_transpose(
                WS1[:, 0:2048], wf16[:, c_ * 128:(c_ + 1) * 128]))(c),
                reads=["wf16"], writes=["wsB:0", "wsB:1", "wsB:2", "wsB:3"])
            p.step("act", lambda e: e.sign(WS1[:, 2048:4096], WS1[:, 0:2048]),
                   reads=["wsB:0", "wsB:1", "wsB:2", "wsB:3"],
                   writes=["wsB:4", "wsB:5", "wsB:6", "wsB:7"])
            for g in range(4):
                p.step("pe", (lambda c_, g_: lambda e: e.matmul(
                    pb[2 + g_][0:1, 0:512], lnbc16[:, c_:c_ + 1],
                    WS1[:, 2048 + g_ * 512:2048 + (g_ + 1) * 512],
                    start=(c_ == 0), stop=(c_ == 15)))(c, g),
                    reads=["lnbc16", f"wsB:{4 + g}"], writes=[f"pb{2 + g}"])
            p.step("dve", lambda e: e.tensor_mul(
                WS1[:, 4096:6144], WS1[:, 2048:4096], G0[:, 0:2048]),
                reads=["wsB:4", "wsB:5", "wsB:6", "wsB:7", "g0:0"],
                writes=["wsB:8", "wsB:9", "wsB:10", "wsB:11"])
            p.step("dve", (lambda c_: lambda e: e.tensor_scalar_mul(
                WS1[:, 6144:8192], WS1[:, 4096:6144], lnwc[:, c_:c_ + 1]))(c),
                reads=["wsB:8", "wsB:9", "wsB:10", "wsB:11", "lnwc"],
                writes=["wsB:12", "wsB:13", "wsB:14", "wsB:15"])
            for g in range(4):
                p.dma("pool", (lambda c_, g_: lambda e: e.dma_start(
                    out=wffh[g_][c_ * 128:(c_ + 1) * 128, :],
                    in_=WS1[:, 6144 + g_ * 512:6144 + (g_ + 1) * 512]))(c, g),
                    reads=[f"wsB:{12 + g}"], writes=[f"wff:{g}"])
            yield
        for g in range(4):
            p.step("dve", (lambda g_: lambda e: e.tensor_mul(
                XRB[0:1, 0:512], pb[2 + g_][0:1, 0:512],
                ROWS16[0:1, g_ * 512:(g_ + 1) * 512]))(g),
                reads=[f"pb{2 + g}", "rrow"], writes=["xrb:0"])
            p.step("dve", (lambda g_: lambda e: e.tensor_add(
                XRB[0:1, 512:1024], XRB[0:1, 0:512],
                RB[0:1, 2048 + g_ * 512:2048 + (g_ + 1) * 512]))(g),
                reads=["xrb:0", f"rb:1:{g}"], writes=["xrb:1"])
            p.step("dve", (lambda g_: lambda e: e.tensor_copy(
                ROWS16[0:1, g_ * 512:(g_ + 1) * 512], XRB[0:1, 512:1024]))(g),
                reads=["xrb:1"], writes=["rrow"])
        yield

    def wor_stage_a(s):
        og, h = s // 16, s % 16
        p.dma("sp", (lambda o_, h_, s_: lambda e: e.dma_start_transpose(
            xn_fm[:, s_ * 512:(s_ + 1) * 512],
            wo_a16[o_ * 512:(o_ + 1) * 512, h_ * 128:(h_ + 1) * 128]))(
                og, h, s),
            reads=["wo_a16"], writes=[f"xnfm:{s // 4}"])

    def wor_stage_b(s):
        og, h = s // 16, s % 16
        p.dma("sp", (lambda o_, h_, s_: lambda e: e.dma_start(
            out=xn_fm[:, s_ * 512:(s_ + 1) * 512],
            in_=wo_b16[o_][h_ * 128:(h_ + 1) * 128, :]))(og, h, s),
            reads=[f"wo_b16:{og}"], writes=[f"xnfm:{s // 4}"])

    def t_stats_slice(tt, og, src_slot):
        p.step("act", (lambda s_, c_: lambda e: e.activation(
            XR[:, 0:512], XN2[:, s_ * 512:(s_ + 1) * 512],
            AF.Square, accum_out=SSQ4[:, c_:c_ + 1]))(src_slot, tt * 4 + og),
            reads=[f"xn2:{src_slot}"], writes=["xr:0", f"ssq4:{tt}"])

    def emit_t1_tile(tt, og, pbi, x_src, x_src_key, x_dst, x_dst_key, final):
        oslot = (tt * 4 + og) % 4
        nslot = (tt * 4 + og) % 4
        src_key = "x_in" if x_src_key == "x_in" else f"{x_src_key}:{tt}:{og}"
        p.dma("act", (lambda o_, a_: lambda e: e.dma_start(
            out=XO[:, o_ * 512:(o_ + 1) * 512], in_=a_))(oslot,
            x_src[tt * 128:(tt + 1) * 128, og * 512:(og + 1) * 512]),
            reads=[src_key], writes=[f"xo:{oslot}"])
        p.step("dve", (lambda o_, n_, c_, i_: lambda e: e.scalar_tensor_tensor(
            XN2[:, n_ * 512:(n_ + 1) * 512], pb[i_][:, 0:512], 0.0,
            XO[:, o_ * 512:(o_ + 1) * 512], ALU.bypass, ALU.add,
            accum_out=SUM4[:, c_:c_ + 1]))(oslot, nslot, tt * 4 + og, pbi),
            reads=[f"pb{pbi}", f"xo:{oslot}"],
            writes=[f"xn2:{nslot}", f"sum4:{tt}"])
        p.dma("pool", (lambda n_: lambda e: e.dma_start(
            out=x_dst[tt * 128:(tt + 1) * 128, og * 512:(og + 1) * 512],
            in_=XN2[:, n_ * 512:(n_ + 1) * 512]))(nslot),
            reads=[f"xn2:{nslot}"], writes=[f"{x_dst_key}:{tt}:{og}"])
        if not final:
            t_stats_slice(tt, og, nslot)

    def emit_tstats(ffn_next, grp):
        lo, n = grp * 4, 4
        cs = lambda name: stc(name)[:, lo:lo + n]
        rkeys = [f"sum4:{tt}" for tt in range(lo, lo + n)]
        rkeys2 = [f"ssq4:{tt}" for tt in range(lo, lo + n)]
        p.step("dve", (lambda l_: lambda e: e.reduce_sum(
            stc("s1")[:, l_:l_ + 4],
            SUM4[:, l_ * 4:(l_ + 4) * 4].rearrange("p (t f) -> p t f", f=4),
            axis=AX.X))(lo),
            reads=rkeys, writes=[f"stc:s1:{grp}"])
        p.step("dve", (lambda l_: lambda e: e.reduce_sum(
            stc("q1")[:, l_:l_ + 4],
            SSQ4[:, l_ * 4:(l_ + 4) * 4].rearrange("p (t f) -> p t f", f=4),
            axis=AX.X))(lo),
            reads=rkeys2, writes=[f"stc:q1:{grp}"])
        p.step("dve", lambda e: e.tensor_scalar_mul(cs("m"), cs("s1"), 1.0 / D),
               reads=[f"stc:s1:{grp}"], writes=[f"stc:m:{grp}"])
        p.step("dve", lambda e: e.tensor_mul(cs("msq"), cs("m"), cs("m")),
               reads=[f"stc:m:{grp}"], writes=[f"stc:msq:{grp}"])
        p.step("dve", lambda e: e.scalar_tensor_tensor(
            cs("var"), cs("q1"), 1.0 / D, cs("msq"), ALU.mult, ALU.subtract),
            reads=[f"stc:q1:{grp}", f"stc:msq:{grp}"], writes=[f"stc:var:{grp}"])
        p.step("act", lambda e: e.activation(
            cs("t1"), cs("var"), AF.Ln, bias=epsc[:, 0:1]),
            reads=[f"stc:var:{grp}", "epsc"], writes=[f"stc:t1:{grp}"])
        p.step("act", lambda e: e.activation(cs("rs"), cs("t1"), AF.Exp,
                                             scale=-0.5),
               reads=[f"stc:t1:{grp}"], writes=[f"stc:rs:{grp}"])
        if not ffn_next:
            p.step("dve", lambda e: e.tensor_copy(cs("A"), cs("rs")),
                   reads=[f"stc:rs:{grp}"], writes=[f"stc:A:{grp}"])
        else:
            p.step("dve", lambda e: e.tensor_mul(cs("t2"), cs("var"), cs("rs")),
                   reads=[f"stc:var:{grp}", f"stc:rs:{grp}"],
                   writes=[f"stc:t2:{grp}"])
            p.step("dve", lambda e: e.tensor_mul(cs("t2"), cs("t2"), cs("rs")),
                   reads=[f"stc:rs:{grp}", f"stc:t2:{grp}"],
                   writes=[f"stc:t2:{grp}"])
            p.step("act", lambda e: e.activation(
                cs("t1"), cs("t2"), AF.Ln, bias=epsc[:, 0:1]),
                reads=[f"stc:t2:{grp}", "epsc"], writes=[f"stc:t1:{grp}"])
            p.step("act", lambda e: e.activation(cs("t3"), cs("t1"),
                                                 AF.Exp, scale=-0.5),
                   reads=[f"stc:t1:{grp}"], writes=[f"stc:t3:{grp}"])
            p.step("dve", lambda e: e.tensor_mul(cs("A"), cs("rs"), cs("t3")),
                   reads=[f"stc:rs:{grp}", f"stc:t3:{grp}"],
                   writes=[f"stc:A:{grp}"])
        p.step("dve", lambda e: e.scalar_tensor_tensor(
            cs("B"), cs("m"), -1.0, cs("A"), ALU.mult, ALU.mult),
            reads=[f"stc:m:{grp}", f"stc:A:{grp}"], writes=[f"stc:B:{grp}"])

    def t2_norm(i, tt, akey, buf, s_, bkey):
        ps = i % 4
        p.step("dve", (lambda s2, ps_, t_, b_: lambda e: e.tensor_scalar(
            PST[:, ps_ * 512:(ps_ + 1) * 512],
            b_[:, s2 * 512:(s2 + 1) * 512],
            stc("A")[:, t_:t_ + 1], stc("B")[:, t_:t_ + 1],
            ALU.mult, ALU.add))(s_, ps, tt, buf),
            reads=[bkey, f"stc:A:{akey}", f"stc:B:{akey}"],
            writes=[f"pst:{ps}"])

    def t2_load_norm(i, tt, og, x_src, x_src_key, akey):
        xs = i % 4
        src_key = ("x_in" if x_src_key == "x_in"
                   else f"{x_src_key}:{tt}:{og}")
        p.dma("act" if xs % 2 == 0 else "sp",
              (lambda xs_, a_: lambda e: e.dma_start(
            out=XRB[:, xs_ * 512:(xs_ + 1) * 512], in_=a_))(xs,
            x_src[tt * 128:(tt + 1) * 128, og * 512:(og + 1) * 512]),
            reads=[src_key], writes=[f"xrb:{xs}"])
        t2_norm(i, tt, akey, XRB, xs, f"xrb:{xs}")

    def t2_transpose(i, tt, og):
        ps = i % 4
        bank = 4 + (i % 2)
        for j in range(4):
            p.step("pe", (lambda ps_, j_, b_: lambda e: e.matmul(
                pb[b_][:, j_ * 128:(j_ + 1) * 128],
                PST[:, ps_ * 512 + j_ * 128:ps_ * 512 + (j_ + 1) * 128],
                IDT[:], start=True, stop=True))(ps, j, bank),
                reads=[f"pst:{ps}", "idt"], writes=[f"pb{bank}"])
        off = (tt % HTT) * 128
        p.step("act", (lambda b_, g_, o_: lambda e: e.copy(
            xn_fm[:].rearrange("p (c x) -> p c x", x=TH)[
                :, g_ * 4:(g_ + 1) * 4, o_:o_ + 128],
            pb[b_][:, 0:512].rearrange("p (c x) -> p c x", x=128)))(
                bank, og, off),
            reads=[f"pb{bank}"],
            writes=[f"xnfm:{og * 4 + j}" for j in range(4)])

    T2D = 3

    def t2_gen(x_src, x_src_key, grps, t0_keys=False):
        slices = []
        for grp in grps:
            for tt in range(grp * 4, (grp + 1) * 4):
                ak = f"t{tt}" if t0_keys else str(grp)
                for og in range(4):
                    slices.append((tt, og, ak))
        n = len(slices)
        for i, (tt, og, ak) in enumerate(slices):
            t2_load_norm(i, tt, og, x_src, x_src_key, ak)
            hb()
            if i >= T2D:
                t2_transpose(i - T2D, slices[i - T2D][0], slices[i - T2D][1])
                yield
        for i in range(max(0, n - T2D), n):
            t2_transpose(i, slices[i][0], slices[i][1])
            yield

    def drain(gen):
        if gen is None:
            return
        for _ in gen:
            pass

    def step_n(gen, n):
        if gen is None:
            return
        for _ in range(n):
            try:
                next(gen)
            except StopIteration:
                return

    def drive_t2(main_gen, bg, k, t2g, per, batch):
        i = 0
        for _ in main_gen:
            if bg is not None:
                bg.step(k)
            i += 1
            if i % per == 0:
                step_n(t2g, batch)

    def emit_tstats1(ffn_next, tt):
        cs = lambda name: stc(name)[:, tt:tt + 1]
        g = f"t{tt}"
        p.step("dve", lambda e: e.reduce_sum(
            cs("s1"), SUM4[:, tt * 4:tt * 4 + 4], axis=AX.X),
            reads=[f"sum4:{tt}"], writes=[f"stc:s1:{g}"])
        p.step("dve", lambda e: e.reduce_sum(
            cs("q1"), SSQ4[:, tt * 4:tt * 4 + 4], axis=AX.X),
            reads=[f"ssq4:{tt}"], writes=[f"stc:q1:{g}"])
        p.step("dve", lambda e: e.tensor_scalar_mul(cs("m"), cs("s1"), 1.0 / D),
               reads=[f"stc:s1:{g}"], writes=[f"stc:m:{g}"])
        p.step("dve", lambda e: e.tensor_mul(cs("msq"), cs("m"), cs("m")),
               reads=[f"stc:m:{g}"], writes=[f"stc:msq:{g}"])
        p.step("dve", lambda e: e.scalar_tensor_tensor(
            cs("var"), cs("q1"), 1.0 / D, cs("msq"), ALU.mult, ALU.subtract),
            reads=[f"stc:q1:{g}", f"stc:msq:{g}"], writes=[f"stc:var:{g}"])
        p.step("act", lambda e: e.activation(
            cs("t1"), cs("var"), AF.Ln, bias=epsc[:, 0:1]),
            reads=[f"stc:var:{g}", "epsc"], writes=[f"stc:t1:{g}"])
        p.step("act", lambda e: e.activation(cs("rs"), cs("t1"), AF.Exp,
                                             scale=-0.5),
               reads=[f"stc:t1:{g}"], writes=[f"stc:rs:{g}"])
        if not ffn_next:
            p.step("dve", lambda e: e.tensor_copy(cs("A"), cs("rs")),
                   reads=[f"stc:rs:{g}"], writes=[f"stc:A:{g}"])
        else:
            p.step("dve", lambda e: e.tensor_mul(cs("t2"), cs("var"), cs("rs")),
                   reads=[f"stc:var:{g}", f"stc:rs:{g}"], writes=[f"stc:t2:{g}"])
            p.step("dve", lambda e: e.tensor_mul(cs("t2"), cs("t2"), cs("rs")),
                   reads=[f"stc:rs:{g}", f"stc:t2:{g}"], writes=[f"stc:t2:{g}"])
            p.step("act", lambda e: e.activation(
                cs("t1"), cs("t2"), AF.Ln, bias=epsc[:, 0:1]),
                reads=[f"stc:t2:{g}", "epsc"], writes=[f"stc:t1:{g}"])
            p.step("act", lambda e: e.activation(cs("t3"), cs("t1"),
                                                 AF.Exp, scale=-0.5),
                   reads=[f"stc:t1:{g}"], writes=[f"stc:t3:{g}"])
            p.step("dve", lambda e: e.tensor_mul(cs("A"), cs("rs"), cs("t3")),
                   reads=[f"stc:rs:{g}", f"stc:t3:{g}"], writes=[f"stc:A:{g}"])
        p.step("dve", lambda e: e.scalar_tensor_tensor(
            cs("B"), cs("m"), -1.0, cs("A"), ALU.mult, ALU.mult),
            reads=[f"stc:m:{g}", f"stc:A:{g}"], writes=[f"stc:B:{g}"])

    def _t0_slot_of(tt, og):
        if tt < HTT:
            if og < 3:
                return XO, og, f"xo:{og}"
            return XRB, tt % 2, f"xrb:{tt % 2}"
        s_ = (tt * 4 + og) % 4
        return XO, s_, f"xo:{s_}"

    def _t0_tile_stats(tt, ffn_next):
        for og in range(4):
            buf, s_, bkey = _t0_slot_of(tt, og)
            p.dma("sp" if og % 2 == 0 else "act",
                  (lambda o_, t_, s2, b_: lambda e: e.dma_start(
                      out=b_[:, s2 * 512:(s2 + 1) * 512],
                      in_=X[t_ * 128:(t_ + 1) * 128,
                            o_ * 512:(o_ + 1) * 512]))(og, tt, s_, buf),
                  writes=[bkey])
            p.step("dve", (lambda s2, c_, b_: lambda e: e.reduce_sum(
                SUM4[:, c_:c_ + 1], b_[:, s2 * 512:(s2 + 1) * 512],
                axis=AX.X))(s_, tt * 4 + og, buf),
                reads=[bkey], writes=[f"sum4:{tt}"])
            p.step("act", (lambda s2, n_, c_, b_: lambda e: e.activation(
                XN2[:, n_ * 512:(n_ + 1) * 512],
                b_[:, s2 * 512:(s2 + 1) * 512],
                AF.Square, accum_out=SSQ4[:, c_:c_ + 1]))(
                    s_, og % 2, tt * 4 + og, buf),
                reads=[bkey],
                writes=[f"xn2:{og % 2}", f"ssq4:{tt}"])
        emit_tstats1(ffn_next, tt)

    def emit_t0_h0(ffn_next, wff_gen=None):
        t0i = 0
        t0q = []
        for tt in range(HTT):
            _t0_tile_stats(tt, ffn_next)
            hb()
            if tt == 12:
                stage_wT_og(wk_a16, "wk_a16", 0, "wsA")
            for og in range(4):
                buf, s_, bkey = _t0_slot_of(tt, og)
                t2_norm(t0i, tt, f"t{tt}", buf, s_, bkey)
                t0q.append((tt, og))
                if t0i >= T2D:
                    t2_transpose(t0i - T2D, *t0q[t0i - T2D])
                t0i += 1
            if tt % 2 == 1:
                step_n(wff_gen, 1)
        for i in range(max(0, t0i - T2D), t0i):
            t2_transpose(i, *t0q[i])

    def t0_h1_gen(ffn_next):
        for tt in range(HTT, NTT):
            _t0_tile_stats(tt, ffn_next)
            yield

    WSBUF = {"wsA": WS0, "wsB": WS1}

    def stage_rows(src_og, src_key, wskey="wsA"):
        ws = WSBUF[wskey]
        for c in range(NC):
            p.dma("sp", (lambda c_, w_: lambda e: e.dma_start(
                out=w_[:, c_ * 512:(c_ + 1) * 512],
                in_=src_og[c_ * 128:(c_ + 1) * 128, :]))(c, ws),
                reads=[src_key], writes=[f"{wskey}:{c}"])

    def stage_wT_og(src16, src_key, og, wskey="wsA"):
        ws = WSBUF[wskey]
        for c in range(NC):
            p.dma("sp", (lambda c_, g_, w_: lambda e: e.dma_start_transpose(
                w_[:, c_ * 512:(c_ + 1) * 512],
                src16[g_ * 512:(g_ + 1) * 512, c_ * 128:(c_ + 1) * 128]))(c, og, ws),
                reads=[src_key], writes=[f"{wskey}:{c}"])

    def stage_tables(tg_abs, ring):
        p.dma("sp", (lambda r_, t_: lambda e: e.dma_start(
            out=CSR[:, r_ * 1024:r_ * 1024 + 512],
            in_=C_COS[:, t_ * 512:(t_ + 1) * 512]))(ring, tg_abs),
            writes=[f"csrc:{ring}"])
        p.dma("sp", (lambda r_, t_: lambda e: e.dma_start(
            out=CSR[:, r_ * 1024 + 512:r_ * 1024 + 1024],
            in_=C_SIN[:, t_ * 512:(t_ + 1) * 512]))(ring, tg_abs),
            writes=[f"csrs:{ring}"])

    def proj_mms(pbi, och, tgl, wskey="wsA"):
        ws = WSBUF[wskey]
        for c in range(NC):
            p.step("pe", (lambda c_, o_, t_, i_, w_: lambda e: e.matmul(
                pb[i_][:, 0:512],
                w_[:, c_ * 512 + o_ * 128:c_ * 512 + (o_ + 1) * 128],
                xn_fm[:, c_ * TH + t_ * 512:c_ * TH + (t_ + 1) * 512],
                start=(c_ == 0), stop=(c_ == 15)))(c, och, tgl, pbi, ws),
                reads=[f"{wskey}:{c}", f"xnfm:{c}"], writes=[f"pb{pbi}"])

    def rope_elu_evac(pbi, ring, dst, dst_key, tg_abs, rr):
        cosap = CSR[:, ring * 1024:ring * 1024 + 512]
        sinap = CSR[:, ring * 1024 + 512:ring * 1024 + 1024]
        b0, b1, b2, b3 = rr * 4, rr * 4 + 1, rr * 4 + 2, rr * 4 + 3
        k0, k1, k2, k3 = (f"rb:{rr}:0", f"rb:{rr}:1", f"rb:{rr}:2", f"rb:{rr}:3")
        p.step("dve", (lambda i_, c_, a_: lambda e: e.tensor_mul(
            rb(a_), pb[i_][:, 0:512], c_))(pbi, cosap, b0),
            reads=[f"pb{pbi}", f"csrc:{ring}"], writes=[k0])
        p.step("dve", (lambda i_, s_, a_: lambda e: e.tensor_mul(
            rb(a_)[0:64, :], pb[i_][64:128, 0:512], s_[64:128, :]))(pbi, sinap, b1),
            reads=[f"pb{pbi}", f"csrs:{ring}"], writes=[k1 + "lo"])
        p.step("dve", (lambda i_, s_, a_: lambda e: e.tensor_mul(
            rb(a_)[64:128, :], pb[i_][0:64, 0:512], s_[0:64, :]))(pbi, sinap, b1),
            reads=[f"pb{pbi}", f"csrs:{ring}"], writes=[k1 + "hi"])
        p.step("dve", (lambda a_, x_, y_: lambda e: e.tensor_add(
            rb(a_), rb(x_), rb(y_)))(b2, b0, b1),
            reads=[k0, k1 + "lo", k1 + "hi"], writes=[k2])
        p.step("act", (lambda a_, x_: lambda e: e.activation(
            rb(a_), rb(x_), AF.Relu, scale=-1.0))(b3, b2),
            reads=[k2], writes=[k3])
        p.step("act", (lambda a_, x_: lambda e: e.activation(
            rb(a_), rb(x_), AF.Exp, scale=-1.0))(b0, b3),
            reads=[k3], writes=[k0])
        ps_slot = rr
        p.step("dve", (lambda s_, x_, y_: lambda e: e.scalar_tensor_tensor(
            XT[:, s_ * 512:(s_ + 1) * 512], rb(x_), 0.0, rb(y_),
            ALU.max, ALU.add))(ps_slot, b2, b0),
            reads=[k0, k2], writes=[f"xt:{ps_slot}"])
        p.dma("pool", (lambda s_, t_: lambda e: e.dma_start(
            out=dst[:, t_ * 512:(t_ + 1) * 512],
            in_=XT[:, s_ * 512:(s_ + 1) * 512]))(ps_slot, tg_abs),
            reads=[f"xt:{ps_slot}"], writes=[dst_key])

    def emit_attn_a(pre, bg0=None):
        wsrot = [1]

        def nextws():
            wsrot[0] ^= 1
            return "wsA" if wsrot[0] == 0 else "wsB"

        def q_proj_og(half, og):
            wk_ = nextws()
            stage_wT_og(wq_a16, "wq_a16", og, wk_)
            for tgl in range(4):
                tg_abs = half * 4 + tgl
                ring = 0
                stage_tables(tg_abs, ring)
                for och in range(4):
                    h = og * 4 + och
                    pbi = (tg_abs * 4 + och) % 2
                    proj_mms(pbi, och, tgl, wk_)
                    rope_elu_evac(pbi, ring, qh_a[h], f"qh_a:{h}", tg_abs,
                                  (tg_abs * 4 + och) % 2)
                    yield

        def k_proj(half):
            wk_ = nextws()
            if half != 0:
                stage_wT_og(wk_a16, "wk_a16", 0, wk_)
            for tgl in range(4):
                tg_abs = half * 4 + tgl
                ring = 0
                stage_tables(tg_abs, ring)
                for och in range(4):
                    pbi = (tg_abs * 4 + och) % 2
                    proj_mms(pbi, och, tgl, wk_)
                    rope_elu_evac(pbi, ring, kh_a[och], f"kh_a:{och}", tg_abs,
                                  (tg_abs * 4 + och) % 2)
                    yield

        def v_proj(half):
            wk_ = nextws()
            stage_wT_og(wv_a16, "wv_a16", 0, wk_)
            wsv = WSBUF[wk_]
            for ttl in range(HTT):
                tt = half * HTT + ttl
                pbi = tt % 2
                for c in range(NC):
                    p.step("pe", (lambda c_, t_, i_, w_: lambda e: e.matmul(
                        pb[i_][:, 0:512],
                        xn_fm[:, c_ * TH + t_ * 128:c_ * TH + (t_ + 1) * 128],
                        w_[:, c_ * 512:(c_ + 1) * 512],
                        start=(c_ == 0), stop=(c_ == 15)))(c, ttl, pbi, wsv),
                        reads=[f"{wk_}:{c}", f"xnfm:{c}"], writes=[f"pb{pbi}"])
                aslot = 2 + tt % 2
                p.step("act", (lambda s_, i_: lambda e: e.copy(
                    XT[:, s_ * 512:(s_ + 1) * 512], pb[i_][:, 0:512]))(aslot, pbi),
                    reads=[f"pb{pbi}"], writes=[f"xt:{aslot}"])
                for kvj in range(4):
                    p.dma("pool", (lambda s_, k_, t_: lambda e: e.dma_start(
                        out=vh_a[k_][t_ * 128:(t_ + 1) * 128, :],
                        in_=XT[:, s_ * 512 + k_ * 128:s_ * 512 + (k_ + 1) * 128]))(
                            aslot, kvj, tt),
                        reads=[f"xt:{aslot}"], writes=[f"vh_a:{kvj}"])
                yield

        def scan_kv(kv):
            yield
            for part in range(4):
                p.dma("sp", (lambda k_, i_: lambda e: e.dma_start(
                    out=G0[:, i_ * 1024:(i_ + 1) * 1024],
                    in_=kh_a[k_][:, i_ * 1024:(i_ + 1) * 1024]))(kv, part),
                    reads=[f"kh_a:{kv}"], writes=[f"g0:{part}"])
                p.dma("pool", (lambda k_, i_: lambda e: e.dma_start(
                    out=G1[:].rearrange("p (i j) -> p i j", j=128)[
                        :, i_ * 8:(i_ + 1) * 8, :],
                    in_=vh_a[k_].rearrange("(i p) j -> p i j", p=128)[
                        :, i_ * 8:(i_ + 1) * 8, :]))(kv, part),
                    reads=[f"vh_a:{kv}"], writes=[f"g1:{part}"])
            for i in range(32):
                p.dma("sp", (lambda k_, i_: lambda e: e.dma_start_transpose(
                    G2[:, i_ * 128:(i_ + 1) * 128],
                    kh_a[k_][:, i_ * 128:(i_ + 1) * 128]))(kv, i),
                    reads=[f"kh_a:{kv}"], writes=[f"g2:{i // 8}"])
            p.step("dve", lambda e: e.memset(ZC[:, 0:1], 0.0), writes=["z32"])
            p.step("dve", lambda e: e.memset(ST32[:], 0.0), writes=["st32"])
            for si in range(16):
                ca, cb = 2 * si, 2 * si + 1
                qp = si // 4
                p.step("pe", (lambda c_: lambda e: e.matmul(
                    pb[6][:, 0:128], G2[:, c_ * 128:(c_ + 1) * 128],
                    G1[:, c_ * 128:(c_ + 1) * 128],
                    start=True, stop=False))(ca),
                    reads=[f"g2:{qp}", f"g1:{qp}"], writes=["pb6"])
                p.step("pe", (lambda c_: lambda e: e.matmul(
                    pb[6][:, 0:128], G2[:, c_ * 128:(c_ + 1) * 128],
                    G1[:, c_ * 128:(c_ + 1) * 128],
                    start=False, stop=True))(cb),
                    reads=[f"g2:{qp}", f"g1:{qp}"], writes=["pb6"])
                p.step("dve", lambda e: e.tensor_add(
                    ST32[:], ST32[:], pb[6][:, 0:128]),
                    reads=["pb6", "st32"], writes=["st32"])
                p.step("act", (lambda s_: lambda e: e.copy(
                    G2[:, s_ * 128:(s_ + 1) * 128], ST32[:]))(si),
                    reads=["st32"], writes=[f"g2:{si // 8}"])
                p.step("dve", (lambda s_: lambda e: e.reduce_sum(
                    ZC[:, 1:2], G0[:, s_ * 256:(s_ + 1) * 256],
                    axis=AX.X))(si),
                    reads=[f"g0:{qp}"], writes=["ztmp"])
                p.step("dve", lambda e: e.tensor_add(
                    ZC[:, 0:1], ZC[:, 0:1], ZC[:, 1:2]),
                    reads=["ztmp", "z32"], writes=["z32"])
                p.step("dve", (lambda s_: lambda e: e.tensor_copy(
                    ZC[:, 4 + s_:5 + s_], ZC[:, 0:1]))(si),
                    reads=["z32"], writes=[f"zs:{si}"])
                yield
            for pp in range(2):
                hA, hB = kv * 4 + 2 * pp, kv * 4 + 2 * pp + 1

                def qwin_load(w):
                    ws_ = w % 4
                    p.dma("sp", (lambda w_, s_: lambda e: e.dma_start(
                        out=QF2[:, s_ * 512:s_ * 512 + 256],
                        in_=qh_a[hA][:, w_ * 256:(w_ + 1) * 256]))(w, ws_),
                        reads=[f"qh_a:{hA}"], writes=[f"qw:{ws_}"])
                    p.dma("pool", (lambda w_, s_: lambda e: e.dma_start(
                        out=QF2[:, s_ * 512 + 256:(s_ + 1) * 512],
                        in_=qh_a[hB][:, w_ * 256:(w_ + 1) * 256]))(w, ws_),
                        reads=[f"qh_a:{hB}"], writes=[f"qw:{ws_}"])

                qwin_load(0)
                pend = [None]

                def flush_epi():
                    if pend[0] is not None:
                        fn = pend[0]
                        pend[0] = None
                        fn()

                for si in range(17):
                    if si < 16:
                        ca, cb = 2 * si, 2 * si + 1
                        qp = si // 4
                        qs = si % 4
                        rhs_full = QF2[:, qs * 512:(qs + 1) * 512]
                        rhs_half = QF2[:, qs * 512:(qs + 1) * 512].rearrange(
                            "p (a b) -> p a b", b=256)[:, :, 128:256]
                        pslot = si % 2
                        qkeys = [f"qw:{qs}"]
                        p.step("pe", (lambda c_, r_, q_, ks: lambda e: e.matmul(
                            pb[2][:, 0:512], G0[:, c_ * 128:(c_ + 1) * 128], r_,
                            start=True, stop=True))(ca, rhs_full, qp, qkeys),
                            reads=[f"g0:{qp}"] + qkeys, writes=["pb2"])
                        if si < 15:
                            qwin_load(si + 1)
                        p.step("act", (lambda s_: lambda e: e.copy(
                            PST[:, s_ * 512:(s_ + 1) * 512],
                            pb[2][:, 0:512]))(pslot),
                            reads=["pb2"], writes=[f"pst:{pslot}"])
                        p.step("dve", (lambda s_: lambda e: e.tensor_mul(
                            PST[:, s_ * 512:(s_ + 1) * 512].rearrange(
                                "p (a b) -> p a b", b=256)[:, :, 0:128],
                            PST[:, s_ * 512:(s_ + 1) * 512].rearrange(
                                "p (a b) -> p a b", b=256)[:, :, 0:128],
                            maskha[:].rearrange(
                                "p (a b) -> p a b", b=256)[:, :, 0:128]))(pslot),
                            reads=[f"pst:{pslot}"], writes=[f"pst:{pslot}"])
                        p.step("pe", (lambda c_, r_: lambda e: e.matmul(
                            pb[2][:, 0:256], G0[:, c_ * 128:(c_ + 1) * 128], r_,
                            start=True, stop=True))(cb, rhs_half),
                            reads=[f"g0:{qp}"] + qkeys, writes=["pb2"])
                        p.step("dve", (lambda s_: lambda e: e.tensor_mul(
                            PST[:, (2 + s_) * 512:(2 + s_) * 512 + 256].rearrange(
                                "p (a b) -> p a b", b=128),
                            pb[2][:, 0:256].rearrange("p (a b) -> p a b", b=128),
                            maskha[:].rearrange(
                                "p (a b) -> p a b", b=256)[:, :, 0:128]))(pslot),
                            reads=["pb2", "mha"], writes=[f"pstm:{pslot}"])
                    hb()
                    if si >= 1:
                        sj = si - 1
                        caj, cbj = 2 * sj, 2 * sj + 1
                        qpj = sj // 4
                        qsj = sj % 4
                        rhs_fj = QF2[:, qsj * 512:(qsj + 1) * 512]
                        nb = 3 + (sj % 2)
                        psj = sj % 2
                        dr_ = sj % 2
                        qkj = [f"qw:{qsj}"]
                        papA = PST[:, psj * 512:(psj + 1) * 512]
                        papB = PST[:, (2 + psj) * 512:(2 + psj) * 512 + 256]
                        flush_epi()
                        if sj > 0:
                            p.step("act", (lambda s_: lambda e:
                                   e.activation(
                                       PST[:, 1280 + (s_ % 2) * 128:
                                           1280 + (s_ % 2 + 1) * 128],
                                       ONESM[:], AF.Identity,
                                       scale=ZC[:, 3 + s_:4 + s_]))(sj),
                                reads=["onesm", f"zs:{sj - 1}"],
                                writes=[f"zm:{sj % 2}"])
                            p.step("pe", (lambda r_, b_, s_: lambda e: e.matmul(
                                pb[b_][:, 0:512],
                                G2[:, (s_ - 1) * 128:s_ * 128], r_,
                                start=True, stop=False))(rhs_fj, nb, sj),
                                reads=[f"g2:{(sj - 1) // 8}"] + qkj,
                                writes=[f"pb{nb}"])
                            p.step("pe", (lambda r_, s_: lambda e: e.matmul(
                                pb[5][:, 0:512],
                                PST[:, 1280 + (s_ % 2) * 128:
                                    1280 + (s_ % 2 + 1) * 128], r_,
                                start=True, stop=False))(rhs_fj, sj),
                                reads=[f"zm:{sj % 2}"] + qkj, writes=["pb5"])
                        p.step("pe", (lambda c_, a_, b_, f_: lambda e: e.matmul(
                            pb[b_][:, 0:512], G1[:, c_ * 128:(c_ + 1) * 128],
                            a_, start=f_, stop=False))(caj, papA, nb, sj == 0),
                            reads=[f"g1:{qpj}", f"pst:{psj}"],
                            writes=[f"pb{nb}"])
                        p.step("pe", (lambda c_, a_, b_: lambda e: e.matmul(
                            pb[b_][:].rearrange("p (h q) -> p h q", h=2)[
                                :, :, 128:256],
                            G1[:, c_ * 128:(c_ + 1) * 128], a_,
                            start=False, stop=True))(cbj, papB, nb),
                            reads=[f"g1:{qpj}", f"pstm:{psj}"],
                            writes=[f"pb{nb}"])
                        p.step("pe", (lambda a_, f_: lambda e: e.matmul(
                            pb[5][:, 0:512], ONESM[:], a_,
                            start=f_, stop=False))(papA, sj == 0),
                            reads=["onesm", f"pst:{psj}"],
                            writes=["pb5"])
                        p.step("pe", (lambda a_: lambda e: e.matmul(
                            pb[5][:].rearrange("p (h q) -> p h q", h=2)[
                                :, :, 128:256],
                            ONESM[:], a_, start=False, stop=True))(papB),
                            reads=["onesm", f"pstm:{psj}"], writes=["pb5"])

                        def mk_epi(si_, dr2, nb2, hA_, hB_):
                            def epi():
                                p.step("act", lambda e: e.activation(
                                    rb(dr2 * 4), pb[5][:, 0:512], AF.Identity,
                                    bias=epsc[:, 1:2]),
                                    reads=["pb5", "epsca"],
                                    writes=[f"rb:{dr2}:0"])
                                p.step("dve", lambda e: e.reciprocal(
                                    rb(dr2 * 4 + 2), rb(dr2 * 4)),
                                    reads=[f"rb:{dr2}:0"],
                                    writes=[f"rb:{dr2}:2"])
                                p.step("dve", lambda e: e.tensor_mul(
                                    AOT[:, dr2 * 512:(dr2 + 1) * 512],
                                    pb[nb2][:, 0:512], rb(dr2 * 4 + 2)),
                                    reads=[f"pb{nb2}", f"rb:{dr2}:2"],
                                    writes=[f"aot:{dr2}"])
                                for hj, h in enumerate((hA_, hB_)):
                                    p.dma("pool", (lambda j_, h_: lambda e:
                                          e.dma_start(
                                              out=aoh_a[h_][:, si_ * 256:
                                                            (si_ + 1) * 256],
                                              in_=AOT[:, dr2 * 512 + j_ * 256:
                                                      dr2 * 512 +
                                                      (j_ + 1) * 256]))(
                                                  hj, h),
                                        reads=[f"aot:{dr2}"],
                                        writes=[f"aoh_a:{h}:{2 * si_}",
                                                f"aoh_a:{h}:{2 * si_ + 1}"])
                            return epi
                        pend[0] = mk_epi(sj, dr_, nb, hA, hB)
                    yield
                flush_epi()

        bg = bg0 if bg0 is not None else BG()
        for half in range(2):
            if half == 1:
                drain(pre.get("h1"))
            drive(k_proj(half), bg, 4)
            drive(v_proj(half), bg, 4)
            for og in range(4):
                if half == 0 and og == 3:
                    drive_t2(q_proj_og(half, og), bg, 4,
                             pre.get("h1"), 4, 16)
                else:
                    drive(q_proj_og(half, og), bg, 4)
                if half == 1:
                    bg.set(scan_kv(og))
        return bg

    def emit_attn_b(pre):
        emit_attn_b_setup()
        wsrot = [1]

        def nextws():
            wsrot[0] ^= 1
            return "wsA" if wsrot[0] == 0 else "wsB"

        def qk_og(half, og, wog, wkey, dsts, dkey, bias, bias_key):
            wk_ = nextws()
            stage_rows(wog[og], f"{wkey}:{og}", wk_)
            for tgl in range(4):
                tg_abs = half * 4 + tgl
                for och in range(4):
                    h = og * 4 + och
                    pbi = (tg_abs * 4 + och) % 2
                    if half == 0:
                        hb()
                    proj_mms(pbi, och, tgl, wk_)
                    ps_slot = (tg_abs * 4 + och) % 2
                    p.step("act", (lambda i_, s_, h_, b_: lambda e:
                           e.activation(
                               XT[:, s_ * 512:(s_ + 1) * 512],
                               pb[i_][:, 0:512], AF.Identity,
                               bias=b_[:, h_:h_ + 1]))(pbi, ps_slot, h, bias),
                        reads=[f"pb{pbi}", bias_key],
                        writes=[f"xt:{ps_slot}"])
                    p.dma("pool", (lambda s_, h_, t_, d_: lambda e:
                          e.dma_start(
                              out=d_[:, t_ * 512:(t_ + 1) * 512],
                              in_=XT[:, s_ * 512:(s_ + 1) * 512]))(
                                  ps_slot, h, tg_abs, dsts[h]),
                        reads=[f"xt:{ps_slot}"], writes=[f"{dkey}:{h}"])
                    yield

        def v_og(half, og):
            wk_ = nextws()
            stage_rows(wv_b16[og], f"wv_b16:{og}", wk_)
            wsv = WSBUF[wk_]
            for ttl in range(HTT):
                tt = half * HTT + ttl
                pbi = tt % 2
                for c in range(NC):
                    p.step("pe", (lambda c_, t_, i_, w_: lambda e: e.matmul(
                        pb[i_][:, 0:512],
                        xn_fm[:, c_ * TH + t_ * 128:c_ * TH + (t_ + 1) * 128],
                        w_[:, c_ * 512:(c_ + 1) * 512],
                        start=(c_ == 0), stop=False))(c, ttl, pbi, wsv),
                        reads=[f"{wk_}:{c}", f"xnfm:{c}"], writes=[f"pb{pbi}"])
                p.step("pe", (lambda g_, i_: lambda e: e.matmul(
                    pb[i_][:, 0:512], ones_r16[:],
                    ROWS16[0:1, 4096 + g_ * 512:4096 + (g_ + 1) * 512],
                    start=False, stop=True))(og, pbi),
                    reads=["ones_r16", "bvrow"], writes=[f"pb{pbi}"])
                aslot = 2 + tt % 2
                p.step("act", (lambda s_, i_: lambda e: e.copy(
                    XT[:, s_ * 512:(s_ + 1) * 512], pb[i_][:, 0:512]))(
                        aslot, pbi),
                    reads=[f"pb{pbi}"], writes=[f"xt:{aslot}"])
                for hh in range(4):
                    h = og * 4 + hh
                    p.dma("pool", (lambda s_, j_, h_, t_: lambda e: e.dma_start(
                        out=vh_b[h_][t_ * 128:(t_ + 1) * 128, :],
                        in_=XT[:, s_ * 512 + j_ * 128:
                                s_ * 512 + (j_ + 1) * 128]))(aslot, hh, h, tt),
                        reads=[f"xt:{aslot}"], writes=[f"vh_b:{h}"])
                yield

        pend_f = [None]

        def flush_f():
            if pend_f[0] is not None:
                fn = pend_f[0]
                pend_f[0] = None
                fn()

        def flash_head(h):
            yield
            for part in range(4):
                p.dma("sp", (lambda h_, i_: lambda e: e.dma_start(
                    out=G0[:, i_ * 1024:(i_ + 1) * 1024],
                    in_=kh_b[h_][:, i_ * 1024:(i_ + 1) * 1024]))(h, part),
                    reads=[f"kh_b:{h}"], writes=[f"g0:{part}"])
                p.dma("pool", (lambda h_, i_: lambda e: e.dma_start(
                    out=G1[:].rearrange("p (i j) -> p i j", j=128)[
                        :, i_ * 8:(i_ + 1) * 8, :],
                    in_=vh_b[h_].rearrange("(i p) j -> p i j", p=128)[
                        :, i_ * 8:(i_ + 1) * 8, :]))(h, part),
                    reads=[f"vh_b:{h}"], writes=[f"g1:{part}"])
                p.dma("sp", (lambda h_, i_: lambda e: e.dma_start(
                    out=G2[:, i_ * 1024:(i_ + 1) * 1024],
                    in_=qh_b[h_][:, i_ * 1024:(i_ + 1) * 1024]))(h, part),
                    reads=[f"qh_b:{h}"], writes=[f"g2:{part}"])
            for qg in range(8):
                nkt = 4 * qg + 4
                ob = 4 if qg % 2 == 0 else 5
                db = 6
                dr_ = qg % 2
                DEP = 3
                for kt in range(nkt + DEP):
                    if kt < nkt:
                        sbank = (2, 3, 7)[kt % 3]
                        p.step("pe", (lambda k_, b_, q_: lambda e: e.matmul(
                            pb[b_][:, 0:512], G0[:, k_ * 128:(k_ + 1) * 128],
                            G2[:, q_ * 512:(q_ + 1) * 512],
                            start=True, stop=True))(kt, sbank, qg),
                            reads=[f"g0:{kt // 8}", f"g2:{qg // 2}"],
                            writes=[f"pb{sbank}"])
                        pslot = kt % 4
                        p.step("act", (lambda b_, s_: lambda e: e.activation(
                            PST[:, s_ * 512:(s_ + 1) * 512], pb[b_][:, 0:512],
                            AF.Exp, scale=SM_SCALE))(sbank, pslot),
                            reads=[f"pb{sbank}"], writes=[f"pst:{pslot}"])
                        if kt >= 4 * qg:
                            r = kt - 4 * qg
                            p.step("dve", (lambda s_, r_: lambda e: e.tensor_mul(
                                PST[:, s_ * 512:(s_ + 1) * 512],
                                PST[:, s_ * 512:(s_ + 1) * 512],
                                dmask[:, r_ * 512:(r_ + 1) * 512]))(pslot, r),
                                reads=[f"pst:{pslot}", "dm"],
                                writes=[f"pst:{pslot}"])
                    if kt == 1:
                        flush_f()
                    if kt >= DEP:
                        j = kt - DEP
                        js = j % 4
                        pk = f"pst:{js}"
                        pap = PST[:, js * 512:(js + 1) * 512]
                        p.step("pe", (lambda k_, a_, f_, l_, b_: lambda e:
                               e.matmul(
                                   pb[b_][:, 0:512],
                                   G1[:, k_ * 128:(k_ + 1) * 128], a_,
                                   start=f_, stop=l_))(
                                       j, pap, j == 0, j == nkt - 1, ob),
                            reads=[f"g1:{j // 8}", pk], writes=[f"pb{ob}"])
                        p.step("pe", (lambda a_, f_, l_, d_: lambda e: e.matmul(
                            pb[d_][:, 0:512], ONESM[:], a_,
                            start=f_, stop=l_))(pap, j == 0, j == nkt - 1, db),
                            reads=["onesm", pk], writes=[f"pb{db}"])
                    yield
                p.step("act", (lambda d_, b_: lambda e: e.copy(
                    rb(d_ * 4), pb[b_][:, 0:512]))(dr_, db),
                    reads=[f"pb{db}"], writes=[f"rb:{dr_}:0"])

                def mk_epi_f(h_, qg_, dr2, ob2):
                    def epi():
                        p.step("dve", lambda e: e.reciprocal(
                            rb(dr2 * 4 + 2), rb(dr2 * 4)),
                            reads=[f"rb:{dr2}:0"], writes=[f"rb:{dr2}:2"])
                        p.step("dve", lambda e: e.tensor_mul(
                            AOT[:, dr2 * 512:(dr2 + 1) * 512],
                            pb[ob2][:, 0:512], rb(dr2 * 4 + 2)),
                            reads=[f"pb{ob2}", f"rb:{dr2}:2"],
                            writes=[f"aot:{dr2}"])
                        p.dma("pool", lambda e: e.dma_start(
                            out=aoh_b[h_][:, qg_ * 512:(qg_ + 1) * 512],
                            in_=AOT[:, dr2 * 512:(dr2 + 1) * 512]),
                            reads=[f"aot:{dr2}"], writes=[f"aoh_b:{h_}:{qg_}"])
                    return epi
                pend_f[0] = mk_epi_f(h, qg, dr_, ob)

        def flash_og(og):
            for hh in range(4):
                yield from flash_head(og * 4 + hh)

        bg = BG()
        for half in range(2):
            if half == 1:
                drain(pre.get("h1"))
            for og in range(4):
                if half == 0 and og == 0:
                    step_n(pre.get("h0"), 16)
                    drive_t2(qk_og(half, og, wq_b16, "wq_b16", qh_b, "qh_b",
                                   bqc, "bqc"), bg, 9, pre.get("h0"), 4, 16)
                else:
                    drive(qk_og(half, og, wq_b16, "wq_b16", qh_b, "qh_b",
                                bqc, "bqc"), bg, 9)
                drive(qk_og(half, og, wk_b16, "wk_b16", kh_b, "kh_b",
                            bkc, "bkc"), bg, 9)
                if half == 0 and og == 3:
                    drive_t2(v_og(half, og), bg, 9, pre.get("h1"), 1, 4)
                else:
                    drive(v_og(half, og), bg, 9)
                if half == 1:
                    bg.set(flash_og(og))
        bg.finish()
        flush_f()
        return bg

    def emit_outproj_attn(aoh, aoh_key, wor_stage, with_bo, x_src, x_src_key,
                          x_dst, x_dst_key, final, boundary_ffn=(False,),
                          bg=None, subkeys=None, hb_ok=False):
        if subkeys is None:
            subkeys = lambda hh, blk: [f"{aoh_key}:{hh}:{blk}"]
        for s in range(64):
            wor_stage(s)
            if bg is not None:
                bg.step(3)
        for blk in range(8):
            need = [k for hh in range(NH) for k in subkeys(hh, blk)]
            if bg is not None:
                bg.step(12)
                bg.step_until(p, need)
            for hh in range(NH):
                p.dma("sp", (lambda h_, b_: lambda e: e.dma_start(
                    out=WS0[:, h_ * 512:(h_ + 1) * 512],
                    in_=aoh[h_][:, b_ * 512:(b_ + 1) * 512]))(hh, blk),
                    reads=subkeys(hh, blk), writes=[f"wsA:{hh}"])
            for og in range(4):
                for ttl in range(4):
                    if hb_ok:
                        hb()
                    tt = blk * 4 + ttl
                    pbi = tt % 2
                    for hh in range(NH):
                        s = og * 16 + hh
                        p.step("pe", (lambda h_, t_, i_, s_: lambda e: e.matmul(
                            pb[i_][:, 0:512],
                            WS0[:, h_ * 512 + t_ * 128:h_ * 512 + (t_ + 1) * 128],
                            xn_fm[:, s_ * 512:(s_ + 1) * 512],
                            start=(h_ == 0),
                            stop=(not with_bo and h_ == NH - 1)))(
                                hh, ttl, pbi, s),
                            reads=[f"wsA:{hh}", f"xnfm:{s // 4}"],
                            writes=[f"pb{pbi}"])
                    if with_bo:
                        p.step("pe", (lambda g_, i_: lambda e: e.matmul(
                            pb[i_][:, 0:512], ones_r16[:],
                            ROWS16[0:1, 2048 + g_ * 512:2048 + (g_ + 1) * 512],
                            start=False, stop=True))(og, pbi),
                            reads=["ones_r16", "borow"], writes=[f"pb{pbi}"])
                    emit_t1_tile(tt, og, pbi, x_src, x_src_key,
                                 x_dst, x_dst_key, final)
            if not final:
                emit_tstats(boundary_ffn[0], blk)
        if bg is not None:
            bg.finish()

    def emit_ffn_mm(x_src, x_src_key, x_dst, x_dst_key, final,
                    boundary_ffn=(False,), pre=None):
        pre = pre or {}
        wsrot = [1]

        def nextws():
            wsrot[0] ^= 1
            return "wsA" if wsrot[0] == 0 else "wsB"

        for half in range(2):
            if half == 1:
                drain(pre.get("h0"))
                drain(pre.get("h1"))
            for og in range(4):
                wk_ = nextws()
                stage_rows(wffh[og], f"wff:{og}", wk_)
                wsv = WSBUF[wk_]
                for ttl in range(HTT):
                    hb()
                    if half == 0 and og == 0:
                        step_n(pre.get("h0"), 4)
                    tt = half * HTT + ttl
                    pbi = tt % 4
                    for c in range(NC):
                        p.step("pe", (lambda c_, t_, i_, w_: lambda e: e.matmul(
                            pb[i_][:, 0:512],
                            xn_fm[:, c_ * TH + t_ * 128:c_ * TH + (t_ + 1) * 128],
                            w_[:, c_ * 512:(c_ + 1) * 512],
                            start=(c_ == 0), stop=False))(c, ttl, pbi, wsv),
                            reads=[f"{wk_}:{c}", f"xnfm:{c}"], writes=[f"pb{pbi}"])
                    p.step("pe", (lambda g_, i_: lambda e: e.matmul(
                        pb[i_][:, 0:512], ones_r16[:],
                        ROWS16[0:1, g_ * 512:(g_ + 1) * 512],
                        start=False, stop=True))(og, pbi),
                        reads=["ones_r16", "rrow"], writes=[f"pb{pbi}"])
                    emit_t1_tile(tt, og, pbi, x_src, x_src_key,
                                 x_dst, x_dst_key, final)
                    if og == 3 and not final and ttl % 4 == 3:
                        emit_tstats(boundary_ffn[0], tt // 4)
                    if half == 0 and og == 3:
                        step_n(pre.get("h1"), 4)

    cur_src, cur_key = X, "x_in"
    n_sub = len(sublayers)
    pre = {}
    for si, sl in enumerate(sublayers):
        final = (si == n_sub - 1)
        x_dst = OUT if final else xh[si % 2]
        x_dst_key = "out" if final else f"xh{si % 2}"
        bffn = (not final and sublayers[si + 1].startswith("ffn"),)
        if si == 0:
            wffg = wff_scan_gen() if any_ffn else None
            emit_t0_h0(sl.startswith("ffn"), wffg)
            bg0 = BG()
            bg0.set(t0_h1_gen(sl.startswith("ffn")))
            if wffg is not None:
                bg0.set(wffg)
            pre = {"h1": t2_gen(X, "x_in", range(4, 8), t0_keys=True)}
        if sl in ("t0", "wff"):
            continue
        if sl.startswith("ffn"):
            emit_ffn_mm(cur_src, cur_key, x_dst, x_dst_key, final, bffn,
                        pre=pre)
        elif sl == "attn_a":
            bg_a = emit_attn_a(pre, bg0)
            if any_ffn:
                bg_a.set(wff_xform_gen())
            emit_outproj_attn(aoh_a, "aoh_a", wor_stage_a, False,
                              cur_src, cur_key, x_dst, x_dst_key, final, bffn,
                              bg=bg_a, hb_ok=True,
                              subkeys=lambda hh, blk: [
                                  f"aoh_a:{hh}:{4 * blk + j}" for j in range(4)])
        elif sl == "attn_b":
            bg_b = emit_attn_b(pre)
            emit_outproj_attn(aoh_b, "aoh_b", wor_stage_b, True,
                              cur_src, cur_key, x_dst, x_dst_key, final, bffn,
                              bg=bg_b)
        if not final:
            pre = {"h0": t2_gen(x_dst, x_dst_key, range(0, 4)),
                   "h1": t2_gen(x_dst, x_dst_key, range(4, 8))}
        cur_src, cur_key = x_dst, x_dst_key

    if debug:
        want = debug if isinstance(debug, (set, list, tuple)) else None
        sel = lambda nm: want is None or nm in want
        dbg_specs = [
            ("dbg_wff0", wffh[0], [D, 512], BF16, ["wff:0"]),
            ("dbg_qa0", qh_a[0], [128, T], BF16, ["qh_a:0"]),
            ("dbg_ka0", kh_a[0], [128, T], BF16, ["kh_a:0"]),
            ("dbg_va0", vh_a[0], [T, 128], BF16, ["vh_a:0"]),
            ("dbg_aoa0", aoh_a[0], [128, T], BF16, [f"aoh_a:0:{b}" for b in range(32)]),
            ("dbg_aob0", aoh_b[0], [128, T], BF16, [f"aoh_b:0:{b}" for b in range(8)]),
            ("dbg_aob5", aoh_b[5], [128, T], BF16, [f"aoh_b:5:{b}" for b in range(8)]),
            ("dbg_qb0", qh_b[0], [128, T], BF16, ["qh_b:0"]),
            ("dbg_kb0", kh_b[0], [128, T], BF16, ["kh_b:0"]),
            ("dbg_vb0", vh_b[0], [T, 128], BF16, ["vh_b:0"]),
        ]
        for nm, src, shp, dt, rkeys in dbg_specs:
            if not sel(nm):
                continue
            dout = nc.dram_tensor(nm, shp, dt, kind="ExternalOutput").ap()
            p.dma("pool", (lambda d_, s_: lambda e: e.dma_start(out=d_, in_=s_))(
                dout, src), reads=rkeys, writes=[nm])
    block = ctx.enter_context(nc.Block())
    p.finalize(block, sems)
    ctx.close()
    return nc


_NC_CACHE = {}


def kernel(**inputs):
    key = tuple(SUBLAYERS)
    if key not in _NC_CACHE:
        _NC_CACHE[key] = build_program(SUBLAYERS)
    nc = _NC_CACHE[key]
    consts = _host_constants()
    x = np.ascontiguousarray(np.asarray(inputs["L0_attn_a_x"], dtype=np.float32))
    base = dict(
        wq_a=np.asarray(inputs["L0_attn_a_Wq"], np.float32),
        wk_a=np.asarray(inputs["L0_attn_a_Wk"], np.float32),
        wv_a=np.asarray(inputs["L0_attn_a_Wv"], np.float32),
        wo_a=np.asarray(inputs["L0_attn_a_Wo"], np.float32),
        wq_b=np.asarray(inputs["L0_attn_b_Wq"], np.float32),
        wk_b=np.asarray(inputs["L0_attn_b_Wk"], np.float32),
        wv_b=np.asarray(inputs["L0_attn_b_Wv"], np.float32),
        wo_b=np.asarray(inputs["L0_attn_b_Wo"], np.float32),
        bq_b=np.asarray(inputs["L0_attn_b_bq"], np.float32),
        bk_b=np.asarray(inputs["L0_attn_b_bk"], np.float32),
        bv_b=np.asarray(inputs["L0_attn_b_bv"], np.float32),
        bo_b=np.asarray(inputs["L0_attn_b_bo"], np.float32),
        w_ffn=np.asarray(inputs["L0_ffn_weight"], np.float32),
        gamma=np.asarray(inputs["L0_ffn_gamma"], np.float32),
        beta=np.asarray(inputs["L0_ffn_beta"], np.float32),
        ln_w=np.asarray(inputs["L0_ffn_ln_weight"], np.float32),
        ln_b=np.asarray(inputs["L0_ffn_ln_bias"], np.float32),
        ident32=consts["ident32"], ident16=consts["ident16"],
        maskha=consts["maskha"],
        dmasks=consts["dmasks"],
        cos_tbl=consts["cos_tbl"], sin_tbl=consts["sin_tbl"],
    )
    in_maps = [dict(base, x=x[c]) for c in range(8)]
    res = run_bass_kernel_spmd(nc, in_maps, core_ids=list(range(8)))
    return np.stack([np.asarray(r["out"], dtype=np.float32)
                     for r in res.results])
